# version 01
# baseline (speedup 1.0000x reference)
import numpy as np
import ml_dtypes
from contextlib import ExitStack

import concourse.bass as bass
import concourse.mybir as mybir
from concourse.bass_utils import run_bass_kernel_spmd

F32 = mybir.dt.float32
BF16 = mybir.dt.bfloat16
AF = mybir.ActivationFunctionType
ALU = mybir.AluOpType
AX = mybir.AxisListType

D = 1024
DH = 64
NPAIR = 8
H = 2048
M = 64
NE = 8
EPS = 1e-5
NEG = -60000.0

ENG_ORDER = ("pe", "act", "dve", "pool", "sp")


class Roll:

    def __init__(self):
        self.v = []

    def push(self, m):
        self.v.append(m)

    def back(self, k):
        return self.v[-k] if len(self.v) >= k else 0


class Em:
    def __init__(self, name, eng, sems):
        self.name = name
        self.eng = eng
        self.sems = sems
        self.cnt = {k: 0 for k in sems}

    def n(self, s):
        return self.cnt[s]

    def op(self, engine, fn, wait=(), inc=()):
        if self.name == engine:
            for s, v in wait:
                if v > 0:
                    self.eng.wait_ge(self.sems[s], v)
            ins = fn(self.eng)
            for s, k in inc:
                ins.then_inc(self.sems[s], k)
        for s, k in inc:
            self.cnt[s] += k

    def wait(self, engine, *sv):
        if self.name == engine:
            for s, v in sv:
                if v > 0:
                    self.eng.wait_ge(self.sems[s], v)

    def dma_mark(self, queue_eng, queue_sem):
        total = self.n(queue_sem)
        self.op(queue_eng, lambda e: e.nop(), wait=[(queue_sem, total)],
                inc=[(queue_eng, 1)])
        return (queue_eng, self.n(queue_eng))

    def barrier(self):
        for en in ENG_ORDER:
            self.op(en, lambda e: e.nop(), inc=[(en, 1)])
        targets = [(s, self.n(s)) for s in ("pe", "act", "dve", "pool", "sp",
                                            "in", "ing", "st", "stg")]
        for en in ENG_ORDER:
            self.wait(en, *targets)


def build_program(S, phases=3):
    import os as _os
    KAPARTS = int(_os.environ.get("KAPARTS", "255"))
    KBPARTS = int(_os.environ.get("KBPARTS", "255"))
    KBEPI = int(_os.environ.get("KBEPI", "15"))
    KNPAIR = int(_os.environ.get("KNPAIR", "8"))
    KBPROJ = int(_os.environ.get("KBPROJ", "63"))
    NQG = S // 512
    nc = bass.Bass("TRN2", target_bir_lowering=False, detect_race_conditions=False)

    x_in = nc.dram_tensor("x", [S, D], F32, kind="ExternalInput").ap()
    wqkv_in = nc.dram_tensor("wqkv", [D, 3 * D], F32, kind="ExternalInput").ap()
    wproj_in = nc.dram_tensor("wproj", [D, D], F32, kind="ExternalInput").ap()
    bproj_in = nc.dram_tensor("bproj", [1, D], F32, kind="ExternalInput").ap()
    wenc_in = nc.dram_tensor("wenc", [D, H], F32, kind="ExternalInput").ap()
    benc_in = nc.dram_tensor("benc", [16, 128], F32, kind="ExternalInput").ap()
    ws2c_in = nc.dram_tensor("ws2c", [H, M], F32, kind="ExternalInput").ap()
    bs2c_in = nc.dram_tensor("bs2c", [1, M], F32, kind="ExternalInput").ap()
    wr_in = nc.dram_tensor("wr", [M, NE], F32, kind="ExternalInput").ap()
    br_in = nc.dram_tensor("br", [1, NE], F32, kind="ExternalInput").ap()
    w1_in = nc.dram_tensor("w1", [128, 4096], F32, kind="ExternalInput").ap()
    b1_in = nc.dram_tensor("b1", [64, 128], F32, kind="ExternalInput").ap()
    w2_in = nc.dram_tensor("w2", [128, 4096], F32, kind="ExternalInput").ap()
    b2_in = nc.dram_tensor("b2", [4, 128], F32, kind="ExternalInput").ap()
    wc2s_in = nc.dram_tensor("wc2s", [M, H], F32, kind="ExternalInput").ap()
    bc2s_in = nc.dram_tensor("bc2s", [16, 128], F32, kind="ExternalInput").ap()
    wdec_in = nc.dram_tensor("wdec", [H, D], F32, kind="ExternalInput").ap()
    bdec_in = nc.dram_tensor("bdec", [1, D], F32, kind="ExternalInput").ap()
    lng_in = nc.dram_tensor("lng", [1, D], F32, kind="ExternalInput").ap()
    lnb_in = nc.dram_tensor("lnb", [1, D], F32, kind="ExternalInput").ap()
    ident_in = nc.dram_tensor("ident", [128, 128], BF16, kind="ExternalInput").ap()
    perm_in = nc.dram_tensor("perm", [128, 128], BF16, kind="ExternalInput").ap()
    cos_in = nc.dram_tensor("cos_t", [128, S], BF16, kind="ExternalInput").ap()
    sin_in = nc.dram_tensor("sin_t", [128, S], BF16, kind="ExternalInput").ap()
    mask_in = nc.dram_tensor("mask2", [128, 256], BF16, kind="ExternalInput").ap()
    el_in = nc.dram_tensor("el", [65, 128], BF16, kind="ExternalInput").ap()
    elr_in = nc.dram_tensor("elr", [128, 128], BF16, kind="ExternalInput").ap()
    ones1_in = nc.dram_tensor("ones1", [128, 2], BF16, kind="ExternalInput").ap()
    onesr_in = nc.dram_tensor("onesr", [1, 128], BF16, kind="ExternalInput").ap()
    sel_in = nc.dram_tensor("sel", [8, 512], BF16, kind="ExternalInput").ap()

    out_hbm = nc.dram_tensor("out", [S, D], F32, kind="ExternalOutput").ap()
    q_scr = nc.dram_tensor("q_scr", [NQG, 8, 128, 512], BF16, kind="Internal").ap()
    x1_scr = nc.dram_tensor("x1_scr", [S, D], F32, kind="Internal").ap()

    ostk = ExitStack()
    oe = ostk.enter_context

    id_sb = nc.alloc_sbuf_tensor("id_sb", [128, 128], BF16).ap()
    onesr_sb = nc.alloc_sbuf_tensor("onesr_sb", [1, 128], BF16).ap()
    c_col = nc.alloc_sbuf_tensor("c_col", [128, 2], F32).ap()[:, 0:1]
    small = nc.alloc_sbuf_tensor("small", [128, 64], F32).ap()
    stats_sb = nc.alloc_sbuf_tensor("stats_sb", [128, 48], F32).ap()

    pbank = nc.alloc_psum_tensor("pbank", [128, 4096], F32).ap()

    cur = [0]
    hold = [None]

    def carve_bf(n, p=128):
        c0 = cur[0]
        cur[0] = (cur[0] + n + 31) & ~31
        if hold[0] is None:
            return None
        return hold[0][0:p, c0:c0 + n]

    def carve_f32(n, p=128):
        c0 = cur[0]
        cur[0] = (cur[0] + 2 * n + 31) & ~31
        if hold[0] is None:
            return None
        return hold[0][:, c0:c0 + 2 * n].bitcast(F32)[0:p, 0:n]

    sems = {}
    for k in ("in", "ing", "st", "stg", "pe", "act", "dve", "pool", "sp"):
        sems[k] = oe(nc.semaphore(f"s_{k}"))

    block = oe(nc.Block())

    T = {}

    def mk_layout():
        cur[0] = 0
        T["kf"] = carve_bf(NPAIR * S)
        T["vres"] = carve_bf((S // 128) * 16 * 65)
        ab_end = cur[0]
        T["wqkv"] = carve_bf(8 * 3 * D)
        T["xnT"] = carve_bf(8 * 512)
        T["cos"] = carve_bf(512)
        T["sin"] = carve_bf(512)
        T["perm"] = carve_bf(128)
        T["ones1"] = carve_bf(2)
        T["qcb0"] = carve_bf(512)
        T["qcb1"] = carve_bf(512)
        T["rotb"] = carve_bf(512)
        T["t1b"] = carve_bf(512)
        T["rop0"] = carve_bf(512)
        T["rop1"] = carve_bf(512)
        T["xna"] = carve_bf(D)
        for i in range(4):
            T[f"xa{i}"] = carve_bf(D)
        a_end = cur[0]
        cur[0] = ab_end
        T["q"] = carve_bf(8 * 512)
        T["wproj"] = carve_bf(8 * D)
        T["p0"] = carve_bf(1024)
        T["p1"] = carve_bf(1024)
        T["l"] = carve_bf(512)
        T["elr"] = carve_bf(128)
        T["rl"] = carve_f32(512)
        T["rlb"] = carve_f32(512, p=64)
        T["tb"] = carve_bf(512, p=64)
        T["attn"] = carve_bf(8 * 512)
        T["mask"] = carve_bf(256)
        T["bproj_bc"] = carve_f32(D)
        T["t32b"] = carve_f32(D)
        T["x1b"] = carve_f32(D)
        T["xb0"] = carve_f32(D)
        T["xb1"] = carve_f32(D)
        b_end = cur[0]
        cur[0] = 0
        T["wenc"] = carve_bf(8 * H)
        T["wdec"] = carve_bf(16 * D)
        T["ws2c"] = carve_bf(16 * M)
        T["wc2s"] = carve_bf(H)
        T["w1"] = carve_bf(4096)
        T["w2"] = carve_bf(4096)
        T["wr"] = carve_f32(8, p=64)
        T["sel"] = carve_bf(512, p=8)
        T["benc"] = carve_f32(16)
        T["bc2s"] = carve_f32(16)
        T["b1"] = carve_f32(64)
        T["b2"] = carve_f32(4)
        T["bs2c"] = carve_f32(2, p=64)
        T["br"] = carve_f32(8)
        T["lng"] = carve_f32(D)
        T["lnb"] = carve_f32(D)
        T["bdec_bc"] = carve_f32(D)
        T["xn2T"] = carve_bf(8 * 512)
        T["spik"] = carve_bf(16 * 512)
        T["smoe"] = carve_bf(16 * 512)
        T["cdup"] = carve_bf(512)
        T["cont32"] = carve_f32(512, p=64)
        T["h0"] = carve_bf(8 * 512)
        T["h1"] = carve_bf(8 * 512)
        T["gate"] = carve_bf(32)
        T["gT"] = carve_bf(512, p=8)
        T["gbc"] = carve_f32(512)
        T["tp"] = carve_f32(512)
        T["tp2"] = carve_f32(512, p=64)
        T["eo32"] = carve_f32(512, p=64)
        T["eod"] = carve_bf(512)
        T["rsm"] = carve_f32(16)
        T["xnc"] = carve_bf(D)
        T["t32c"] = carve_f32(D)
        T["x1c"] = carve_f32(D)
        for i in range(4):
            T[f"xc{i}"] = carve_f32(D)
        c_end = cur[0]
        return max(a_end, b_end, c_end)

    need = mk_layout()
    arena = nc.alloc_sbuf_tensor("arena", [128, need + 16], BF16).ap()
    hold[0] = arena
    mk_layout()
    T["bs2c"] = T["bs2c"][:, 0:1]
    dly_sb = nc.alloc_sbuf_tensor("dly_sb", [128, 1024], BF16).ap().bitcast(F32)
    dump_sb = nc.alloc_sbuf_tensor("dump_sb", [128, 512], BF16).ap()


    def ln_stats_act(E, src_ap, i, wait=()):
        for hf in range(2):
            E.op("act", lambda a, hf=hf: a.activation(
                dump_sb[:, :], src_ap[:, hf * 512:(hf + 1) * 512], AF.Copy,
                accum_out=stats_sb[:, 12 + 2 * i + hf:13 + 2 * i + hf]),
                wait=(list(wait) if hf == 0 else []), inc=[("act", 1)])
            E.op("act", lambda a, hf=hf: a.activation(
                dump_sb[:, :], src_ap[:, hf * 512:(hf + 1) * 512], AF.Square,
                accum_out=stats_sb[:, 24 + 2 * i + hf:25 + 2 * i + hf]),
                inc=[("act", 1)])
        am = E.n("act")
        E.op("dve", lambda v, i=i: v.tensor_add(
            stats_sb[:, 32 + i:33 + i], stats_sb[:, 12 + 2 * i:13 + 2 * i],
            stats_sb[:, 13 + 2 * i:14 + 2 * i]), wait=[("act", am)])
        dve_pad(E)
        E.op("dve", lambda v, i=i: v.tensor_add(
            stats_sb[:, 36 + i:37 + i], stats_sb[:, 24 + 2 * i:25 + 2 * i],
            stats_sb[:, 25 + 2 * i:26 + 2 * i]))
        dve_pad(E)
        E.op("dve", lambda v, i=i: v.tensor_scalar(
            small[:, 8 + 2 * i:9 + 2 * i], stats_sb[:, 32 + i:33 + i],
            1.0 / D, None, ALU.mult))
        dve_pad(E)
        E.op("dve", lambda v, i=i: v.tensor_mul(
            stats_sb[:, 20 + i:21 + i], small[:, 8 + 2 * i:9 + 2 * i],
            small[:, 8 + 2 * i:9 + 2 * i]))
        dve_pad(E)
        E.op("dve", lambda v, i=i: v.scalar_tensor_tensor(
            small[:, 9 + 2 * i:10 + 2 * i], stats_sb[:, 36 + i:37 + i], 1.0 / D,
            stats_sb[:, 20 + i:21 + i], ALU.mult, ALU.subtract))
        dve_pad(E)
        E.op("dve", lambda v: v.engine_nop(), inc=[("dve", 1)])
        return E.n("dve")

    def dve_pad(E):
        E.op("dve", lambda v: v.tensor_scalar(dly_sb[:, 0:64], dly_sb[:, 0:64],
                                              1.0, None, ALU.mult))

    def rsqrt_newton(E, v_ap, out_ap, t1_ap, t2_ap, n_iter=3):
        I32 = mybir.dt.int32

        def ss(fn):
            E.op("dve", fn)
            dve_pad(E)

        dve_pad(E)
        ss(lambda e: e.tensor_scalar(t1_ap.bitcast(I32), v_ap.bitcast(I32), 1, None,
                                     ALU.logical_shift_right))
        ss(lambda e: e.tensor_scalar(t2_ap.bitcast(I32), t1_ap.bitcast(I32),
                                     -1, None, ALU.bitwise_xor))
        ss(lambda e: e.tensor_scalar(out_ap.bitcast(I32), t2_ap.bitcast(I32),
                                     0x5F3759E0, None, ALU.add))
        for _ in range(n_iter):
            ss(lambda e: e.tensor_mul(t1_ap, out_ap, out_ap))
            ss(lambda e: e.tensor_mul(t1_ap, t1_ap, v_ap))
            ss(lambda e: e.tensor_scalar(t2_ap, t1_ap, -0.5, 1.5, ALU.mult, ALU.add))
            ss(lambda e: e.tensor_mul(out_ap, out_ap, t2_ap))
        E.op("dve", lambda v: v.engine_nop(), wait=[], inc=[("dve", 1)])

    def rstd_from_vars(E, n):
        if n == 1:
            var_ap = small[:, 9:10]
            mean_ap = small[:, 8:9]
        else:
            var_ap = small[:, 8:8 + 2 * n].rearrange("p (a b) -> p a b", b=2)[:, :, 1]
            mean_ap = small[:, 8:8 + 2 * n].rearrange("p (a b) -> p a b", b=2)[:, :, 0]
        E.op("dve", lambda v: v.tensor_scalar(
            small[:, 16:16 + n], var_ap,
            EPS, None, ALU.add), wait=[("dve", E.n("dve"))], inc=[("dve", 1)])
        rsqrt_newton(E, small[:, 16:16 + n], small[:, 20:20 + n],
                     stats_sb[:, 0:n], stats_sb[:, 4:4 + n])
        E.op("dve", lambda v: v.scalar_tensor_tensor(
            small[:, 16:16 + n], mean_ap,
            -1.0, small[:, 20:20 + n], ALU.mult, ALU.mult),
            wait=[("dve", E.n("dve"))], inc=[("dve", 1)])
        return E.n("dve")

    def pe_guard(E, pe_val, extra=()):
        E.op("dve", lambda v: v.tensor_scalar(dly_sb[:, :], dly_sb[:, :], 1.0,
                                              None, ALU.mult),
             wait=[("pe", pe_val)] + list(extra) + [("dve", E.n("dve"))],
             inc=[("dve", 1)])
        return E.n("dve")


    def program(E):
        E.op("sp", lambda s: s.dma_start(id_sb[:, :], ident_in[:, :]), inc=[("in", 16)])
        E.op("sp", lambda s: s.dma_start(onesr_sb[:, :], onesr_in[:, :]), inc=[("in", 16)])
        E.op("dve", lambda v: v.memset(dly_sb[:, :], 0.0), inc=[("dve", 1)])

        phase_a(E, T["kf"], T["vres"])
        E.barrier()
        if phases >= 2:
            phase_b(E, T["kf"], T["vres"])
            E.barrier()
        if phases >= 3:
            phase_c(E)
        E.wait("sp", ("st", E.n("st")), ("stg", E.n("stg")))


    def phase_a(E, kf, vres):
        if True:
            wqkv_sb = T["wqkv"]
            xnT = T["xnT"]
            cos_sb = T["cos"]
            sin_sb = T["sin"]
            perm_sb = T["perm"]
            ones1_sb = T["ones1"]
            qcb = [T["qcb0"], T["qcb1"]]
            rotb = T["rotb"]
            t1b = T["t1b"]
            rop = [T["rop0"], T["rop1"]]
            xn_sb = T["xna"]
            xa_sb = [T[f"xa{i}"] for i in range(4)]

            ps_qkv = [pbank[:, 0:512], pbank[:, 512:1024]]
            ps_rot = [pbank[:, 1024:1536], pbank[:, 1536:2048]]
            ps_tr = [pbank[:, 2048:2112].bitcast(BF16)[:, 0:128],
                     pbank[:, 2560:2624].bitcast(BF16)[:, 0:128]]
            ps_v = [pbank[:, 3072:3584], pbank[:, 3584:4096]]

            E.op("sp", lambda s: s.dma_start(perm_sb[:, :], perm_in[:, :]), inc=[("in", 16)])
            E.op("sp", lambda s: s.dma_start(ones1_sb[:, :], ones1_in[:, :]), inc=[("in", 16)])
            m_aconst = E.dma_mark("sp", "in")
            for dc in range(8):
                E.op("pool", lambda g, dc=dc: g.dma_start(
                    wqkv_sb[:, dc * 3 * D:(dc + 1) * 3 * D],
                    wqkv_in[dc * 128:(dc + 1) * 128, :]), inc=[("ing", 16)])
            m_wq = E.dma_mark("pool", "ing")
            E.op("dve", lambda v: v.memset(
                vres[:, :].rearrange("p (k o) -> p k o", o=65)[:, :, 64], 1.0),
                inc=[("dve", 1)])
            E.op("dve", lambda v: v.memset(small[:, 0:8], 0.0),
                 wait=[("dve", E.n("dve"))], inc=[("dve", 1)])

            r_psqkv = Roll()
            r_psrot = Roll()
            r_qcb = Roll()
            r_rop = Roll()
            r_pstr = Roll()
            r_xslot = Roll()
            r_xn = Roll()
            r_rotb = Roll()
            r_psv = Roll()
            r_nrm = Roll()
            r_nrmpe = Roll()
            tabs_prev = Roll()

            for g in range(NQG):
                if not (KAPARTS & 1):
                    break
                war = tabs_prev.back(1)
                E.op("sp", lambda s, g=g: s.dma_start(
                    cos_sb[:, :], cos_in[:, g * 512:(g + 1) * 512]),
                    wait=[("dve", war)], inc=[("in", 16)])
                E.op("sp", lambda s, g=g: s.dma_start(
                    sin_sb[:, :], sin_in[:, g * 512:(g + 1) * 512]), inc=[("in", 16)])
                m_tab = E.dma_mark("sp", "in")

                for qt in range(4):
                    qa = g * 4 + qt
                    wb = r_xslot.back(4)
                    E.op("pool", lambda p, qa=qa, qt=qt: p.dma_start(
                        xa_sb[qt][:, :], x_in[qa * 128:(qa + 1) * 128, :]),
                        wait=(([("dve", wb[1]), ("dve", tabs_prev.back(1)),
                                ("pe", r_nrmpe.back(1))]
                               if wb else []) if qt == 0 else []),
                        inc=[("ing", 16)])
                m_xa = E.dma_mark("pool", "ing")
                if not (KAPARTS & 16):
                    continue
                for i in range(4):
                    ln_stats_act(E, xa_sb[i][:, :], i, wait=([m_xa] if i == 0 else []))
                if not (KAPARTS & 32):
                    continue
                rstd_done = rstd_from_vars(E, 4)

                if not (KAPARTS & 64):
                    continue
                for qt in range(4):
                    E.op("dve", lambda v, qt=qt: v.tensor_scalar(
                        xn_sb[:, :], xa_sb[qt][:, :],
                        small[:, 20 + qt:21 + qt], small[:, 16 + qt:17 + qt],
                        ALU.mult, ALU.add),
                        wait=[("dve", rstd_done), ("pe", r_xn.back(1)),
                              ("dve", E.n("dve"))],
                        inc=[("dve", 1)])
                    apply_done = E.n("dve")
                    for dc in range(8 if (KAPARTS & 128) else 0):
                        half = dc % 2
                        E.op("pe", lambda t, dc=dc, half=half: t.transpose(
                            ps_tr[half], xn_sb[:, dc * 128:(dc + 1) * 128], id_sb[:, :]),
                            wait=[("dve", apply_done), m_aconst,
                                  ("act", r_pstr.back(2)),
                                  ("dve", r_nrm.back(1))],
                            inc=[("pe", 1)])
                        trd = E.n("pe")
                        E.op("act", lambda a, dc=dc, qt=qt, half=half: a.activation(
                            xnT[:, dc * 512 + qt * 128: dc * 512 + qt * 128 + 128],
                            ps_tr[half], AF.Copy),
                            wait=[("pe", trd)], inc=[("act", 1)])
                        r_pstr.push(E.n("act"))
                    r_xn.push(E.n("pe"))
                    r_xslot.push((0, E.n("dve")))
                xnt_done_a = E.n("act")
                xnt_done = E.n("dve")

                for fc in range(16 if (KAPARTS & 2) else 0):
                    par = fc % 2
                    wq_war = r_psqkv.back(2)
                    for dc in range(8):
                        E.op("pe", lambda t, fc=fc, dc=dc, par=par: t.matmul(
                            ps_qkv[par][:, :],
                            wqkv_sb[:, dc * 3 * D + fc * 128: dc * 3 * D + fc * 128 + 128],
                            xnT[:, dc * 512:(dc + 1) * 512],
                            start=(dc == 0), stop=(dc == 7)),
                            wait=([("act", xnt_done_a), m_wq,
                                   ("act", wq_war[0] if wq_war else 0),
                                   ("dve", wq_war[1] if wq_war else 0)] if dc == 0 else []),
                            inc=([("pe", 1)] if dc == 7 else []))
                    qkv_done = E.n("pe")
                    E.op("act", lambda a, par=par: a.activation(
                        qcb[par][:, :], ps_qkv[par][:, :], AF.Copy),
                        wait=[("pe", qkv_done), ("dve", r_qcb.back(2))],
                        inc=[("act", 1)])
                    qc_done = E.n("act")
                    E.op("pe", lambda t, par=par: t.matmul(
                        ps_rot[par][:, :], perm_sb[:, :], qcb[par][:, :],
                        start=True, stop=True),
                        wait=[("act", qc_done), m_aconst,
                              ("act", r_psrot.back(2)), ("dve", r_nrm.back(2))],
                        inc=[("pe", 1)])
                    rot_done = E.n("pe")
                    E.op("act", lambda a, par=par: a.activation(
                        rotb[:, :], ps_rot[par][:, :], AF.Copy),
                        wait=[("pe", rot_done), ("dve", r_rotb.back(1)),
                              ("pe", r_nrmpe.back(1))],
                        inc=[("act", 1)])
                    r_psrot.push(E.n("act"))
                    rotb_done = E.n("act")
                    E.op("dve", lambda v, par=par: v.tensor_mul(
                        t1b[:, :], qcb[par][:, :], cos_sb[:, :]),
                        wait=[m_tab, ("act", qc_done), ("dve", E.n("dve"))],
                        inc=[("dve", 1)])
                    r_qcb.push(E.n("dve"))
                    E.op("dve", lambda v: v.tensor_mul(
                        rotb[:, :], rotb[:, :], sin_sb[:, :]),
                        wait=[("act", rotb_done), ("dve", E.n("dve"))], inc=[("dve", 1)])
                    if fc < 8:
                        dest = rop[par][:, :]
                        wm = r_rop.back(2)
                        dwait = [wm] if wm else []
                    else:
                        dest = kf[:, (fc - 8) * S + g * 512:(fc - 8) * S + g * 512 + 512]
                        dwait = []
                    E.op("dve", lambda v, dest=dest: v.tensor_add(
                        dest, t1b[:, :], rotb[:, :]),
                        wait=dwait + [("dve", E.n("dve"))], inc=[("dve", 1)])
                    rope_done = E.n("dve")
                    E.op("dve", lambda v, dest=dest: v.tensor_mul(rotb[:, :], dest, dest),
                         wait=[("dve", E.n("dve"))], inc=[("dve", 1)])
                    sq_done = E.n("dve")
                    r_rotb.push(E.n("dve"))
                    E.op("pe", lambda t, par=par: t.matmul(
                        ps_rot[par][0:1, :], ones1_sb[:, 0:1], rotb[:, :],
                        start=True, stop=True),
                        wait=[("dve", sq_done), ("dve", r_nrm.back(1))],
                        inc=[("pe", 1)])
                    nrm_done = E.n("pe")
                    r_nrmpe.push(nrm_done)
                    tgt = 0 if fc < 8 else 1
                    gn = pe_guard(E, nrm_done)
                    E.op("dve", lambda v, par=par: v.reduce_max(
                        small[0:1, 2:3], ps_rot[par][0:1, :], axis=AX.X),
                        wait=[("dve", gn)], inc=[("dve", 1)])
                    E.op("dve", lambda v, tgt=tgt: v.tensor_max(
                        small[0:1, tgt:tgt + 1], small[0:1, tgt:tgt + 1],
                        small[0:1, 2:3]),
                        wait=[("dve", E.n("dve"))], inc=[("dve", 1)])
                    r_nrm.push(E.n("dve"))
                    if fc < 8:
                        E.op("sp", lambda p, g=g, fc=fc, par=par: p.dma_start(
                            q_scr[g, fc, :, :], rop[par][:, :]),
                            wait=[("dve", rope_done)], inc=[("st", 16)])
                        r_rop.push(E.dma_mark("sp", "st"))
                    r_psqkv.push((E.n("act"), E.n("dve")))
                tabs_prev.push(E.n("dve"))

                for qt in range(4 if (KAPARTS & 4) else 0):
                    kt = g * 4 + qt
                    for half in range(2):
                        par = half
                        for dc in range(8):
                            E.op("pe", lambda t, qt=qt, half=half, dc=dc, par=par: t.matmul(
                                ps_v[par][:, :],
                                xnT[:, dc * 512 + qt * 128: dc * 512 + qt * 128 + 128],
                                wqkv_sb[:, dc * 3 * D + 2048 + half * 512:
                                        dc * 3 * D + 2048 + half * 512 + 512],
                                start=(dc == 0), stop=(dc == 7)),
                                wait=([("dve", r_psv.back(2))] if dc == 0 else []),
                                inc=([("pe", 1)] if dc == 7 else []))
                        vmm = E.n("pe")
                        base = kt * 16 * 65 + half * 8 * 65
                        gv = pe_guard(E, vmm)
                        E.op("dve", lambda v, base=base, par=par: v.tensor_copy(
                            vres[:, base:base + 8 * 65].rearrange(
                                "p (h o) -> p h o", o=65)[:, :, 0:64],
                            ps_v[par][:, :].rearrange("p (h d) -> p h d", d=64)),
                            wait=[("dve", gv)], inc=[("dve", 1)])
                        r_psv.push(E.n("dve"))

            if not (KAPARTS & 8):
                return
            def ssd(fn):
                E.op("dve", fn)
                dve_pad(E)
            dve_pad(E)
            ssd(lambda v: v.tensor_mul(small[0:1, 2:3], small[0:1, 0:1], small[0:1, 1:2]))
            rsqrt_newton(E, small[0:1, 2:3], small[0:1, 3:4],
                         small[0:1, 4:5], small[0:1, 5:6])
            ssd(lambda v: v.tensor_mul(small[0:1, 4:5], small[0:1, 2:3], small[0:1, 3:4]))
            ssd(lambda v: v.tensor_scalar(small[0:1, 5:6], small[0:1, 4:5], -0.125,
                                          None, ALU.mult))
            ssd(lambda v: v.tensor_copy(rotb[0:1, 0:1], small[0:1, 5:6]))
            E.op("dve", lambda v: v.engine_nop(), inc=[("dve", 1)])
            cval = E.n("dve")
            E.op("pe", lambda t: t.matmul(ps_rot[0][:, 0:1], onesr_sb[0:1, :],
                                          rotb[0:1, 0:1], start=True, stop=True),
                 wait=[("dve", cval), ("dve", r_nrm.back(1))], inc=[("pe", 1)])
            cbc = E.n("pe")
            E.op("dve", lambda v: v.tensor_copy(c_col[:, :], ps_rot[0][:, 0:1]),
                 wait=[("pe", cbc), ("dve", E.n("dve"))], inc=[("dve", 1)])


    def phase_b(E, kf, vres):
        if True:
            q_sb = T["q"]
            wproj_sb = T["wproj"]
            p_sb = [T["p0"], T["p1"]]
            l_sb = T["l"]
            elr_sb = T["elr"]
            rl_sb = T["rl"]
            rlb_sb = T["rlb"]
            tb_sb = T["tb"]
            attn_sb = T["attn"]
            mask_sb = T["mask"]
            bproj_bc = T["bproj_bc"]
            t32_sb = T["t32b"]
            x1_sb = T["x1b"]
            xb_sb = [T["xb0"], T["xb1"]]

            ps_s = [pbank[:, 0:1024], pbank[:, 1024:2048]]
            ps_pv = [pbank[0:65, 2048:2560], pbank[0:65, 2560:3072]]
            ps_bc = pbank[:, 3072:3584]
            ps_proj = pbank[:, 3584:4096]

            E.op("sp", lambda s: s.dma_start(elr_sb[:, :], elr_in[:, :]), inc=[("in", 16)])
            E.op("sp", lambda s: s.dma_start(mask_sb[:, :], mask_in[:, :]), inc=[("in", 16)])
            E.op("sp", lambda s: s.dma_start(
                bproj_bc[:, :], bproj_in[0:1, :].partition_broadcast(128)),
                inc=[("in", 16)])
            m_bconst = E.dma_mark("sp", "in")
            for dc in range(8):
                E.op("pool", lambda g2, dc=dc: g2.dma_start(
                    wproj_sb[:, dc * D:(dc + 1) * D],
                    wproj_in[dc * 128:(dc + 1) * 128, :]), inc=[("ing", 16)])
            m_bw = E.dma_mark("pool", "ing")
            E.op("dve", lambda v: v.memset(l_sb[:, :], 0.0),
                 wait=[("dve", E.n("dve"))], inc=[("dve", 1)])

            r_pss = Roll()
            r_pexp = Roll()
            r_pvfree = Roll()
            r_rl = Roll()
            r_x1 = Roll()
            r_x1dma = Roll()
            r_pev = Roll()
            r_xb = Roll()
            stlast = Roll()
            r_qscr = Roll()

            for g in range(NQG if (KBPARTS & 1) else 0):
                NK = 4 * g + 4
                for fc in range(8):
                    E.op("sp", lambda s, g=g, fc=fc: s.dma_start(
                        q_sb[:, fc * 512:(fc + 1) * 512], q_scr[g, fc, :, :]),
                        wait=([("pe", stlast.back(1)),
                               ("st", r_qscr.back(1))] if fc == 0 else []),
                        inc=[("in", 16)])
                m_q = E.dma_mark("sp", "in")

                for c in range(min(NPAIR, KNPAIR) if (KBPARTS & 2) else 0):
                    for kc in range(NK):
                        par = kc % 2
                        j = kc - 4 * g
                        off = 128 * j if j > 0 else 0
                        kbase = c * S + kc * 128
                        E.op("pe", lambda t, kbase=kbase: t.ldweights(
                            kf[0:64, kbase:kbase + 128], tile_position=(0, 0)),
                            wait=[m_q, ("act", r_pss.back(2)), m_bconst])
                        E.op("pe", lambda t, kbase=kbase: t.ldweights(
                            kf[64:128, kbase:kbase + 128], tile_position=(64, 0)))
                        E.op("pe", lambda t, kbase=kbase, par=par, off=off, c=c: t.matmul(
                            ps_s[par][:, off:512], kf[0:64, kbase:kbase + 128],
                            q_sb[0:64, c * 512 + off:(c + 1) * 512],
                            start=True, stop=True, tile_position=(0, 0)))
                        E.op("pe", lambda t, kbase=kbase, par=par, off=off, c=c: t.matmul(
                            ps_s[par][:, 512 + off:1024], kf[64:128, kbase:kbase + 128],
                            q_sb[64:128, c * 512 + off:(c + 1) * 512],
                            start=True, stop=True, tile_position=(64, 0)),
                            inc=[("pe", 1)])
                        st_done = E.n("pe")
                        E.op("act", lambda a, par=par, off=off: a.activation(
                            p_sb[par][:, :].rearrange("p (a n) -> p a n", a=2)[:, :, off:512],
                            ps_s[par][:, :].rearrange("p (a n) -> p a n", a=2)[:, :, off:512],
                            AF.Exp, scale=0.125, bias=c_col[:, 0:1]),
                            wait=[("pe", st_done), ("pe", r_pexp.back(2))],
                            inc=[("act", 1)])
                        r_pss.push(E.n("act"))
                        exp_done = E.n("act")
                        mask_done = 0
                        if j >= 0 and _os.environ.get("KNOMASK", "") != "1":
                            E.op("dve", lambda v, par=par, off=off: v.tensor_mul(
                                p_sb[par][:, :].rearrange("p (a n) -> p a n", a=2)
                                [:, :, off:off + 128],
                                p_sb[par][:, :].rearrange("p (a n) -> p a n", a=2)
                                [:, :, off:off + 128],
                                mask_sb[:, :].rearrange("p (a n) -> p a n", a=2)),
                                wait=[("act", exp_done), ("dve", E.n("dve"))],
                                inc=[("dve", 1)])
                            mask_done = E.n("dve")
                        for hh in range(2 if (KBPARTS & 4) else 0):
                            h = 2 * c + hh
                            vbase = kc * 16 * 65 + h * 65
                            E.op("pe", lambda t, vbase=vbase, hh=hh, par=par, off=off,
                                 kc=kc, NK=NK: t.matmul(
                                ps_pv[hh][:, off:512], vres[:, vbase:vbase + 65],
                                p_sb[par][:, hh * 512 + off: hh * 512 + 512],
                                start=(kc == 0), stop=(kc == NK - 1),
                                skip_group_check=True),
                                wait=(([("act", exp_done), ("dve", mask_done)]
                                       + ([("dve", r_pvfree.back(1))]
                                          if (hh == 0 and kc == 0) else []))
                                      if hh == 0 else []),
                                inc=([("pe", 1)] if hh == 1 else []))
                        r_pexp.push(E.n("pe"))
                    pv_done = E.n("pe")
                    if not (KBEPI & 8):
                        continue
                    g1 = pe_guard(E, pv_done)
                    E.op("dve", lambda v: v.tensor_copy(l_sb[0:1, :], ps_pv[0][64:65, :]),
                         wait=[("dve", g1)], inc=[("dve", 1)])
                    E.op("dve", lambda v: v.tensor_copy(l_sb[64:65, :], ps_pv[1][64:65, :]),
                         wait=[("dve", E.n("dve"))], inc=[("dve", 1)])
                    lrows = E.n("dve")
                    E.op("pe", lambda t: t.matmul(ps_bc[0:64, :], elr_sb[:, 0:64],
                                                  l_sb[:, :], start=True, stop=True),
                         wait=[("dve", lrows), ("dve", r_rl.back(1)),
                               ("dve", r_pev.back(1))])
                    E.op("pe", lambda t: t.matmul(ps_proj[0:64, :], elr_sb[:, 64:128],
                                                  l_sb[:, :], start=True, stop=True),
                         wait=[("dve", r_pev.back(1))],
                         inc=[("pe", 1)])
                    bcd = E.n("pe")
                    g2 = pe_guard(E, bcd)
                    E.op("dve", lambda v: v.tensor_copy(rl_sb[0:64, :], ps_bc[0:64, :]),
                         wait=[("dve", g2)], inc=[("dve", 1)])
                    E.op("dve", lambda v: v.tensor_copy(rlb_sb[0:64, :], ps_proj[0:64, :]),
                         wait=[("dve", E.n("dve"))], inc=[("dve", 1)])
                    E.op("dve", lambda v: v.reciprocal(rl_sb[0:64, :], rl_sb[0:64, :]),
                         wait=[("dve", E.n("dve"))], inc=[("dve", 1)])
                    E.op("dve", lambda v: v.reciprocal(rlb_sb[0:64, :], rlb_sb[0:64, :]),
                         wait=[("dve", E.n("dve"))], inc=[("dve", 1)])
                    E.op("dve", lambda v, c=c: v.tensor_mul(
                        attn_sb[0:64, c * 512:(c + 1) * 512],
                        ps_pv[0][0:64, :], rl_sb[0:64, :]),
                        wait=[("dve", E.n("dve"))], inc=[("dve", 1)])
                    E.op("dve", lambda v: v.tensor_mul(
                        tb_sb[0:64, :], ps_pv[1][0:64, :], rlb_sb[0:64, :]),
                        wait=[("dve", E.n("dve"))], inc=[("dve", 1)])
                    tbd = E.n("dve")
                    E.op("pe", lambda t: t.matmul(
                        ps_bc[64:128, :], id_sb[0:64, 0:64], tb_sb[0:64, :],
                        start=True, stop=True, tile_position=(0, 64)),
                        wait=[("dve", tbd)], inc=[("pe", 1)])
                    mvd = E.n("pe")
                    g3 = pe_guard(E, mvd)
                    E.op("dve", lambda v, c=c: v.tensor_copy(
                        attn_sb[64:128, c * 512:(c + 1) * 512], ps_bc[64:128, :]),
                        wait=[("dve", g3)], inc=[("dve", 1)])
                    r_rl.push(E.n("dve"))
                    r_pvfree.push(E.n("dve"))
                stlast.push(E.n("pe"))
                attn_ready = E.n("dve")

                for qt in range(4 if (KBPARTS & 16) else 0):
                    qa = g * 4 + qt
                    slot = qt % 2
                    wb = r_xb.back(2)
                    E.op("sp", lambda s, qa=qa, slot=slot: s.dma_start(
                        xb_sb[slot][:, :], x_in[qa * 128:(qa + 1) * 128, :]),
                        wait=[("dve", wb)] if wb else [], inc=[("in", 16)])
                    m_xb = E.dma_mark("sp", "in")
                    if not (KBPROJ & 2):
                        continue
                    for half in range(2):
                        ps_pr = ps_proj if half == 0 else ps_bc
                        for dc in range(8):
                            E.op("pe", lambda t, qt=qt, half=half, dc=dc, ps_pr=ps_pr:
                                 t.matmul(
                                ps_pr[:, :],
                                attn_sb[:, dc * 512 + qt * 128: dc * 512 + qt * 128 + 128],
                                wproj_sb[:, dc * D + half * 512: dc * D + half * 512 + 512],
                                start=(dc == 0), stop=(dc == 7)),
                                wait=([("dve", attn_ready), m_bw,
                                       ("dve", r_pev.back(2))] if dc == 0 else []),
                                inc=([("pe", 1)] if dc == 7 else []))
                        pj = E.n("pe")
                        if not (KBPROJ & 32):
                            continue
                        gpj = pe_guard(E, pj, extra=[m_bconst])
                        E.op("dve", lambda v, half=half, ps_pr=ps_pr: v.tensor_add(
                            t32_sb[:, half * 512:(half + 1) * 512], ps_pr[:, :],
                            bproj_bc[:, half * 512:(half + 1) * 512]),
                            wait=[("dve", gpj)],
                            inc=[("dve", 1)])
                        r_pev.push(E.n("dve"))
                    if not (KBPROJ & 4):
                        continue
                    KST = int(_os.environ.get("KST", "7"))
                    if KST & 1:
                        ln_stats_act(E, t32_sb[:, :], 0,
                                     wait=[("dve", E.n("dve"))])
                    rd = E.n("dve")
                    if KST & 2:
                        rd = rstd_from_vars(E, 1)
                    if not (KST & 4):
                        continue
                    E.op("dve", lambda v: v.tensor_scalar(
                        x1_sb[:, :], t32_sb[:, :],
                        small[:, 20:21], small[:, 16:17], ALU.mult, ALU.add),
                        wait=[("dve", rd)]
                             + ([r_x1dma.back(1)] if r_x1dma.back(1) else []),
                        inc=[("dve", 1)])
                    if not (KBPROJ & 8):
                        continue
                    E.op("dve", lambda v, slot=slot: v.tensor_add(
                        x1_sb[:, :], x1_sb[:, :], xb_sb[slot][:, :]),
                        wait=[m_xb, ("dve", E.n("dve"))],
                        inc=[("dve", 1)])
                    r_x1.push(E.n("dve"))
                    r_xb.push(E.n("dve"))
                    fin = E.n("dve")
                    E.op("pool", lambda p, qa=qa: p.dma_start(
                        x1_scr[qa * 128:(qa + 1) * 128, :], x1_sb[:, :]),
                        wait=[("dve", fin)], inc=[("stg", 16)])
                    r_x1dma.push(E.dma_mark("pool", "stg"))


    def phase_c(E):
        if True:
            wenc_sb = T["wenc"]
            wdec_sb = T["wdec"]
            ws2c_sb = T["ws2c"]
            wc2s_sb = T["wc2s"]
            w1_sb = T["w1"]
            w2_sb = T["w2"]
            wr_sb = T["wr"]
            sel_sb = T["sel"]
            benc_sb = T["benc"]
            bc2s_sb = T["bc2s"]
            b1_sb = T["b1"]
            b2_sb = T["b2"]
            bs2c_sb = T["bs2c"]
            br_sb = T["br"]
            lng_sb = T["lng"]
            lnb_sb = T["lnb"]
            bdec_bc = T["bdec_bc"]
            xn2T = T["xn2T"]
            spik = T["spik"]
            smoe = T["smoe"]
            cdup = T["cdup"]
            cont32 = T["cont32"]
            h_sb = [T["h0"], T["h1"]]
            gate_sb = T["gate"]
            gT_sb = T["gT"]
            gbc_sb = T["gbc"]
            tpair_sb = T["tp"]
            tpair2_sb = T["tp2"]
            eo32 = T["eo32"]
            eodup = T["eod"]
            rsm_sb = T["rsm"]
            xn_sb = T["xnc"]
            t32_sb = T["t32c"]
            x1_sb = T["x1c"]
            xc_sb = [T[f"xc{i}"] for i in range(4)]

            ps_c = [pbank[:, 0:512], pbank[:, 512:1024]]
            ps_y = [pbank[:, 1024:1536], pbank[:, 1536:2048]]
            ps_cont = pbank[0:64, 2048:2560]
            ps_g = pbank[:, 2560:3072]
            ps_trc = [pbank[:, 3072:3136].bitcast(BF16)[:, 0:128],
                      pbank[:, 3584:3648].bitcast(BF16)[:, 0:128]]

            for dc in range(8):
                E.op("pool", lambda g2, dc=dc: g2.dma_start(
                    wenc_sb[:, dc * H:(dc + 1) * H],
                    wenc_in[dc * 128:(dc + 1) * 128, :]), inc=[("ing", 16)])
            for hc in range(16):
                E.op("pool", lambda g2, hc=hc: g2.dma_start(
                    wdec_sb[:, hc * D:(hc + 1) * D],
                    wdec_in[hc * 128:(hc + 1) * 128, :]), inc=[("ing", 16)])
                E.op("pool", lambda g2, hc=hc: g2.dma_start(
                    ws2c_sb[:, hc * M:(hc + 1) * M],
                    ws2c_in[hc * 128:(hc + 1) * 128, :]), inc=[("ing", 16)])
            E.op("pool", lambda g2: g2.dma_start(wc2s_sb[0:64, :], wc2s_in[:, :]),
                 inc=[("ing", 16)])
            E.op("pool", lambda g2: g2.dma_start(wc2s_sb[64:128, :], wc2s_in[:, :]),
                 inc=[("ing", 16)])
            E.op("pool", lambda g2: g2.dma_start(w1_sb[:, :], w1_in[:, :]), inc=[("ing", 16)])
            E.op("pool", lambda g2: g2.dma_start(w2_sb[:, :], w2_in[:, :]), inc=[("ing", 16)])
            E.op("pool", lambda g2: g2.dma_start(sel_sb[:, :], sel_in[:, :]), inc=[("ing", 16)])
            E.op("sp", lambda s: s.dma_start(wr_sb[:, :], wr_in[:, :]), inc=[("in", 16)])
            E.op("sp", lambda s: s.dma_start(
                br_sb[:, :], br_in[0:1, :].partition_broadcast(128)), inc=[("in", 16)])
            E.op("sp", lambda s: s.dma_start(
                lng_sb[:, :], lng_in[0:1, :].partition_broadcast(128)), inc=[("in", 16)])
            E.op("sp", lambda s: s.dma_start(
                lnb_sb[:, :], lnb_in[0:1, :].partition_broadcast(128)), inc=[("in", 16)])
            E.op("sp", lambda s: s.dma_start(
                bdec_bc[:, :], bdec_in[0:1, :].partition_broadcast(128)), inc=[("in", 16)])
            with nc.allow_non_contiguous_dma(reason="one-time tiny bias columns"):
                E.op("sp", lambda s: s.dma_start(
                    bs2c_sb[:, :], bs2c_in[0:1, :].rearrange("o m -> m o")),
                    inc=[("in", 16)])
                E.op("sp", lambda s: s.dma_start(
                    benc_sb[:, :], benc_in[:, :].rearrange("a p -> p a")),
                    inc=[("in", 16)])
                E.op("sp", lambda s: s.dma_start(
                    bc2s_sb[:, :], bc2s_in[:, :].rearrange("a p -> p a")),
                    inc=[("in", 16)])
                E.op("sp", lambda s: s.dma_start(
                    b1_sb[:, :], b1_in[:, :].rearrange("a p -> p a")),
                    inc=[("in", 16)])
                E.op("sp", lambda s: s.dma_start(
                    b2_sb[:, :], b2_in[:, :].rearrange("a p -> p a")),
                    inc=[("in", 16)])
            m_cconst = E.dma_mark("sp", "in")
            m_cw = E.dma_mark("pool", "ing")
            E.op("dve", lambda v: v.tensor_scalar(
                benc_sb[:, :], benc_sb[:, :], 0.5, None, ALU.mult),
                wait=[m_cconst, ("dve", E.n("dve"))], inc=[("dve", 1)])
            E.op("dve", lambda v: v.tensor_scalar(
                bc2s_sb[:, :], bc2s_sb[:, :], 0.5, None, ALU.mult),
                wait=[("dve", E.n("dve"))], inc=[("dve", 1)])
            chalf = E.n("dve")

            r_x1slot = Roll()
            r_xnc = Roll()
            r_pstrc = Roll()
            r_psc = Roll()
            r_spk = Roll()
            r_contw = Roll()
            r_gate = Roll()
            r_gT = Roll()
            r_hread = Roll()
            r_gbc = Roll()
            r_psy = Roll()
            r_smoe = Roll()
            r_dev = Roll()
            r_odma = Roll()

            for g in range(NQG):
                for qt in range(4):
                    qa = g * 4 + qt
                    wb = r_x1slot.back(1)
                    E.op("sp", lambda s, qa=qa, qt=qt: s.dma_start(
                        xc_sb[qt][:, :], x1_scr[qa * 128:(qa + 1) * 128, :]),
                        wait=([("dve", wb)] if (wb and qt == 0) else []),
                        inc=[("in", 16)])
                m_xc = E.dma_mark("sp", "in")
                for i in range(4):
                    ln_stats_act(E, xc_sb[i][:, :], i, wait=([m_xc] if i == 0 else []))
                c_rstd = rstd_from_vars(E, 4)
                for qt in range(4):
                    E.op("dve", lambda v, qt=qt: v.tensor_scalar(
                        xn_sb[:, :], xc_sb[qt][:, :],
                        small[:, 20 + qt:21 + qt], small[:, 16 + qt:17 + qt],
                        ALU.mult, ALU.add),
                        wait=[("dve", c_rstd), ("pe", r_xnc.back(1)),
                              ("dve", E.n("dve"))], inc=[("dve", 1)])
                    ap2 = E.n("dve")
                    for dc in range(8):
                        half = dc % 2
                        E.op("pe", lambda t, dc=dc, half=half: t.transpose(
                            ps_trc[half], xn_sb[:, dc * 128:(dc + 1) * 128], id_sb[:, :]),
                            wait=[("dve", ap2), ("act", r_pstrc.back(2))],
                            inc=[("pe", 1)])
                        trd = E.n("pe")
                        E.op("act", lambda a, dc=dc, qt=qt, half=half: a.activation(
                            xn2T[:, dc * 512 + qt * 128: dc * 512 + qt * 128 + 128],
                            ps_trc[half], AF.Copy),
                            wait=[("pe", trd)], inc=[("act", 1)])
                        r_pstrc.push(E.n("act"))
                    r_xnc.push(E.n("pe"))
                xn2_done = E.n("act")

                for hc in range(16):
                    par = hc % 2
                    wpc = r_psc.back(2)
                    for dc in range(8):
                        E.op("pe", lambda t, hc=hc, dc=dc, par=par: t.matmul(
                            ps_c[par][:, :],
                            wenc_sb[:, dc * H + hc * 128: dc * H + hc * 128 + 128],
                            xn2T[:, dc * 512:(dc + 1) * 512],
                            start=(dc == 0), stop=(dc == 7)),
                            wait=([("act", xn2_done), m_cw,
                                   ("act", wpc[0] if wpc else 0),
                                   ("dve", wpc[1] if wpc else 0)] if dc == 0 else []),
                            inc=([("pe", 1)] if dc == 7 else []))
                    encd = E.n("pe")
                    E.op("act", lambda a, hc=hc, par=par: a.activation(
                        spik[:, hc * 512:(hc + 1) * 512], ps_c[par][:, :], AF.Tanh,
                        scale=0.5, bias=benc_sb[:, hc:hc + 1]),
                        wait=[("pe", encd), ("dve", chalf)], inc=[("act", 1)])
                    tnh = E.n("act")
                    E.op("dve", lambda v, hc=hc: v.tensor_scalar(
                        spik[:, hc * 512:(hc + 1) * 512],
                        spik[:, hc * 512:(hc + 1) * 512],
                        0.5, 0.5, ALU.mult, ALU.add),
                        wait=[("act", tnh), ("dve", E.n("dve"))], inc=[("dve", 1)])
                    r_psc.push((E.n("act"), E.n("dve")))
                spikes_done = E.n("dve")

                wcont = r_contw.back(1)
                for hc in range(16):
                    E.op("pe", lambda t, hc=hc: t.matmul(
                        ps_cont[0:64, :], ws2c_sb[:, hc * M:(hc + 1) * M],
                        spik[:, hc * 512:(hc + 1) * 512],
                        start=(hc == 0), stop=(hc == 15)),
                        wait=([("dve", spikes_done), ("act", wcont[0] if wcont else 0),
                               ("dve", wcont[1] if wcont else 0)] if hc == 0 else []),
                        inc=([("pe", 1)] if hc == 15 else []))
                contd = E.n("pe")
                gcont = pe_guard(E, contd, extra=[m_cconst])
                E.op("dve", lambda v: v.tensor_scalar(
                    cdup[0:64, :], ps_cont[0:64, :], bs2c_sb[0:64, 0:1], None, ALU.add),
                    wait=[("dve", gcont)],
                    inc=[("dve", 1)])
                E.op("dve", lambda v: v.tensor_scalar(
                    cont32[0:64, :], ps_cont[0:64, :], bs2c_sb[0:64, 0:1], None, ALU.add),
                    wait=[("dve", E.n("dve"))], inc=[("dve", 1)])
                r_contw.push((E.n("act"), E.n("dve")))
                cont_done = E.n("dve")
                E.op("pe", lambda t: t.matmul(
                    ps_y[0][64:128, :], id_sb[0:64, 0:64], cdup[0:64, :],
                    start=True, stop=True, tile_position=(0, 64)),
                    wait=[("dve", cont_done), ("dve", r_psy.back(1))],
                    inc=[("pe", 1)])
                dupd = E.n("pe")
                gdup = pe_guard(E, dupd)
                E.op("dve", lambda v: v.tensor_copy(cdup[64:128, :], ps_y[0][64:128, :]),
                     wait=[("dve", gdup)], inc=[("dve", 1)])
                cdup_done = E.n("dve")

                for qt in range(4):
                    E.op("pe", lambda t, qt=qt: t.matmul(
                        ps_g[0:128, 0:8], cont32[:, qt * 128:(qt + 1) * 128],
                        wr_sb[:, :], start=True, stop=True),
                        wait=[("dve", cont_done), m_cconst,
                              ("dve", cdup_done if qt == 0 else r_gate.back(1))],
                        inc=[("pe", 1)])
                    rd = E.n("pe")
                    grd = pe_guard(E, rd)

                    def sg(fn):
                        E.op("dve", fn)
                        dve_pad(E)
                    E.op("dve", lambda v: v.tensor_add(
                        rsm_sb[:, 0:8], ps_g[0:128, 0:8], br_sb[:, 0:8]),
                        wait=[("dve", grd)])
                    dve_pad(E)
                    sg(lambda v: v.reduce_max(rsm_sb[:, 8:9], rsm_sb[:, 0:8],
                                              axis=AX.X, negate=True))
                    E.op("dve", lambda v: v.engine_nop(), inc=[("dve", 1)])
                    E.op("act", lambda a: a.activation(
                        rsm_sb[:, 0:8], rsm_sb[:, 0:8], AF.Exp, bias=rsm_sb[:, 8:9]),
                        wait=[("dve", E.n("dve"))], inc=[("act", 1)])
                    exd = E.n("act")
                    E.op("dve", lambda v: v.reduce_max(rsm_sb[:, 9:10], rsm_sb[:, 0:8],
                                                       axis=AX.X),
                         wait=[("act", exd)])
                    dve_pad(E)
                    sg(lambda v: v.tensor_scalar(stats_sb[:, 0:8], rsm_sb[:, 0:8],
                                                 rsm_sb[:, 9:10], None, ALU.is_ge))
                    sg(lambda v: v.tensor_mul(stats_sb[:, 8:16], rsm_sb[:, 0:8],
                                              stats_sb[:, 0:8]))
                    sg(lambda v: v.tensor_sub(stats_sb[:, 8:16], rsm_sb[:, 0:8],
                                              stats_sb[:, 8:16]))
                    sg(lambda v: v.reduce_max(rsm_sb[:, 10:11], stats_sb[:, 8:16],
                                              axis=AX.X))
                    sg(lambda v: v.tensor_scalar(stats_sb[:, 0:8], rsm_sb[:, 0:8],
                                                 rsm_sb[:, 10:11], None, ALU.is_ge))
                    sg(lambda v: v.tensor_mul(stats_sb[:, 8:16], rsm_sb[:, 0:8],
                                              stats_sb[:, 0:8]))
                    sg(lambda v: v.reduce_sum(rsm_sb[:, 11:12], stats_sb[:, 8:16],
                                              axis=AX.X))
                    sg(lambda v: v.reciprocal(rsm_sb[:, 12:13], rsm_sb[:, 11:12]))
                    sg(lambda v, qt=qt: v.tensor_scalar(
                        gate_sb[:, qt * 8:(qt + 1) * 8], stats_sb[:, 8:16],
                        rsm_sb[:, 12:13], None, ALU.mult))
                    E.op("dve", lambda v: v.engine_nop(), inc=[("dve", 1)])
                    r_gate.push(E.n("dve"))
                    gdone = E.n("dve")
                    E.op("pe", lambda t, qt=qt: t.transpose(
                        ps_trc[qt % 2][0:8, :], gate_sb[:, qt * 8:(qt + 1) * 8],
                        id_sb[:, :]),
                        wait=[("dve", gdone), ("dve", r_gT.back(2))],
                        inc=[("pe", 1)])
                    gtd = E.n("pe")
                    E.op("act", lambda a, qt=qt: a.activation(
                        gT_sb[0:8, qt * 128:(qt + 1) * 128], ps_trc[qt % 2][0:8, :],
                        AF.Copy),
                        wait=[("pe", gtd)], inc=[("act", 1)])
                    r_gT.push(E.n("act"))
                    r_pstrc.push(E.n("act"))
                    r_pstrc.push(E.n("act"))
                gT_done = E.n("dve")

                for i in range(4):
                    hread = r_hread.back(1)
                    wev = (0, 0)
                    for hc in range(8):
                        wof = (i * 8 + hc) * 128
                        E.op("pe", lambda t, wof=wof: t.ldweights(
                            w1_sb[0:64, wof:wof + 128], tile_position=(0, 0)),
                            wait=(([("dve", cdup_done),
                                    ("act", hread[0] if hread else 0),
                                    ("dve", hread[1] if hread else 0),
                                    ("dve", r_psy.back(1))] if hc == 0 else [])
                                  + [("act", wev[0]), ("dve", wev[1])]))
                        E.op("pe", lambda t, wof=wof: t.ldweights(
                            w1_sb[64:128, wof:wof + 128], tile_position=(64, 0)))
                        pc0 = ps_c[0] if hc % 2 == 0 else ps_y[0]
                        pc1 = ps_c[1] if hc % 2 == 0 else ps_y[1]
                        E.op("pe", lambda t, wof=wof, pc0=pc0: t.matmul(
                            pc0[0:128, :], w1_sb[0:64, wof:wof + 128], cdup[0:64, :],
                            start=True, stop=True, tile_position=(0, 0)))
                        E.op("pe", lambda t, wof=wof, pc1=pc1: t.matmul(
                            pc1[0:128, :], w1_sb[64:128, wof:wof + 128], cdup[64:128, :],
                            start=True, stop=True, tile_position=(64, 0)),
                            inc=[("pe", 1)])
                        w1d = E.n("pe")
                        E.op("act", lambda a, i=i, hc=hc, pc1=pc1: a.activation(
                            h_sb[1][:, hc * 512:(hc + 1) * 512], pc1[0:128, :], AF.Relu,
                            bias=b1_sb[:, (2 * i + 1) * 8 + hc:(2 * i + 1) * 8 + hc + 1]),
                            wait=[("pe", w1d), m_cconst], inc=[("act", 1)])
                        E.op("dve", lambda v, i=i, hc=hc, pc0=pc0: v.tensor_scalar(
                            h_sb[0][:, hc * 512:(hc + 1) * 512], pc0[0:128, :],
                            b1_sb[:, (2 * i) * 8 + hc:(2 * i) * 8 + hc + 1],
                            0.0, ALU.add, ALU.max),
                            wait=[("pe", w1d), ("dve", E.n("dve"))], inc=[("dve", 1)])
                        wev = (E.n("act"), E.n("dve"))
                    h_a = E.n("act")
                    h_d = E.n("dve")
                    E.op("pe", lambda t, i=i: t.matmul(
                        ps_g[:, :], sel_sb[:, i * 128:(i + 1) * 128], gT_sb[0:8, :],
                        start=True, stop=True),
                        wait=[("dve", gT_done), ("dve", r_gbc.back(1))],
                        inc=[("pe", 1)])
                    gbcd = E.n("pe")
                    ggbc = pe_guard(E, gbcd)
                    E.op("dve", lambda v: v.tensor_copy(gbc_sb[:, :], ps_g[:, :]),
                         wait=[("dve", ggbc)], inc=[("dve", 1)])
                    gbc_done = E.n("dve")
                    wy = r_psy.back(1)
                    for hc in range(8):
                        o0 = ((2 * i) * 8 + hc) * 64
                        o1 = ((2 * i + 1) * 8 + hc) * 64
                        E.op("pe", lambda t, o0=o0, hc=hc: t.matmul(
                            ps_y[0][0:64, :], w2_sb[:, o0:o0 + 64],
                            h_sb[0][:, hc * 512:(hc + 1) * 512],
                            start=(hc == 0), stop=(hc == 7), tile_position=(0, 0),
                            skip_group_check=True),
                            wait=([("act", h_a), ("dve", h_d),
                                   ("dve", wy if wy else 0)] if hc == 0 else []))
                        E.op("pe", lambda t, o1=o1, hc=hc: t.matmul(
                            ps_y[1][64:128, :], w2_sb[:, o1:o1 + 64],
                            h_sb[1][:, hc * 512:(hc + 1) * 512],
                            start=(hc == 0), stop=(hc == 7), tile_position=(0, 64),
                            skip_group_check=True),
                            inc=([("pe", 1)] if hc == 7 else []))
                    w2d = E.n("pe")
                    r_hread.push((h_a, h_d))
                    E.op("dve", lambda v, i=i: v.scalar_tensor_tensor(
                        tpair_sb[0:64, :], ps_y[0][0:64, :], b2_sb[0:64, i:i + 1],
                        gbc_sb[0:64, :], ALU.add, ALU.mult),
                        wait=[("pe", w2d), ("dve", E.n("dve"))], inc=[("dve", 1)])
                    E.op("dve", lambda v, i=i: v.scalar_tensor_tensor(
                        tpair2_sb[0:64, :], ps_y[1][64:128, :], b2_sb[64:128, i:i + 1],
                        gbc_sb[64:128, :], ALU.add, ALU.mult),
                        wait=[("dve", E.n("dve"))], inc=[("dve", 1)])
                    if i == 0:
                        E.op("dve", lambda v: v.tensor_add(
                            eo32[0:64, :], tpair_sb[0:64, :], tpair2_sb[0:64, :]),
                            wait=[("dve", E.n("dve"))], inc=[("dve", 1)])
                    else:
                        E.op("dve", lambda v: v.tensor_add(
                            eo32[0:64, :], eo32[0:64, :], tpair_sb[0:64, :]),
                            wait=[("dve", E.n("dve"))], inc=[("dve", 1)])
                        E.op("dve", lambda v: v.tensor_add(
                            eo32[0:64, :], eo32[0:64, :], tpair2_sb[0:64, :]),
                            wait=[("dve", E.n("dve"))], inc=[("dve", 1)])
                    r_psy.push(E.n("dve"))
                    r_gbc.push(E.n("dve"))
                E.op("dve", lambda v: v.tensor_copy(eodup[0:64, :], eo32[0:64, :]),
                     wait=[("dve", E.n("dve"))], inc=[("dve", 1)])
                eod = E.n("dve")
                E.op("pe", lambda t: t.matmul(
                    ps_y[0][64:128, :], id_sb[0:64, 0:64], eodup[0:64, :],
                    start=True, stop=True, tile_position=(0, 64)),
                    wait=[("dve", eod), ("dve", r_psy.back(1))], inc=[("pe", 1)])
                eodd = E.n("pe")
                geod = pe_guard(E, eodd)
                E.op("dve", lambda v: v.tensor_copy(eodup[64:128, :], ps_y[0][64:128, :]),
                     wait=[("dve", geod)], inc=[("dve", 1)])
                r_psy.push(E.n("dve"))
                eodup_done = E.n("dve")

                for pi in range(8):
                    wpc = r_psc.back(2)
                    pcs = [ps_c[0], ps_c[1]] if pi % 2 == 0 else [ps_y[0], ps_y[1]]
                    E.op("pe", lambda t, pi=pi: t.ldweights(
                        wc2s_sb[0:64, pi * 128:(pi + 1) * 128], tile_position=(0, 0)),
                        wait=[("dve", eodup_done),
                              ("act", wpc[0] if wpc else 0),
                              ("dve", wpc[1] if wpc else 0)])
                    E.op("pe", lambda t, pi=pi: t.ldweights(
                        wc2s_sb[64:128, (pi + 8) * 128:(pi + 9) * 128],
                        tile_position=(64, 0)))
                    E.op("pe", lambda t, pi=pi, pcs=pcs: t.matmul(
                        pcs[0][0:128, :], wc2s_sb[0:64, pi * 128:(pi + 1) * 128],
                        eodup[0:64, :], start=True, stop=True, tile_position=(0, 0)))
                    E.op("pe", lambda t, pi=pi, pcs=pcs: t.matmul(
                        pcs[1][0:128, :], wc2s_sb[64:128, (pi + 8) * 128:(pi + 9) * 128],
                        eodup[64:128, :], start=True, stop=True, tile_position=(64, 0)),
                        inc=[("pe", 1)])
                    c2d = E.n("pe")
                    for which in range(2):
                        hcv = pi if which == 0 else pi + 8
                        E.op("act", lambda a, hcv=hcv, which=which, pcs=pcs: a.activation(
                            smoe[:, hcv * 512:(hcv + 1) * 512], pcs[which][0:128, :],
                            AF.Tanh, scale=0.5, bias=bc2s_sb[:, hcv:hcv + 1]),
                            wait=[("pe", c2d)], inc=[("act", 1)])
                        t2d = E.n("act")
                        E.op("dve", lambda v, hcv=hcv: v.tensor_scalar(
                            smoe[:, hcv * 512:(hcv + 1) * 512],
                            smoe[:, hcv * 512:(hcv + 1) * 512],
                            0.5, 0.5, ALU.mult, ALU.add),
                            wait=[("act", t2d), ("dve", E.n("dve"))], inc=[("dve", 1)])
                    r_psc.push((E.n("act"), E.n("dve")))
                smoe_done = E.n("dve")

                for qt in range(4):
                    qa = g * 4 + qt
                    slot = qt % 2
                    for half in range(2):
                        par = half
                        wpc = r_psc.back(2)
                        for hc in range(16):
                            E.op("pe", lambda t, qt=qt, half=half, hc=hc, par=par: t.matmul(
                                ps_c[par][:, :],
                                smoe[:, hc * 512 + qt * 128: hc * 512 + qt * 128 + 128],
                                wdec_sb[:, hc * D + half * 512: hc * D + half * 512 + 512],
                                start=(hc == 0), stop=(hc == 15)),
                                wait=([("dve", smoe_done),
                                       ("act", wpc[0] if wpc else 0),
                                       ("dve", wpc[1] if wpc else 0)] if hc == 0 else []),
                                inc=([("pe", 1)] if hc == 15 else []))
                        dcd = E.n("pe")
                        gdc = pe_guard(E, dcd, extra=[m_cconst])
                        E.op("dve", lambda v, half=half, par=par: v.tensor_add(
                            t32_sb[:, half * 512:(half + 1) * 512], ps_c[par][:, :],
                            bdec_bc[:, half * 512:(half + 1) * 512]),
                            wait=[("dve", gdc)],
                            inc=[("dve", 1)])
                        r_psc.push((E.n("act"), E.n("dve")))
                    ln_stats_act(E, t32_sb[:, :], 0, wait=[("dve", E.n("dve"))])
                    r1 = rstd_from_vars(E, 1)
                    E.op("dve", lambda v: v.tensor_scalar(
                        t32_sb[:, :], t32_sb[:, :],
                        small[:, 20:21], small[:, 16:17], ALU.mult, ALU.add),
                        wait=[("dve", r1)], inc=[("dve", 1)])
                    E.op("dve", lambda v: v.tensor_mul(t32_sb[:, :], t32_sb[:, :],
                                                       lng_sb[:, :]),
                         wait=[m_cconst, ("dve", E.n("dve"))],
                         inc=[("dve", 1)])
                    E.op("dve", lambda v: v.tensor_add(t32_sb[:, :], t32_sb[:, :],
                                                       lnb_sb[:, :]),
                         wait=[("dve", E.n("dve"))], inc=[("dve", 1)])
                    ln_stats_act(E, t32_sb[:, :], 0, wait=[("dve", E.n("dve"))])
                    r2 = rstd_from_vars(E, 1)
                    E.op("dve", lambda v: v.tensor_scalar(
                        t32_sb[:, :], t32_sb[:, :],
                        small[:, 20:21], small[:, 16:17], ALU.mult, ALU.add),
                        wait=[("dve", r2)], inc=[("dve", 1)])
                    E.op("dve", lambda v, qt=qt: v.tensor_add(
                        x1_sb[:, :], t32_sb[:, :], xc_sb[qt][:, :]),
                        wait=(([r_odma.back(1)] if r_odma.back(1) else [])
                              + [("dve", E.n("dve"))]),
                        inc=[("dve", 1)])
                    fin = E.n("dve")
                    r_x1slot.push(fin)
                    E.op("pool", lambda p, qa=qa: p.dma_start(
                        out_hbm[qa * 128:(qa + 1) * 128, :], x1_sb[:, :]),
                        wait=[("dve", fin)], inc=[("stg", 16)])
                    r_odma.push(E.dma_mark("pool", "stg"))

    def section(name):
        def f(eng):
            E = Em(name, eng, sems)
            program(E)
        return f

    block.tensor(section("pe"))
    block.scalar(section("act"))
    block.vector(section("dve"))
    block.gpsimd(section("pool"))
    block.sync(section("sp"))

    ostk.close()
    return nc



def _consts(S):
    inv = 1.0 / (10000.0 ** (np.arange(0, DH, 2, dtype=np.float64) / DH))
    t = np.arange(S, dtype=np.float64)
    freq_row = inv[np.arange(128) % 32]
    ang = freq_row[:, None] * t[None, :]
    cos_t = np.cos(ang).astype(ml_dtypes.bfloat16)
    sin_t = np.sin(ang).astype(ml_dtypes.bfloat16)
    ident = np.eye(128, dtype=ml_dtypes.bfloat16)
    perm = np.zeros((128, 128), dtype=np.float32)
    for m in range(128):
        base = (m // 64) * 64
        mm = m % 64
        if mm < 32:
            perm[base + mm + 32, m] = -1.0
        else:
            perm[base + mm - 32, m] = 1.0
    perm = perm.astype(ml_dtypes.bfloat16)
    kk, qq = np.meshgrid(np.arange(128), np.arange(128), indexing="ij")
    m1 = np.where(kk > qq, 0.0, 1.0).astype(np.float32)
    mask2 = np.concatenate([m1, m1], axis=1).astype(ml_dtypes.bfloat16)
    el = np.zeros((65, 128), dtype=np.float32)
    el[0, 0:64] = 1.0
    el[64, 64:128] = 1.0
    el = el.astype(ml_dtypes.bfloat16)
    elr = np.zeros((128, 128), dtype=np.float32)
    elr[0, 0:64] = 1.0
    elr[64, 64:128] = 1.0
    elr = elr.astype(ml_dtypes.bfloat16)
    ones1 = np.ones((128, 2), dtype=ml_dtypes.bfloat16)
    onesr = np.ones((1, 128), dtype=ml_dtypes.bfloat16)
    sel = np.zeros((8, 512), dtype=np.float32)
    for i in range(4):
        sel[2 * i, i * 128: i * 128 + 64] = 1.0
        sel[2 * i + 1, i * 128 + 64: i * 128 + 128] = 1.0
    sel = sel.astype(ml_dtypes.bfloat16)
    return dict(ident=ident, perm=perm, cos_t=cos_t, sin_t=sin_t, mask2=mask2,
                el=el, elr=elr, ones1=ones1, onesr=onesr, sel=sel)


def _prep_inputs(S, L0_attn_w_qkv, L0_attn_w_proj, L0_attn_b_proj,
                 L0_ffn_W_enc, L0_ffn_b_enc, L0_ffn_W_s2c, L0_ffn_b_s2c,
                 L0_ffn_Wr, L0_ffn_br, L0_ffn_W1, L0_ffn_b1, L0_ffn_W2, L0_ffn_b2,
                 L0_ffn_W_c2s, L0_ffn_b_c2s, L0_ffn_W_dec, L0_ffn_b_dec,
                 L0_ffn_ln_g, L0_ffn_ln_b):
    def lay(a):
        return np.ascontiguousarray(np.asarray(a, dtype=np.float32))

    w1 = lay(L0_ffn_W1)
    w1p = np.zeros((128, 4096), dtype=np.float32)
    for i in range(4):
        for hc in range(8):
            w1p[0:64, (i * 8 + hc) * 128:(i * 8 + hc) * 128 + 128] = \
                w1[2 * i, :, hc * 128:(hc + 1) * 128]
            w1p[64:128, (i * 8 + hc) * 128:(i * 8 + hc) * 128 + 128] = \
                w1[2 * i + 1, :, hc * 128:(hc + 1) * 128]
    w2 = lay(L0_ffn_W2)
    w2p = np.zeros((128, 4096), dtype=np.float32)
    for ee in range(8):
        for hc in range(8):
            w2p[:, (ee * 8 + hc) * 64:(ee * 8 + hc) * 64 + 64] = \
                w2[ee, hc * 128:(hc + 1) * 128, :]
    b1r = lay(L0_ffn_b1).reshape(8, 8, 128).reshape(64, 128)
    b2 = lay(L0_ffn_b2)
    b2r = np.zeros((4, 128), dtype=np.float32)
    for i in range(4):
        b2r[i, 0:64] = b2[2 * i]
        b2r[i, 64:128] = b2[2 * i + 1]

    return dict(
        wqkv=lay(L0_attn_w_qkv), wproj=lay(L0_attn_w_proj),
        bproj=lay(L0_attn_b_proj).reshape(1, D),
        wenc=lay(L0_ffn_W_enc), benc=lay(L0_ffn_b_enc).reshape(16, 128),
        ws2c=lay(L0_ffn_W_s2c), bs2c=lay(L0_ffn_b_s2c).reshape(1, M),
        wr=lay(L0_ffn_Wr), br=lay(L0_ffn_br).reshape(1, NE),
        w1=w1p, b1=b1r, w2=w2p, b2=b2r,
        wc2s=lay(L0_ffn_W_c2s), bc2s=lay(L0_ffn_b_c2s).reshape(16, 128),
        wdec=lay(L0_ffn_W_dec), bdec=lay(L0_ffn_b_dec).reshape(1, D),
        lng=lay(L0_ffn_ln_g).reshape(1, D), lnb=lay(L0_ffn_ln_b).reshape(1, D),
        **_consts(S),
    )


def kernel(L0_attn_x, L0_attn_w_qkv, L0_attn_w_proj, L0_attn_b_proj,
           L0_ffn_W_enc, L0_ffn_b_enc, L0_ffn_W_s2c, L0_ffn_b_s2c,
           L0_ffn_Wr, L0_ffn_br, L0_ffn_W1, L0_ffn_b1, L0_ffn_W2, L0_ffn_b2,
           L0_ffn_W_c2s, L0_ffn_b_c2s, L0_ffn_W_dec, L0_ffn_b_dec,
           L0_ffn_ln_g, L0_ffn_ln_b):
    x = np.ascontiguousarray(np.asarray(L0_attn_x, dtype=np.float32))
    B, S, _ = x.shape
    base = _prep_inputs(S, L0_attn_w_qkv, L0_attn_w_proj, L0_attn_b_proj,
                        L0_ffn_W_enc, L0_ffn_b_enc, L0_ffn_W_s2c, L0_ffn_b_s2c,
                        L0_ffn_Wr, L0_ffn_br, L0_ffn_W1, L0_ffn_b1, L0_ffn_W2,
                        L0_ffn_b2, L0_ffn_W_c2s, L0_ffn_b_c2s, L0_ffn_W_dec,
                        L0_ffn_b_dec, L0_ffn_ln_g, L0_ffn_ln_b)
    in_maps = []
    for b in range(B):
        m = dict(base)
        m["x"] = np.ascontiguousarray(x[b])
        in_maps.append(m)
    import os as _os
    nc = build_program(S, phases=int(_os.environ.get("KPHASES", "3")))
    res = run_bass_kernel_spmd(nc, in_maps, core_ids=list(range(B)))
    out = np.stack([np.asarray(r["out"], dtype=np.float32) for r in res.results],
                   axis=0)
    return out


# version 02
# speedup vs baseline: 1.2888x; 1.2888x over previous
import numpy as np
import ml_dtypes
from contextlib import ExitStack

import concourse.bass as bass
import concourse.mybir as mybir
from concourse.bass_utils import run_bass_kernel_spmd

F32 = mybir.dt.float32
BF16 = mybir.dt.bfloat16
AF = mybir.ActivationFunctionType
ALU = mybir.AluOpType
AX = mybir.AxisListType

D = 1024
DH = 64
NPAIR = 8
H = 2048
M = 64
NE = 8
EPS = 1e-5
NEG = -60000.0

ENG_ORDER = ("pe", "act", "dve", "pool", "sp")


class Roll:

    def __init__(self):
        self.v = []

    def push(self, m):
        self.v.append(m)

    def back(self, k):
        return self.v[-k] if len(self.v) >= k else 0


class Em:
    def __init__(self, name, eng, sems):
        self.name = name
        self.eng = eng
        self.sems = sems
        self.cnt = {k: 0 for k in sems}

    def n(self, s):
        return self.cnt[s]

    def op(self, engine, fn, wait=(), inc=()):
        if self.name == engine:
            for s, v in wait:
                if v > 0:
                    self.eng.wait_ge(self.sems[s], v)
            ins = fn(self.eng)
            for s, k in inc:
                ins.then_inc(self.sems[s], k)
        for s, k in inc:
            self.cnt[s] += k

    def wait(self, engine, *sv):
        if self.name == engine:
            for s, v in sv:
                if v > 0:
                    self.eng.wait_ge(self.sems[s], v)

    def dma_mark(self, queue_eng, queue_sem):
        total = self.n(queue_sem)
        self.op(queue_eng, lambda e: e.nop(), wait=[(queue_sem, total)],
                inc=[(queue_eng, 1)])
        return (queue_eng, self.n(queue_eng))

    def barrier(self):
        for en in ENG_ORDER:
            self.op(en, lambda e: e.nop(), inc=[(en, 1)])
        targets = [(s, self.n(s)) for s in ("pe", "act", "dve", "pool", "sp",
                                            "in", "ing", "st", "stg")]
        for en in ENG_ORDER:
            self.wait(en, *targets)


def build_program(S, phases=3):
    import os as _os
    KAPARTS = int(_os.environ.get("KAPARTS", "255"))
    KBPARTS = int(_os.environ.get("KBPARTS", "255"))
    KBEPI = int(_os.environ.get("KBEPI", "15"))
    KNPAIR = int(_os.environ.get("KNPAIR", "8"))
    KBPROJ = int(_os.environ.get("KBPROJ", "63"))
    NQG = S // 512
    nc = bass.Bass("TRN2", target_bir_lowering=False, detect_race_conditions=False)

    x_in = nc.dram_tensor("x", [S, D], F32, kind="ExternalInput").ap()
    wqkv_in = nc.dram_tensor("wqkv", [D, 3 * D], F32, kind="ExternalInput").ap()
    wproj_in = nc.dram_tensor("wproj", [D, D], F32, kind="ExternalInput").ap()
    bproj_in = nc.dram_tensor("bproj", [1, D], F32, kind="ExternalInput").ap()
    wenc_in = nc.dram_tensor("wenc", [D, H], F32, kind="ExternalInput").ap()
    benc_in = nc.dram_tensor("benc", [16, 128], F32, kind="ExternalInput").ap()
    ws2c_in = nc.dram_tensor("ws2c", [H, M], F32, kind="ExternalInput").ap()
    bs2c_in = nc.dram_tensor("bs2c", [1, M], F32, kind="ExternalInput").ap()
    wr_in = nc.dram_tensor("wr", [M, NE], F32, kind="ExternalInput").ap()
    br_in = nc.dram_tensor("br", [1, NE], F32, kind="ExternalInput").ap()
    w1_in = nc.dram_tensor("w1", [128, 4096], F32, kind="ExternalInput").ap()
    b1_in = nc.dram_tensor("b1", [64, 128], F32, kind="ExternalInput").ap()
    w2_in = nc.dram_tensor("w2", [128, 4096], F32, kind="ExternalInput").ap()
    b2_in = nc.dram_tensor("b2", [4, 128], F32, kind="ExternalInput").ap()
    wc2s_in = nc.dram_tensor("wc2s", [M, H], F32, kind="ExternalInput").ap()
    bc2s_in = nc.dram_tensor("bc2s", [16, 128], F32, kind="ExternalInput").ap()
    wdec_in = nc.dram_tensor("wdec", [H, D], F32, kind="ExternalInput").ap()
    bdec_in = nc.dram_tensor("bdec", [1, D], F32, kind="ExternalInput").ap()
    lng_in = nc.dram_tensor("lng", [1, D], F32, kind="ExternalInput").ap()
    lnb_in = nc.dram_tensor("lnb", [1, D], F32, kind="ExternalInput").ap()
    ident_in = nc.dram_tensor("ident", [128, 128], BF16, kind="ExternalInput").ap()
    perm_in = nc.dram_tensor("perm", [128, 128], BF16, kind="ExternalInput").ap()
    cos_in = nc.dram_tensor("cos_t", [128, S], BF16, kind="ExternalInput").ap()
    sin_in = nc.dram_tensor("sin_t", [128, S], BF16, kind="ExternalInput").ap()
    mask_in = nc.dram_tensor("mask2", [128, 256], BF16, kind="ExternalInput").ap()
    el_in = nc.dram_tensor("el", [65, 128], BF16, kind="ExternalInput").ap()
    elr_in = nc.dram_tensor("elr", [128, 128], BF16, kind="ExternalInput").ap()
    ones1_in = nc.dram_tensor("ones1", [128, 2], BF16, kind="ExternalInput").ap()
    onesr_in = nc.dram_tensor("onesr", [1, 128], BF16, kind="ExternalInput").ap()
    sel_in = nc.dram_tensor("sel", [8, 512], BF16, kind="ExternalInput").ap()

    out_hbm = nc.dram_tensor("out", [S, D], F32, kind="ExternalOutput").ap()
    q_scr = nc.dram_tensor("q_scr", [NQG, 8, 128, 512], BF16, kind="Internal").ap()
    x1_scr = nc.dram_tensor("x1_scr", [S, D], F32, kind="Internal").ap()

    ostk = ExitStack()
    oe = ostk.enter_context

    id_sb = nc.alloc_sbuf_tensor("id_sb", [128, 128], BF16).ap()
    onesr_sb = nc.alloc_sbuf_tensor("onesr_sb", [1, 128], BF16).ap()
    c_col = nc.alloc_sbuf_tensor("c_col", [128, 2], F32).ap()[:, 0:1]
    small = nc.alloc_sbuf_tensor("small", [128, 64], F32).ap()
    stats_sb = nc.alloc_sbuf_tensor("stats_sb", [128, 48], F32).ap()

    pbank = nc.alloc_psum_tensor("pbank", [128, 4096], F32).ap()

    cur = [0]
    hold = [None]

    def carve_bf(n, p=128):
        c0 = cur[0]
        cur[0] = (cur[0] + n + 31) & ~31
        if hold[0] is None:
            return None
        return hold[0][0:p, c0:c0 + n]

    def carve_f32(n, p=128):
        c0 = cur[0]
        cur[0] = (cur[0] + 2 * n + 31) & ~31
        if hold[0] is None:
            return None
        return hold[0][:, c0:c0 + 2 * n].bitcast(F32)[0:p, 0:n]

    sems = {}
    for k in ("in", "ing", "st", "stg", "pe", "act", "dve", "pool", "sp"):
        sems[k] = oe(nc.semaphore(f"s_{k}"))

    block = oe(nc.Block())

    T = {}

    def mk_layout():
        cur[0] = 0
        T["kf"] = carve_bf(NPAIR * S)
        T["vres"] = carve_bf((S // 128) * 16 * 65)
        ab_end = cur[0]
        T["wqkv"] = carve_bf(8 * 3 * D)
        T["xnT"] = carve_bf(8 * 512)
        T["cos"] = carve_bf(512)
        T["sin"] = carve_bf(512)
        T["perm"] = carve_bf(128)
        T["ones1"] = carve_bf(2)
        T["qcb0"] = carve_bf(512)
        T["qcb1"] = carve_bf(512)
        T["rotb"] = carve_bf(512)
        T["t1b"] = carve_bf(512)
        T["rop0"] = carve_bf(512)
        T["rop1"] = carve_bf(512)
        T["xna"] = carve_bf(D)
        for i in range(4):
            T[f"xa{i}"] = carve_bf(D)
        a_end = cur[0]
        cur[0] = ab_end
        T["q"] = carve_bf(8 * 512)
        T["wproj"] = carve_bf(8 * D)
        T["p0"] = carve_bf(1024)
        T["p1"] = carve_bf(1024)
        T["l"] = carve_bf(512)
        T["elr"] = carve_bf(128)
        T["rl"] = carve_f32(512)
        T["rlb"] = carve_f32(512, p=64)
        T["tb"] = carve_bf(512, p=64)
        T["pv0"] = carve_f32(512)
        T["pv1"] = carve_f32(512)
        T["attn"] = carve_bf(8 * 512)
        T["mask"] = carve_bf(256)
        T["bproj_bc"] = carve_f32(D)
        T["t32b"] = carve_f32(D)
        T["x1b"] = carve_f32(D)
        T["xb0"] = carve_f32(D)
        T["xb1"] = carve_f32(D)
        b_end = cur[0]
        cur[0] = 0
        T["wenc"] = carve_bf(8 * H)
        T["wdec"] = carve_bf(16 * D)
        T["ws2c"] = carve_bf(16 * M)
        T["wc2s"] = carve_bf(H)
        T["w1"] = carve_bf(4096)
        T["w2"] = carve_bf(4096)
        T["wr"] = carve_f32(8, p=64)
        T["sel"] = carve_bf(512, p=8)
        T["benc"] = carve_f32(16)
        T["bc2s"] = carve_f32(16)
        T["b1"] = carve_f32(64)
        T["b2"] = carve_f32(4)
        T["bs2c"] = carve_f32(2, p=64)
        T["br"] = carve_f32(8)
        T["lng"] = carve_f32(D)
        T["lnb"] = carve_f32(D)
        T["bdec_bc"] = carve_f32(D)
        T["xn2T"] = carve_bf(8 * 512)
        T["spik"] = carve_bf(16 * 512)
        T["smoe"] = carve_bf(16 * 512)
        T["cdup"] = carve_bf(512)
        T["cont32"] = carve_f32(512, p=64)
        T["h0"] = carve_bf(8 * 512)
        T["h1"] = carve_bf(8 * 512)
        T["gate"] = carve_bf(32)
        T["gT"] = carve_bf(512, p=8)
        T["gbc"] = carve_f32(512)
        T["tp"] = carve_f32(512)
        T["tp2"] = carve_f32(512, p=64)
        T["eo32"] = carve_f32(512, p=64)
        T["eod"] = carve_bf(512)
        T["rsm"] = carve_f32(16)
        T["xnc"] = carve_bf(D)
        T["t32c"] = carve_f32(D)
        T["x1c"] = carve_f32(D)
        for i in range(4):
            T[f"xc{i}"] = carve_f32(D)
        c_end = cur[0]
        return max(a_end, b_end, c_end)

    need = mk_layout()
    arena = nc.alloc_sbuf_tensor("arena", [128, need + 16], BF16).ap()
    hold[0] = arena
    mk_layout()
    T["bs2c"] = T["bs2c"][:, 0:1]
    dly_sb = nc.alloc_sbuf_tensor("dly_sb", [128, 1024], BF16).ap().bitcast(F32)
    dump_sb = nc.alloc_sbuf_tensor("dump_sb", [128, 512], BF16).ap()


    def ln_stats_act(E, src_ap, i, wait=()):
        for hf in range(2):
            E.op("act", lambda a, hf=hf: a.activation(
                dump_sb[:, :], src_ap[:, hf * 512:(hf + 1) * 512], AF.Copy,
                accum_out=stats_sb[:, 12 + 2 * i + hf:13 + 2 * i + hf]),
                wait=(list(wait) if hf == 0 else []), inc=[("act", 1)])
            E.op("act", lambda a, hf=hf: a.activation(
                dump_sb[:, :], src_ap[:, hf * 512:(hf + 1) * 512], AF.Square,
                accum_out=stats_sb[:, 24 + 2 * i + hf:25 + 2 * i + hf]),
                inc=[("act", 1)])
        am = E.n("act")
        E.op("dve", lambda v, i=i: v.tensor_add(
            stats_sb[:, 32 + i:33 + i], stats_sb[:, 12 + 2 * i:13 + 2 * i],
            stats_sb[:, 13 + 2 * i:14 + 2 * i]), wait=[("act", am)])
        dve_pad(E)
        E.op("dve", lambda v, i=i: v.tensor_add(
            stats_sb[:, 36 + i:37 + i], stats_sb[:, 24 + 2 * i:25 + 2 * i],
            stats_sb[:, 25 + 2 * i:26 + 2 * i]))
        dve_pad(E)
        E.op("dve", lambda v, i=i: v.tensor_scalar(
            small[:, 8 + 2 * i:9 + 2 * i], stats_sb[:, 32 + i:33 + i],
            1.0 / D, None, ALU.mult))
        dve_pad(E)
        E.op("dve", lambda v, i=i: v.tensor_mul(
            stats_sb[:, 20 + i:21 + i], small[:, 8 + 2 * i:9 + 2 * i],
            small[:, 8 + 2 * i:9 + 2 * i]))
        dve_pad(E)
        E.op("dve", lambda v, i=i: v.scalar_tensor_tensor(
            small[:, 9 + 2 * i:10 + 2 * i], stats_sb[:, 36 + i:37 + i], 1.0 / D,
            stats_sb[:, 20 + i:21 + i], ALU.mult, ALU.subtract))
        dve_pad(E)
        E.op("dve", lambda v: v.engine_nop(), inc=[("dve", 1)])
        return E.n("dve")

    def dve_pad(E):
        E.op("dve", lambda v: v.tensor_scalar(dly_sb[:, 0:64], dly_sb[:, 0:64],
                                              1.0, None, ALU.mult))

    def rsqrt_newton(E, v_ap, out_ap, t1_ap, t2_ap, n_iter=3):
        I32 = mybir.dt.int32

        def ss(fn):
            E.op("dve", fn)
            dve_pad(E)

        dve_pad(E)
        ss(lambda e: e.tensor_scalar(t1_ap.bitcast(I32), v_ap.bitcast(I32), 1, None,
                                     ALU.logical_shift_right))
        ss(lambda e: e.tensor_scalar(t2_ap.bitcast(I32), t1_ap.bitcast(I32),
                                     -1, None, ALU.bitwise_xor))
        ss(lambda e: e.tensor_scalar(out_ap.bitcast(I32), t2_ap.bitcast(I32),
                                     0x5F3759E0, None, ALU.add))
        for _ in range(n_iter):
            ss(lambda e: e.tensor_mul(t1_ap, out_ap, out_ap))
            ss(lambda e: e.tensor_mul(t1_ap, t1_ap, v_ap))
            ss(lambda e: e.tensor_scalar(t2_ap, t1_ap, -0.5, 1.5, ALU.mult, ALU.add))
            ss(lambda e: e.tensor_mul(out_ap, out_ap, t2_ap))
        E.op("dve", lambda v: v.engine_nop(), wait=[], inc=[("dve", 1)])

    def rstd_from_vars(E, n):
        if n == 1:
            var_ap = small[:, 9:10]
            mean_ap = small[:, 8:9]
        else:
            var_ap = small[:, 8:8 + 2 * n].rearrange("p (a b) -> p a b", b=2)[:, :, 1]
            mean_ap = small[:, 8:8 + 2 * n].rearrange("p (a b) -> p a b", b=2)[:, :, 0]
        E.op("dve", lambda v: v.tensor_scalar(
            small[:, 16:16 + n], var_ap,
            EPS, None, ALU.add), wait=[("dve", E.n("dve"))], inc=[("dve", 1)])
        rsqrt_newton(E, small[:, 16:16 + n], small[:, 20:20 + n],
                     stats_sb[:, 0:n], stats_sb[:, 4:4 + n])
        E.op("dve", lambda v: v.scalar_tensor_tensor(
            small[:, 16:16 + n], mean_ap,
            -1.0, small[:, 20:20 + n], ALU.mult, ALU.mult),
            wait=[("dve", E.n("dve"))], inc=[("dve", 1)])
        return E.n("dve")

    def pe_guard(E, pe_val, extra=()):
        E.op("dve", lambda v: v.tensor_scalar(dly_sb[:, :], dly_sb[:, :], 1.0,
                                              None, ALU.mult),
             wait=[("pe", pe_val)] + list(extra) + [("dve", E.n("dve"))],
             inc=[("dve", 1)])
        return E.n("dve")


    def program(E):
        E.op("sp", lambda s: s.dma_start(id_sb[:, :], ident_in[:, :]), inc=[("in", 16)])
        E.op("sp", lambda s: s.dma_start(onesr_sb[:, :], onesr_in[:, :]), inc=[("in", 16)])
        E.op("dve", lambda v: v.memset(dly_sb[:, :], 0.0), inc=[("dve", 1)])

        phase_a(E, T["kf"], T["vres"])
        E.barrier()
        if phases >= 2:
            phase_b(E, T["kf"], T["vres"])
            E.barrier()
        if phases >= 3:
            phase_c(E)
        E.wait("sp", ("st", E.n("st")), ("stg", E.n("stg")))


    def phase_a(E, kf, vres):
        if True:
            wqkv_sb = T["wqkv"]
            xnT = T["xnT"]
            cos_sb = T["cos"]
            sin_sb = T["sin"]
            perm_sb = T["perm"]
            ones1_sb = T["ones1"]
            qcb = [T["qcb0"], T["qcb1"]]
            rotb = T["rotb"]
            t1b = T["t1b"]
            rop = [T["rop0"], T["rop1"]]
            xn_sb = T["xna"]
            xa_sb = [T[f"xa{i}"] for i in range(4)]

            ps_qkv = [pbank[:, 0:512], pbank[:, 512:1024]]
            ps_rot = [pbank[:, 1024:1536], pbank[:, 1536:2048]]
            ps_tr = [pbank[:, 2048:2112].bitcast(BF16)[:, 0:128],
                     pbank[:, 2560:2624].bitcast(BF16)[:, 0:128]]
            ps_v = [pbank[:, 3072:3584], pbank[:, 3584:4096]]

            E.op("sp", lambda s: s.dma_start(perm_sb[:, :], perm_in[:, :]), inc=[("in", 16)])
            E.op("sp", lambda s: s.dma_start(ones1_sb[:, :], ones1_in[:, :]), inc=[("in", 16)])
            m_aconst = E.dma_mark("sp", "in")
            for dc in range(8):
                E.op("pool", lambda g, dc=dc: g.dma_start(
                    wqkv_sb[:, dc * 3 * D:(dc + 1) * 3 * D],
                    wqkv_in[dc * 128:(dc + 1) * 128, :]), inc=[("ing", 16)])
            m_wq = E.dma_mark("pool", "ing")
            E.op("dve", lambda v: v.memset(
                vres[:, :].rearrange("p (k o) -> p k o", o=65)[:, :, 64], 1.0),
                inc=[("dve", 1)])
            E.op("dve", lambda v: v.memset(small[:, 0:8], 0.0),
                 wait=[("dve", E.n("dve"))], inc=[("dve", 1)])

            r_psqkv = Roll()
            r_psrot = Roll()
            r_qcb = Roll()
            r_rop = Roll()
            r_pstr = Roll()
            r_xslot = Roll()
            r_xn = Roll()
            r_rotb = Roll()
            r_psv = Roll()
            r_nrm = Roll()
            r_nrmpe = Roll()
            tabs_prev = Roll()

            for g in range(NQG):
                if not (KAPARTS & 1):
                    break
                war = tabs_prev.back(1)
                E.op("sp", lambda s, g=g: s.dma_start(
                    cos_sb[:, :], cos_in[:, g * 512:(g + 1) * 512]),
                    wait=[("dve", war)], inc=[("in", 16)])
                E.op("sp", lambda s, g=g: s.dma_start(
                    sin_sb[:, :], sin_in[:, g * 512:(g + 1) * 512]), inc=[("in", 16)])
                m_tab = E.dma_mark("sp", "in")

                for qt in range(4):
                    qa = g * 4 + qt
                    wb = r_xslot.back(4)
                    E.op("pool", lambda p, qa=qa, qt=qt: p.dma_start(
                        xa_sb[qt][:, :], x_in[qa * 128:(qa + 1) * 128, :]),
                        wait=(([("dve", wb[1]), ("dve", tabs_prev.back(1)),
                                ("pe", r_nrmpe.back(1))]
                               if wb else []) if qt == 0 else []),
                        inc=[("ing", 16)])
                m_xa = E.dma_mark("pool", "ing")
                if not (KAPARTS & 16):
                    continue
                for i in range(4):
                    ln_stats_act(E, xa_sb[i][:, :], i, wait=([m_xa] if i == 0 else []))
                if not (KAPARTS & 32):
                    continue
                rstd_done = rstd_from_vars(E, 4)

                if not (KAPARTS & 64):
                    continue
                for qt in range(4):
                    E.op("dve", lambda v, qt=qt: v.tensor_scalar(
                        xn_sb[:, :], xa_sb[qt][:, :],
                        small[:, 20 + qt:21 + qt], small[:, 16 + qt:17 + qt],
                        ALU.mult, ALU.add),
                        wait=[("dve", rstd_done), ("pe", r_xn.back(1)),
                              ("dve", E.n("dve"))],
                        inc=[("dve", 1)])
                    apply_done = E.n("dve")
                    for dc in range(8 if (KAPARTS & 128) else 0):
                        half = dc % 2
                        E.op("pe", lambda t, dc=dc, half=half: t.transpose(
                            ps_tr[half], xn_sb[:, dc * 128:(dc + 1) * 128], id_sb[:, :]),
                            wait=[("dve", apply_done), m_aconst,
                                  ("act", r_pstr.back(2)),
                                  ("dve", r_nrm.back(1))],
                            inc=[("pe", 1)])
                        trd = E.n("pe")
                        E.op("act", lambda a, dc=dc, qt=qt, half=half: a.activation(
                            xnT[:, dc * 512 + qt * 128: dc * 512 + qt * 128 + 128],
                            ps_tr[half], AF.Copy),
                            wait=[("pe", trd)], inc=[("act", 1)])
                        r_pstr.push(E.n("act"))
                    r_xn.push(E.n("pe"))
                    r_xslot.push((0, E.n("dve")))
                xnt_done_a = E.n("act")
                xnt_done = E.n("dve")

                for fc in range(16 if (KAPARTS & 2) else 0):
                    par = fc % 2
                    wq_war = r_psqkv.back(2)
                    for dc in range(8):
                        E.op("pe", lambda t, fc=fc, dc=dc, par=par: t.matmul(
                            ps_qkv[par][:, :],
                            wqkv_sb[:, dc * 3 * D + fc * 128: dc * 3 * D + fc * 128 + 128],
                            xnT[:, dc * 512:(dc + 1) * 512],
                            start=(dc == 0), stop=(dc == 7)),
                            wait=([("act", xnt_done_a), m_wq,
                                   ("act", wq_war[0] if wq_war else 0),
                                   ("dve", wq_war[1] if wq_war else 0)] if dc == 0 else []),
                            inc=([("pe", 1)] if dc == 7 else []))
                    qkv_done = E.n("pe")
                    E.op("act", lambda a, par=par: a.activation(
                        qcb[par][:, :], ps_qkv[par][:, :], AF.Copy),
                        wait=[("pe", qkv_done), ("dve", r_qcb.back(2))],
                        inc=[("act", 1)])
                    qc_done = E.n("act")
                    E.op("pe", lambda t, par=par: t.matmul(
                        ps_rot[par][:, :], perm_sb[:, :], qcb[par][:, :],
                        start=True, stop=True),
                        wait=[("act", qc_done), m_aconst,
                              ("act", r_psrot.back(2)), ("dve", r_nrm.back(2))],
                        inc=[("pe", 1)])
                    rot_done = E.n("pe")
                    E.op("act", lambda a, par=par: a.activation(
                        rotb[:, :], ps_rot[par][:, :], AF.Copy),
                        wait=[("pe", rot_done), ("dve", r_rotb.back(1)),
                              ("pe", r_nrmpe.back(1))],
                        inc=[("act", 1)])
                    r_psrot.push(E.n("act"))
                    rotb_done = E.n("act")
                    E.op("dve", lambda v, par=par: v.tensor_mul(
                        t1b[:, :], qcb[par][:, :], cos_sb[:, :]),
                        wait=[m_tab, ("act", qc_done), ("dve", E.n("dve"))],
                        inc=[("dve", 1)])
                    r_qcb.push(E.n("dve"))
                    E.op("dve", lambda v: v.tensor_mul(
                        rotb[:, :], rotb[:, :], sin_sb[:, :]),
                        wait=[("act", rotb_done), ("dve", E.n("dve"))], inc=[("dve", 1)])
                    if fc < 8:
                        dest = rop[par][:, :]
                        wm = r_rop.back(2)
                        dwait = [wm] if wm else []
                    else:
                        dest = kf[:, (fc - 8) * S + g * 512:(fc - 8) * S + g * 512 + 512]
                        dwait = []
                    E.op("dve", lambda v, dest=dest: v.tensor_add(
                        dest, t1b[:, :], rotb[:, :]),
                        wait=dwait + [("dve", E.n("dve"))], inc=[("dve", 1)])
                    rope_done = E.n("dve")
                    E.op("dve", lambda v, dest=dest: v.tensor_mul(rotb[:, :], dest, dest),
                         wait=[("dve", E.n("dve"))], inc=[("dve", 1)])
                    sq_done = E.n("dve")
                    r_rotb.push(E.n("dve"))
                    E.op("pe", lambda t, par=par: t.matmul(
                        ps_rot[par][0:1, :], ones1_sb[:, 0:1], rotb[:, :],
                        start=True, stop=True),
                        wait=[("dve", sq_done), ("dve", r_nrm.back(1))],
                        inc=[("pe", 1)])
                    nrm_done = E.n("pe")
                    r_nrmpe.push(nrm_done)
                    tgt = 0 if fc < 8 else 1
                    gn = pe_guard(E, nrm_done)
                    E.op("dve", lambda v, par=par: v.reduce_max(
                        small[0:1, 2:3], ps_rot[par][0:1, :], axis=AX.X),
                        wait=[("dve", gn)], inc=[("dve", 1)])
                    E.op("dve", lambda v, tgt=tgt: v.tensor_max(
                        small[0:1, tgt:tgt + 1], small[0:1, tgt:tgt + 1],
                        small[0:1, 2:3]),
                        wait=[("dve", E.n("dve"))], inc=[("dve", 1)])
                    r_nrm.push(E.n("dve"))
                    if fc < 8:
                        E.op("sp", lambda p, g=g, fc=fc, par=par: p.dma_start(
                            q_scr[g, fc, :, :], rop[par][:, :]),
                            wait=[("dve", rope_done)], inc=[("st", 16)])
                        r_rop.push(E.dma_mark("sp", "st"))
                    r_psqkv.push((E.n("act"), E.n("dve")))
                tabs_prev.push(E.n("dve"))

                for qt in range(4 if (KAPARTS & 4) else 0):
                    kt = g * 4 + qt
                    for half in range(2):
                        par = half
                        for dc in range(8):
                            E.op("pe", lambda t, qt=qt, half=half, dc=dc, par=par: t.matmul(
                                ps_v[par][:, :],
                                xnT[:, dc * 512 + qt * 128: dc * 512 + qt * 128 + 128],
                                wqkv_sb[:, dc * 3 * D + 2048 + half * 512:
                                        dc * 3 * D + 2048 + half * 512 + 512],
                                start=(dc == 0), stop=(dc == 7)),
                                wait=([("dve", r_psv.back(2))] if dc == 0 else []),
                                inc=([("pe", 1)] if dc == 7 else []))
                        vmm = E.n("pe")
                        base = kt * 16 * 65 + half * 8 * 65
                        gv = pe_guard(E, vmm)
                        E.op("dve", lambda v, base=base, par=par: v.tensor_copy(
                            vres[:, base:base + 8 * 65].rearrange(
                                "p (h o) -> p h o", o=65)[:, :, 0:64],
                            ps_v[par][:, :].rearrange("p (h d) -> p h d", d=64)),
                            wait=[("dve", gv)], inc=[("dve", 1)])
                        r_psv.push(E.n("dve"))

            if not (KAPARTS & 8):
                return
            def ssd(fn):
                E.op("dve", fn)
                dve_pad(E)
            dve_pad(E)
            ssd(lambda v: v.tensor_mul(small[0:1, 2:3], small[0:1, 0:1], small[0:1, 1:2]))
            rsqrt_newton(E, small[0:1, 2:3], small[0:1, 3:4],
                         small[0:1, 4:5], small[0:1, 5:6])
            ssd(lambda v: v.tensor_mul(small[0:1, 4:5], small[0:1, 2:3], small[0:1, 3:4]))
            ssd(lambda v: v.tensor_scalar(small[0:1, 5:6], small[0:1, 4:5], -0.125,
                                          None, ALU.mult))
            ssd(lambda v: v.tensor_copy(rotb[0:1, 0:1], small[0:1, 5:6]))
            E.op("dve", lambda v: v.engine_nop(), inc=[("dve", 1)])
            cval = E.n("dve")
            E.op("pe", lambda t: t.matmul(ps_rot[0][:, 0:1], onesr_sb[0:1, :],
                                          rotb[0:1, 0:1], start=True, stop=True),
                 wait=[("dve", cval), ("dve", r_nrm.back(1))], inc=[("pe", 1)])
            cbc = E.n("pe")
            E.op("dve", lambda v: v.tensor_copy(c_col[:, :], ps_rot[0][:, 0:1]),
                 wait=[("pe", cbc), ("dve", E.n("dve"))], inc=[("dve", 1)])


    def phase_b(E, kf, vres):
        if True:
            q_sb = T["q"]
            wproj_sb = T["wproj"]
            p_sb = [T["p0"], T["p1"]]
            l_sb = T["l"]
            elr_sb = T["elr"]
            rl_sb = T["rl"]
            rlb_sb = T["rlb"]
            tb_sb = T["tb"]
            pv0_sb = T["pv0"]
            pv1_sb = T["pv1"]
            attn_sb = T["attn"]
            mask_sb = T["mask"]
            bproj_bc = T["bproj_bc"]
            t32_sb = T["t32b"]
            x1_sb = T["x1b"]
            xb_sb = [T["xb0"], T["xb1"]]

            ps_s = [pbank[:, 0:1024], pbank[:, 1024:2048]]
            ps_pv = [pbank[0:65, 2048:2560], pbank[0:65, 2560:3072]]
            ps_bc = pbank[:, 3072:3584]
            ps_proj = pbank[:, 3584:4096]

            E.op("sp", lambda s: s.dma_start(elr_sb[:, :], elr_in[:, :]), inc=[("in", 16)])
            E.op("sp", lambda s: s.dma_start(mask_sb[:, :], mask_in[:, :]), inc=[("in", 16)])
            E.op("sp", lambda s: s.dma_start(
                bproj_bc[:, :], bproj_in[0:1, :].partition_broadcast(128)),
                inc=[("in", 16)])
            m_bconst = E.dma_mark("sp", "in")
            for dc in range(8):
                E.op("pool", lambda g2, dc=dc: g2.dma_start(
                    wproj_sb[:, dc * D:(dc + 1) * D],
                    wproj_in[dc * 128:(dc + 1) * 128, :]), inc=[("ing", 16)])
            m_bw = E.dma_mark("pool", "ing")
            E.op("dve", lambda v: v.memset(l_sb[:, :], 0.0),
                 wait=[("dve", E.n("dve"))], inc=[("dve", 1)])

            r_pss = Roll()
            r_pexp = Roll()
            r_pvfree = Roll()
            r_pvout = Roll()
            r_recip = Roll()
            r_bcB = Roll()
            r_rl = Roll()
            r_x1 = Roll()
            r_x1dma = Roll()
            r_pev = Roll()
            r_xb = Roll()
            stlast = Roll()
            r_qscr = Roll()

            for g in range(NQG if (KBPARTS & 1) else 0):
                NK = 4 * g + 4
                for fc in range(8):
                    E.op("sp", lambda s, g=g, fc=fc: s.dma_start(
                        q_sb[:, fc * 512:(fc + 1) * 512], q_scr[g, fc, :, :]),
                        wait=([("pe", stlast.back(1)),
                               ("st", r_qscr.back(1))] if fc == 0 else []),
                        inc=[("in", 16)])
                m_q = E.dma_mark("sp", "in")

                ncs = min(NPAIR, KNPAIR) if (KBPARTS & 2) else 0
                st_mark = {}

                def emit_st(c, kc):
                    par = kc % 2
                    j = kc - 4 * g
                    off = 128 * j if j > 0 else 0
                    kbase = c * S + kc * 128
                    E.op("pe", lambda t, kbase=kbase: t.ldweights(
                        kf[0:64, kbase:kbase + 128], tile_position=(0, 0)),
                        wait=[m_q, ("act", r_pss.back(2)), m_bconst])
                    E.op("pe", lambda t, kbase=kbase: t.ldweights(
                        kf[64:128, kbase:kbase + 128], tile_position=(64, 0)))
                    E.op("pe", lambda t, kbase=kbase, par=par, off=off, c=c: t.matmul(
                        ps_s[par][:, off:512], kf[0:64, kbase:kbase + 128],
                        q_sb[0:64, c * 512 + off:(c + 1) * 512],
                        start=True, stop=True, tile_position=(0, 0)))
                    E.op("pe", lambda t, kbase=kbase, par=par, off=off, c=c: t.matmul(
                        ps_s[par][:, 512 + off:1024], kf[64:128, kbase:kbase + 128],
                        q_sb[64:128, c * 512 + off:(c + 1) * 512],
                        start=True, stop=True, tile_position=(64, 0)),
                        inc=[("pe", 1)])
                    st_mark[(c, kc)] = E.n("pe")

                for c in range(ncs):
                    if c == 0:
                        emit_st(0, 0)
                    for kc in range(NK):
                        par = kc % 2
                        j = kc - 4 * g
                        off = 128 * j if j > 0 else 0
                        E.op("act", lambda a, par=par, off=off: a.activation(
                            p_sb[par][:, :].rearrange("p (a n) -> p a n", a=2)[:, :, off:512],
                            ps_s[par][:, :].rearrange("p (a n) -> p a n", a=2)[:, :, off:512],
                            AF.Exp, scale=0.125, bias=c_col[:, 0:1]),
                            wait=[("pe", st_mark[(c, kc)]), ("pe", r_pexp.back(2))],
                            inc=[("act", 1)])
                        r_pss.push(E.n("act"))
                        exp_done = E.n("act")
                        mask_done = 0
                        if j >= 0 and _os.environ.get("KNOMASK", "") != "1":
                            E.op("dve", lambda v, par=par, off=off: v.tensor_mul(
                                p_sb[par][:, :].rearrange("p (a n) -> p a n", a=2)
                                [:, :, off:off + 128],
                                p_sb[par][:, :].rearrange("p (a n) -> p a n", a=2)
                                [:, :, off:off + 128],
                                mask_sb[:, :].rearrange("p (a n) -> p a n", a=2)),
                                wait=[("act", exp_done), ("dve", E.n("dve"))],
                                inc=[("dve", 1)])
                            mask_done = E.n("dve")
                        if kc + 1 < NK:
                            emit_st(c, kc + 1)
                        elif c + 1 < ncs:
                            emit_st(c + 1, 0)
                        for hh in range(2 if (KBPARTS & 4) else 0):
                            h = 2 * c + hh
                            vbase = kc * 16 * 65 + h * 65
                            E.op("pe", lambda t, vbase=vbase, hh=hh, par=par, off=off,
                                 kc=kc, NK=NK: t.matmul(
                                ps_pv[hh][:, off:512], vres[:, vbase:vbase + 65],
                                p_sb[par][:, hh * 512 + off: hh * 512 + 512],
                                start=(kc == 0), stop=(kc == NK - 1),
                                skip_group_check=True),
                                wait=(([("act", exp_done), ("dve", mask_done)]
                                       + ([("act", r_pvfree.back(1))]
                                          if (hh == 0 and kc == 0) else []))
                                      if hh == 0 else []),
                                inc=([("pe", 1)] if hh == 1 else []))
                        r_pexp.push(E.n("pe"))
                    pv_done = E.n("pe")
                    if not (KBEPI & 8):
                        continue
                    E.op("act", lambda a: a.activation(
                        pv0_sb[0:65, :], ps_pv[0][:, :], AF.Copy),
                        wait=[("pe", pv_done), ("dve", r_pvout.back(1))],
                        inc=[("act", 1)])
                    E.op("act", lambda a: a.activation(
                        pv1_sb[0:65, :], ps_pv[1][:, :], AF.Copy),
                        inc=[("act", 1)])
                    pvev = E.n("act")
                    r_pvfree.push(pvev)
                    E.op("dve", lambda v: v.tensor_copy(l_sb[0:1, :], pv0_sb[64:65, :]),
                         wait=[("act", pvev), ("dve", E.n("dve"))], inc=[("dve", 1)])
                    E.op("dve", lambda v: v.tensor_copy(l_sb[64:65, :], pv1_sb[64:65, :]),
                         wait=[("dve", E.n("dve"))], inc=[("dve", 1)])
                    lrows = E.n("dve")
                    E.op("pe", lambda t: t.matmul(ps_bc[0:64, :], elr_sb[:, 0:64],
                                                  l_sb[:, :], start=True, stop=True),
                         wait=[("dve", lrows), ("act", r_rl.back(1)),
                               ("dve", r_bcB.back(1)), ("dve", r_pev.back(1))])
                    E.op("pe", lambda t: t.matmul(ps_proj[0:64, :], elr_sb[:, 64:128],
                                                  l_sb[:, :], start=True, stop=True),
                         wait=[("dve", r_pev.back(1))],
                         inc=[("pe", 1)])
                    bcd = E.n("pe")
                    E.op("act", lambda a: a.activation(
                        rl_sb[0:64, :], ps_bc[0:64, :], AF.Copy),
                        wait=[("pe", bcd), ("dve", r_recip.back(1))],
                        inc=[("act", 1)])
                    E.op("act", lambda a: a.activation(
                        rlb_sb[0:64, :], ps_proj[0:64, :], AF.Copy),
                        inc=[("act", 1)])
                    rlev = E.n("act")
                    r_rl.push(rlev)
                    E.op("dve", lambda v: v.reciprocal(rl_sb[0:64, :], rl_sb[0:64, :]),
                         wait=[("act", rlev), ("dve", E.n("dve"))], inc=[("dve", 1)])
                    E.op("dve", lambda v: v.reciprocal(rlb_sb[0:64, :], rlb_sb[0:64, :]),
                         wait=[("dve", E.n("dve"))], inc=[("dve", 1)])
                    r_recip.push(E.n("dve"))
                    E.op("dve", lambda v, c=c: v.tensor_mul(
                        attn_sb[0:64, c * 512:(c + 1) * 512],
                        pv0_sb[0:64, :], rl_sb[0:64, :]),
                        wait=[("dve", E.n("dve"))], inc=[("dve", 1)])
                    E.op("dve", lambda v: v.tensor_mul(
                        tb_sb[0:64, :], pv1_sb[0:64, :], rlb_sb[0:64, :]),
                        wait=[("dve", E.n("dve"))], inc=[("dve", 1)])
                    tbd = E.n("dve")
                    r_pvout.push(tbd)
                    E.op("pe", lambda t: t.matmul(
                        ps_bc[64:128, :], id_sb[0:64, 0:64], tb_sb[0:64, :],
                        start=True, stop=True, tile_position=(0, 64)),
                        wait=[("dve", tbd)], inc=[("pe", 1)])
                    mvd = E.n("pe")
                    g3 = pe_guard(E, mvd)
                    E.op("dve", lambda v, c=c: v.tensor_copy(
                        attn_sb[64:128, c * 512:(c + 1) * 512], ps_bc[64:128, :]),
                        wait=[("dve", g3)], inc=[("dve", 1)])
                    r_bcB.push(E.n("dve"))
                stlast.push(E.n("pe"))
                attn_ready = E.n("dve")

                for qt in range(4 if (KBPARTS & 16) else 0):
                    qa = g * 4 + qt
                    slot = qt % 2
                    wb = r_xb.back(2)
                    E.op("sp", lambda s, qa=qa, slot=slot: s.dma_start(
                        xb_sb[slot][:, :], x_in[qa * 128:(qa + 1) * 128, :]),
                        wait=[("dve", wb)] if wb else [], inc=[("in", 16)])
                    m_xb = E.dma_mark("sp", "in")
                    if not (KBPROJ & 2):
                        continue
                    for half in range(2):
                        ps_pr = ps_proj if half == 0 else ps_bc
                        for dc in range(8):
                            E.op("pe", lambda t, qt=qt, half=half, dc=dc, ps_pr=ps_pr:
                                 t.matmul(
                                ps_pr[:, :],
                                attn_sb[:, dc * 512 + qt * 128: dc * 512 + qt * 128 + 128],
                                wproj_sb[:, dc * D + half * 512: dc * D + half * 512 + 512],
                                start=(dc == 0), stop=(dc == 7)),
                                wait=([("dve", attn_ready), m_bw,
                                       ("dve", r_pev.back(2))] if dc == 0 else []),
                                inc=([("pe", 1)] if dc == 7 else []))
                        pj = E.n("pe")
                        if not (KBPROJ & 32):
                            continue
                        gpj = pe_guard(E, pj, extra=[m_bconst])
                        E.op("dve", lambda v, half=half, ps_pr=ps_pr: v.tensor_add(
                            t32_sb[:, half * 512:(half + 1) * 512], ps_pr[:, :],
                            bproj_bc[:, half * 512:(half + 1) * 512]),
                            wait=[("dve", gpj)],
                            inc=[("dve", 1)])
                        r_pev.push(E.n("dve"))
                    if not (KBPROJ & 4):
                        continue
                    KST = int(_os.environ.get("KST", "7"))
                    if KST & 1:
                        ln_stats_act(E, t32_sb[:, :], 0,
                                     wait=[("dve", E.n("dve"))])
                    rd = E.n("dve")
                    if KST & 2:
                        rd = rstd_from_vars(E, 1)
                    if not (KST & 4):
                        continue
                    E.op("dve", lambda v: v.tensor_scalar(
                        x1_sb[:, :], t32_sb[:, :],
                        small[:, 20:21], small[:, 16:17], ALU.mult, ALU.add),
                        wait=[("dve", rd)]
                             + ([r_x1dma.back(1)] if r_x1dma.back(1) else []),
                        inc=[("dve", 1)])
                    if not (KBPROJ & 8):
                        continue
                    E.op("dve", lambda v, slot=slot: v.tensor_add(
                        x1_sb[:, :], x1_sb[:, :], xb_sb[slot][:, :]),
                        wait=[m_xb, ("dve", E.n("dve"))],
                        inc=[("dve", 1)])
                    r_x1.push(E.n("dve"))
                    r_xb.push(E.n("dve"))
                    fin = E.n("dve")
                    E.op("pool", lambda p, qa=qa: p.dma_start(
                        x1_scr[qa * 128:(qa + 1) * 128, :], x1_sb[:, :]),
                        wait=[("dve", fin)], inc=[("stg", 16)])
                    r_x1dma.push(E.dma_mark("pool", "stg"))


    def phase_c(E):
        if True:
            wenc_sb = T["wenc"]
            wdec_sb = T["wdec"]
            ws2c_sb = T["ws2c"]
            wc2s_sb = T["wc2s"]
            w1_sb = T["w1"]
            w2_sb = T["w2"]
            wr_sb = T["wr"]
            sel_sb = T["sel"]
            benc_sb = T["benc"]
            bc2s_sb = T["bc2s"]
            b1_sb = T["b1"]
            b2_sb = T["b2"]
            bs2c_sb = T["bs2c"]
            br_sb = T["br"]
            lng_sb = T["lng"]
            lnb_sb = T["lnb"]
            bdec_bc = T["bdec_bc"]
            xn2T = T["xn2T"]
            spik = T["spik"]
            smoe = T["smoe"]
            cdup = T["cdup"]
            cont32 = T["cont32"]
            h_sb = [T["h0"], T["h1"]]
            gate_sb = T["gate"]
            gT_sb = T["gT"]
            gbc_sb = T["gbc"]
            tpair_sb = T["tp"]
            tpair2_sb = T["tp2"]
            eo32 = T["eo32"]
            eodup = T["eod"]
            rsm_sb = T["rsm"]
            xn_sb = T["xnc"]
            t32_sb = T["t32c"]
            x1_sb = T["x1c"]
            xc_sb = [T[f"xc{i}"] for i in range(4)]

            ps_c = [pbank[:, 0:512], pbank[:, 512:1024]]
            ps_y = [pbank[:, 1024:1536], pbank[:, 1536:2048]]
            ps_cont = pbank[0:64, 2048:2560]
            ps_g = pbank[:, 2560:3072]
            ps_trc = [pbank[:, 3072:3136].bitcast(BF16)[:, 0:128],
                      pbank[:, 3584:3648].bitcast(BF16)[:, 0:128]]

            for dc in range(8):
                E.op("pool", lambda g2, dc=dc: g2.dma_start(
                    wenc_sb[:, dc * H:(dc + 1) * H],
                    wenc_in[dc * 128:(dc + 1) * 128, :]), inc=[("ing", 16)])
            for hc in range(16):
                E.op("pool", lambda g2, hc=hc: g2.dma_start(
                    wdec_sb[:, hc * D:(hc + 1) * D],
                    wdec_in[hc * 128:(hc + 1) * 128, :]), inc=[("ing", 16)])
                E.op("pool", lambda g2, hc=hc: g2.dma_start(
                    ws2c_sb[:, hc * M:(hc + 1) * M],
                    ws2c_in[hc * 128:(hc + 1) * 128, :]), inc=[("ing", 16)])
            E.op("pool", lambda g2: g2.dma_start(wc2s_sb[0:64, :], wc2s_in[:, :]),
                 inc=[("ing", 16)])
            E.op("pool", lambda g2: g2.dma_start(wc2s_sb[64:128, :], wc2s_in[:, :]),
                 inc=[("ing", 16)])
            E.op("pool", lambda g2: g2.dma_start(w1_sb[:, :], w1_in[:, :]), inc=[("ing", 16)])
            E.op("pool", lambda g2: g2.dma_start(w2_sb[:, :], w2_in[:, :]), inc=[("ing", 16)])
            E.op("pool", lambda g2: g2.dma_start(sel_sb[:, :], sel_in[:, :]), inc=[("ing", 16)])
            E.op("sp", lambda s: s.dma_start(wr_sb[:, :], wr_in[:, :]), inc=[("in", 16)])
            E.op("sp", lambda s: s.dma_start(
                br_sb[:, :], br_in[0:1, :].partition_broadcast(128)), inc=[("in", 16)])
            E.op("sp", lambda s: s.dma_start(
                lng_sb[:, :], lng_in[0:1, :].partition_broadcast(128)), inc=[("in", 16)])
            E.op("sp", lambda s: s.dma_start(
                lnb_sb[:, :], lnb_in[0:1, :].partition_broadcast(128)), inc=[("in", 16)])
            E.op("sp", lambda s: s.dma_start(
                bdec_bc[:, :], bdec_in[0:1, :].partition_broadcast(128)), inc=[("in", 16)])
            with nc.allow_non_contiguous_dma(reason="one-time tiny bias columns"):
                E.op("sp", lambda s: s.dma_start(
                    bs2c_sb[:, :], bs2c_in[0:1, :].rearrange("o m -> m o")),
                    inc=[("in", 16)])
                E.op("sp", lambda s: s.dma_start(
                    benc_sb[:, :], benc_in[:, :].rearrange("a p -> p a")),
                    inc=[("in", 16)])
                E.op("sp", lambda s: s.dma_start(
                    bc2s_sb[:, :], bc2s_in[:, :].rearrange("a p -> p a")),
                    inc=[("in", 16)])
                E.op("sp", lambda s: s.dma_start(
                    b1_sb[:, :], b1_in[:, :].rearrange("a p -> p a")),
                    inc=[("in", 16)])
                E.op("sp", lambda s: s.dma_start(
                    b2_sb[:, :], b2_in[:, :].rearrange("a p -> p a")),
                    inc=[("in", 16)])
            m_cconst = E.dma_mark("sp", "in")
            m_cw = E.dma_mark("pool", "ing")
            E.op("dve", lambda v: v.tensor_scalar(
                benc_sb[:, :], benc_sb[:, :], 0.5, None, ALU.mult),
                wait=[m_cconst, ("dve", E.n("dve"))], inc=[("dve", 1)])
            E.op("dve", lambda v: v.tensor_scalar(
                bc2s_sb[:, :], bc2s_sb[:, :], 0.5, None, ALU.mult),
                wait=[("dve", E.n("dve"))], inc=[("dve", 1)])
            chalf = E.n("dve")

            r_x1slot = Roll()
            r_xnc = Roll()
            r_pstrc = Roll()
            r_psc = Roll()
            r_spk = Roll()
            r_contw = Roll()
            r_gate = Roll()
            r_gT = Roll()
            r_hread = Roll()
            r_gbc = Roll()
            r_psy = Roll()
            r_smoe = Roll()
            r_dev = Roll()
            r_odma = Roll()

            for g in range(NQG):
                for qt in range(4):
                    qa = g * 4 + qt
                    wb = r_x1slot.back(1)
                    E.op("sp", lambda s, qa=qa, qt=qt: s.dma_start(
                        xc_sb[qt][:, :], x1_scr[qa * 128:(qa + 1) * 128, :]),
                        wait=([("dve", wb)] if (wb and qt == 0) else []),
                        inc=[("in", 16)])
                m_xc = E.dma_mark("sp", "in")
                for i in range(4):
                    ln_stats_act(E, xc_sb[i][:, :], i, wait=([m_xc] if i == 0 else []))
                c_rstd = rstd_from_vars(E, 4)
                for qt in range(4):
                    E.op("dve", lambda v, qt=qt: v.tensor_scalar(
                        xn_sb[:, :], xc_sb[qt][:, :],
                        small[:, 20 + qt:21 + qt], small[:, 16 + qt:17 + qt],
                        ALU.mult, ALU.add),
                        wait=[("dve", c_rstd), ("pe", r_xnc.back(1)),
                              ("dve", E.n("dve"))], inc=[("dve", 1)])
                    ap2 = E.n("dve")
                    for dc in range(8):
                        half = dc % 2
                        E.op("pe", lambda t, dc=dc, half=half: t.transpose(
                            ps_trc[half], xn_sb[:, dc * 128:(dc + 1) * 128], id_sb[:, :]),
                            wait=[("dve", ap2), ("act", r_pstrc.back(2))],
                            inc=[("pe", 1)])
                        trd = E.n("pe")
                        E.op("act", lambda a, dc=dc, qt=qt, half=half: a.activation(
                            xn2T[:, dc * 512 + qt * 128: dc * 512 + qt * 128 + 128],
                            ps_trc[half], AF.Copy),
                            wait=[("pe", trd)], inc=[("act", 1)])
                        r_pstrc.push(E.n("act"))
                    r_xnc.push(E.n("pe"))
                xn2_done = E.n("act")

                for hc in range(16):
                    par = hc % 2
                    wpc = r_psc.back(2)
                    for dc in range(8):
                        E.op("pe", lambda t, hc=hc, dc=dc, par=par: t.matmul(
                            ps_c[par][:, :],
                            wenc_sb[:, dc * H + hc * 128: dc * H + hc * 128 + 128],
                            xn2T[:, dc * 512:(dc + 1) * 512],
                            start=(dc == 0), stop=(dc == 7)),
                            wait=([("act", xn2_done), m_cw,
                                   ("act", wpc[0] if wpc else 0),
                                   ("dve", wpc[1] if wpc else 0)] if dc == 0 else []),
                            inc=([("pe", 1)] if dc == 7 else []))
                    encd = E.n("pe")
                    E.op("act", lambda a, hc=hc, par=par: a.activation(
                        spik[:, hc * 512:(hc + 1) * 512], ps_c[par][:, :], AF.Tanh,
                        scale=0.5, bias=benc_sb[:, hc:hc + 1]),
                        wait=[("pe", encd), ("dve", chalf)], inc=[("act", 1)])
                    tnh = E.n("act")
                    E.op("dve", lambda v, hc=hc: v.tensor_scalar(
                        spik[:, hc * 512:(hc + 1) * 512],
                        spik[:, hc * 512:(hc + 1) * 512],
                        0.5, 0.5, ALU.mult, ALU.add),
                        wait=[("act", tnh), ("dve", E.n("dve"))], inc=[("dve", 1)])
                    r_psc.push((E.n("act"), E.n("dve")))
                spikes_done = E.n("dve")

                wcont = r_contw.back(1)
                for hc in range(16):
                    E.op("pe", lambda t, hc=hc: t.matmul(
                        ps_cont[0:64, :], ws2c_sb[:, hc * M:(hc + 1) * M],
                        spik[:, hc * 512:(hc + 1) * 512],
                        start=(hc == 0), stop=(hc == 15)),
                        wait=([("dve", spikes_done), ("act", wcont[0] if wcont else 0),
                               ("dve", wcont[1] if wcont else 0)] if hc == 0 else []),
                        inc=([("pe", 1)] if hc == 15 else []))
                contd = E.n("pe")
                gcont = pe_guard(E, contd, extra=[m_cconst])
                E.op("dve", lambda v: v.tensor_scalar(
                    cdup[0:64, :], ps_cont[0:64, :], bs2c_sb[0:64, 0:1], None, ALU.add),
                    wait=[("dve", gcont)],
                    inc=[("dve", 1)])
                E.op("dve", lambda v: v.tensor_scalar(
                    cont32[0:64, :], ps_cont[0:64, :], bs2c_sb[0:64, 0:1], None, ALU.add),
                    wait=[("dve", E.n("dve"))], inc=[("dve", 1)])
                r_contw.push((E.n("act"), E.n("dve")))
                cont_done = E.n("dve")
                E.op("pe", lambda t: t.matmul(
                    ps_y[0][64:128, :], id_sb[0:64, 0:64], cdup[0:64, :],
                    start=True, stop=True, tile_position=(0, 64)),
                    wait=[("dve", cont_done), ("dve", r_psy.back(1))],
                    inc=[("pe", 1)])
                dupd = E.n("pe")
                gdup = pe_guard(E, dupd)
                E.op("dve", lambda v: v.tensor_copy(cdup[64:128, :], ps_y[0][64:128, :]),
                     wait=[("dve", gdup)], inc=[("dve", 1)])
                cdup_done = E.n("dve")

                for qt in range(4):
                    E.op("pe", lambda t, qt=qt: t.matmul(
                        ps_g[0:128, 0:8], cont32[:, qt * 128:(qt + 1) * 128],
                        wr_sb[:, :], start=True, stop=True),
                        wait=[("dve", cont_done), m_cconst,
                              ("dve", cdup_done if qt == 0 else r_gate.back(1))],
                        inc=[("pe", 1)])
                    rd = E.n("pe")
                    grd = pe_guard(E, rd)

                    def sg(fn):
                        E.op("dve", fn)
                        dve_pad(E)
                    E.op("dve", lambda v: v.tensor_add(
                        rsm_sb[:, 0:8], ps_g[0:128, 0:8], br_sb[:, 0:8]),
                        wait=[("dve", grd)])
                    dve_pad(E)
                    sg(lambda v: v.reduce_max(rsm_sb[:, 8:9], rsm_sb[:, 0:8],
                                              axis=AX.X, negate=True))
                    E.op("dve", lambda v: v.engine_nop(), inc=[("dve", 1)])
                    E.op("act", lambda a: a.activation(
                        rsm_sb[:, 0:8], rsm_sb[:, 0:8], AF.Exp, bias=rsm_sb[:, 8:9]),
                        wait=[("dve", E.n("dve"))], inc=[("act", 1)])
                    exd = E.n("act")
                    E.op("dve", lambda v: v.reduce_max(rsm_sb[:, 9:10], rsm_sb[:, 0:8],
                                                       axis=AX.X),
                         wait=[("act", exd)])
                    dve_pad(E)
                    sg(lambda v: v.tensor_scalar(stats_sb[:, 0:8], rsm_sb[:, 0:8],
                                                 rsm_sb[:, 9:10], None, ALU.is_ge))
                    sg(lambda v: v.tensor_mul(stats_sb[:, 8:16], rsm_sb[:, 0:8],
                                              stats_sb[:, 0:8]))
                    sg(lambda v: v.tensor_sub(stats_sb[:, 8:16], rsm_sb[:, 0:8],
                                              stats_sb[:, 8:16]))
                    sg(lambda v: v.reduce_max(rsm_sb[:, 10:11], stats_sb[:, 8:16],
                                              axis=AX.X))
                    sg(lambda v: v.tensor_scalar(stats_sb[:, 0:8], rsm_sb[:, 0:8],
                                                 rsm_sb[:, 10:11], None, ALU.is_ge))
                    sg(lambda v: v.tensor_mul(stats_sb[:, 8:16], rsm_sb[:, 0:8],
                                              stats_sb[:, 0:8]))
                    sg(lambda v: v.reduce_sum(rsm_sb[:, 11:12], stats_sb[:, 8:16],
                                              axis=AX.X))
                    sg(lambda v: v.reciprocal(rsm_sb[:, 12:13], rsm_sb[:, 11:12]))
                    sg(lambda v, qt=qt: v.tensor_scalar(
                        gate_sb[:, qt * 8:(qt + 1) * 8], stats_sb[:, 8:16],
                        rsm_sb[:, 12:13], None, ALU.mult))
                    E.op("dve", lambda v: v.engine_nop(), inc=[("dve", 1)])
                    r_gate.push(E.n("dve"))
                    gdone = E.n("dve")
                    E.op("pe", lambda t, qt=qt: t.transpose(
                        ps_trc[qt % 2][0:8, :], gate_sb[:, qt * 8:(qt + 1) * 8],
                        id_sb[:, :]),
                        wait=[("dve", gdone), ("dve", r_gT.back(2))],
                        inc=[("pe", 1)])
                    gtd = E.n("pe")
                    E.op("act", lambda a, qt=qt: a.activation(
                        gT_sb[0:8, qt * 128:(qt + 1) * 128], ps_trc[qt % 2][0:8, :],
                        AF.Copy),
                        wait=[("pe", gtd)], inc=[("act", 1)])
                    r_gT.push(E.n("act"))
                    r_pstrc.push(E.n("act"))
                    r_pstrc.push(E.n("act"))
                gT_done = E.n("dve")

                for i in range(4):
                    hread = r_hread.back(1)
                    wev = (0, 0)
                    for hc in range(8):
                        wof = (i * 8 + hc) * 128
                        E.op("pe", lambda t, wof=wof: t.ldweights(
                            w1_sb[0:64, wof:wof + 128], tile_position=(0, 0)),
                            wait=(([("dve", cdup_done),
                                    ("act", hread[0] if hread else 0),
                                    ("dve", hread[1] if hread else 0),
                                    ("dve", r_psy.back(1))] if hc == 0 else [])
                                  + [("act", wev[0]), ("dve", wev[1])]))
                        E.op("pe", lambda t, wof=wof: t.ldweights(
                            w1_sb[64:128, wof:wof + 128], tile_position=(64, 0)))
                        pc0 = ps_c[0] if hc % 2 == 0 else ps_y[0]
                        pc1 = ps_c[1] if hc % 2 == 0 else ps_y[1]
                        E.op("pe", lambda t, wof=wof, pc0=pc0: t.matmul(
                            pc0[0:128, :], w1_sb[0:64, wof:wof + 128], cdup[0:64, :],
                            start=True, stop=True, tile_position=(0, 0)))
                        E.op("pe", lambda t, wof=wof, pc1=pc1: t.matmul(
                            pc1[0:128, :], w1_sb[64:128, wof:wof + 128], cdup[64:128, :],
                            start=True, stop=True, tile_position=(64, 0)),
                            inc=[("pe", 1)])
                        w1d = E.n("pe")
                        E.op("act", lambda a, i=i, hc=hc, pc1=pc1: a.activation(
                            h_sb[1][:, hc * 512:(hc + 1) * 512], pc1[0:128, :], AF.Relu,
                            bias=b1_sb[:, (2 * i + 1) * 8 + hc:(2 * i + 1) * 8 + hc + 1]),
                            wait=[("pe", w1d), m_cconst], inc=[("act", 1)])
                        E.op("dve", lambda v, i=i, hc=hc, pc0=pc0: v.tensor_scalar(
                            h_sb[0][:, hc * 512:(hc + 1) * 512], pc0[0:128, :],
                            b1_sb[:, (2 * i) * 8 + hc:(2 * i) * 8 + hc + 1],
                            0.0, ALU.add, ALU.max),
                            wait=[("pe", w1d), ("dve", E.n("dve"))], inc=[("dve", 1)])
                        wev = (E.n("act"), E.n("dve"))
                    h_a = E.n("act")
                    h_d = E.n("dve")
                    E.op("pe", lambda t, i=i: t.matmul(
                        ps_g[:, :], sel_sb[:, i * 128:(i + 1) * 128], gT_sb[0:8, :],
                        start=True, stop=True),
                        wait=[("dve", gT_done), ("dve", r_gbc.back(1))],
                        inc=[("pe", 1)])
                    gbcd = E.n("pe")
                    ggbc = pe_guard(E, gbcd)
                    E.op("dve", lambda v: v.tensor_copy(gbc_sb[:, :], ps_g[:, :]),
                         wait=[("dve", ggbc)], inc=[("dve", 1)])
                    gbc_done = E.n("dve")
                    wy = r_psy.back(1)
                    for hc in range(8):
                        o0 = ((2 * i) * 8 + hc) * 64
                        o1 = ((2 * i + 1) * 8 + hc) * 64
                        E.op("pe", lambda t, o0=o0, hc=hc: t.matmul(
                            ps_y[0][0:64, :], w2_sb[:, o0:o0 + 64],
                            h_sb[0][:, hc * 512:(hc + 1) * 512],
                            start=(hc == 0), stop=(hc == 7), tile_position=(0, 0),
                            skip_group_check=True),
                            wait=([("act", h_a), ("dve", h_d),
                                   ("dve", wy if wy else 0)] if hc == 0 else []))
                        E.op("pe", lambda t, o1=o1, hc=hc: t.matmul(
                            ps_y[1][64:128, :], w2_sb[:, o1:o1 + 64],
                            h_sb[1][:, hc * 512:(hc + 1) * 512],
                            start=(hc == 0), stop=(hc == 7), tile_position=(0, 64),
                            skip_group_check=True),
                            inc=([("pe", 1)] if hc == 7 else []))
                    w2d = E.n("pe")
                    r_hread.push((h_a, h_d))
                    E.op("dve", lambda v, i=i: v.scalar_tensor_tensor(
                        tpair_sb[0:64, :], ps_y[0][0:64, :], b2_sb[0:64, i:i + 1],
                        gbc_sb[0:64, :], ALU.add, ALU.mult),
                        wait=[("pe", w2d), ("dve", E.n("dve"))], inc=[("dve", 1)])
                    E.op("dve", lambda v, i=i: v.scalar_tensor_tensor(
                        tpair2_sb[0:64, :], ps_y[1][64:128, :], b2_sb[64:128, i:i + 1],
                        gbc_sb[64:128, :], ALU.add, ALU.mult),
                        wait=[("dve", E.n("dve"))], inc=[("dve", 1)])
                    if i == 0:
                        E.op("dve", lambda v: v.tensor_add(
                            eo32[0:64, :], tpair_sb[0:64, :], tpair2_sb[0:64, :]),
                            wait=[("dve", E.n("dve"))], inc=[("dve", 1)])
                    else:
                        E.op("dve", lambda v: v.tensor_add(
                            eo32[0:64, :], eo32[0:64, :], tpair_sb[0:64, :]),
                            wait=[("dve", E.n("dve"))], inc=[("dve", 1)])
                        E.op("dve", lambda v: v.tensor_add(
                            eo32[0:64, :], eo32[0:64, :], tpair2_sb[0:64, :]),
                            wait=[("dve", E.n("dve"))], inc=[("dve", 1)])
                    r_psy.push(E.n("dve"))
                    r_gbc.push(E.n("dve"))
                E.op("dve", lambda v: v.tensor_copy(eodup[0:64, :], eo32[0:64, :]),
                     wait=[("dve", E.n("dve"))], inc=[("dve", 1)])
                eod = E.n("dve")
                E.op("pe", lambda t: t.matmul(
                    ps_y[0][64:128, :], id_sb[0:64, 0:64], eodup[0:64, :],
                    start=True, stop=True, tile_position=(0, 64)),
                    wait=[("dve", eod), ("dve", r_psy.back(1))], inc=[("pe", 1)])
                eodd = E.n("pe")
                geod = pe_guard(E, eodd)
                E.op("dve", lambda v: v.tensor_copy(eodup[64:128, :], ps_y[0][64:128, :]),
                     wait=[("dve", geod)], inc=[("dve", 1)])
                r_psy.push(E.n("dve"))
                eodup_done = E.n("dve")

                for pi in range(8):
                    wpc = r_psc.back(2)
                    pcs = [ps_c[0], ps_c[1]] if pi % 2 == 0 else [ps_y[0], ps_y[1]]
                    E.op("pe", lambda t, pi=pi: t.ldweights(
                        wc2s_sb[0:64, pi * 128:(pi + 1) * 128], tile_position=(0, 0)),
                        wait=[("dve", eodup_done),
                              ("act", wpc[0] if wpc else 0),
                              ("dve", wpc[1] if wpc else 0)])
                    E.op("pe", lambda t, pi=pi: t.ldweights(
                        wc2s_sb[64:128, (pi + 8) * 128:(pi + 9) * 128],
                        tile_position=(64, 0)))
                    E.op("pe", lambda t, pi=pi, pcs=pcs: t.matmul(
                        pcs[0][0:128, :], wc2s_sb[0:64, pi * 128:(pi + 1) * 128],
                        eodup[0:64, :], start=True, stop=True, tile_position=(0, 0)))
                    E.op("pe", lambda t, pi=pi, pcs=pcs: t.matmul(
                        pcs[1][0:128, :], wc2s_sb[64:128, (pi + 8) * 128:(pi + 9) * 128],
                        eodup[64:128, :], start=True, stop=True, tile_position=(64, 0)),
                        inc=[("pe", 1)])
                    c2d = E.n("pe")
                    for which in range(2):
                        hcv = pi if which == 0 else pi + 8
                        E.op("act", lambda a, hcv=hcv, which=which, pcs=pcs: a.activation(
                            smoe[:, hcv * 512:(hcv + 1) * 512], pcs[which][0:128, :],
                            AF.Tanh, scale=0.5, bias=bc2s_sb[:, hcv:hcv + 1]),
                            wait=[("pe", c2d)], inc=[("act", 1)])
                        t2d = E.n("act")
                        E.op("dve", lambda v, hcv=hcv: v.tensor_scalar(
                            smoe[:, hcv * 512:(hcv + 1) * 512],
                            smoe[:, hcv * 512:(hcv + 1) * 512],
                            0.5, 0.5, ALU.mult, ALU.add),
                            wait=[("act", t2d), ("dve", E.n("dve"))], inc=[("dve", 1)])
                    r_psc.push((E.n("act"), E.n("dve")))
                smoe_done = E.n("dve")

                for qt in range(4):
                    qa = g * 4 + qt
                    slot = qt % 2
                    for half in range(2):
                        par = half
                        wpc = r_psc.back(2)
                        for hc in range(16):
                            E.op("pe", lambda t, qt=qt, half=half, hc=hc, par=par: t.matmul(
                                ps_c[par][:, :],
                                smoe[:, hc * 512 + qt * 128: hc * 512 + qt * 128 + 128],
                                wdec_sb[:, hc * D + half * 512: hc * D + half * 512 + 512],
                                start=(hc == 0), stop=(hc == 15)),
                                wait=([("dve", smoe_done),
                                       ("act", wpc[0] if wpc else 0),
                                       ("dve", wpc[1] if wpc else 0)] if hc == 0 else []),
                                inc=([("pe", 1)] if hc == 15 else []))
                        dcd = E.n("pe")
                        gdc = pe_guard(E, dcd, extra=[m_cconst])
                        E.op("dve", lambda v, half=half, par=par: v.tensor_add(
                            t32_sb[:, half * 512:(half + 1) * 512], ps_c[par][:, :],
                            bdec_bc[:, half * 512:(half + 1) * 512]),
                            wait=[("dve", gdc)],
                            inc=[("dve", 1)])
                        r_psc.push((E.n("act"), E.n("dve")))
                    ln_stats_act(E, t32_sb[:, :], 0, wait=[("dve", E.n("dve"))])
                    r1 = rstd_from_vars(E, 1)
                    E.op("dve", lambda v: v.tensor_scalar(
                        t32_sb[:, :], t32_sb[:, :],
                        small[:, 20:21], small[:, 16:17], ALU.mult, ALU.add),
                        wait=[("dve", r1)], inc=[("dve", 1)])
                    E.op("dve", lambda v: v.tensor_mul(t32_sb[:, :], t32_sb[:, :],
                                                       lng_sb[:, :]),
                         wait=[m_cconst, ("dve", E.n("dve"))],
                         inc=[("dve", 1)])
                    E.op("dve", lambda v: v.tensor_add(t32_sb[:, :], t32_sb[:, :],
                                                       lnb_sb[:, :]),
                         wait=[("dve", E.n("dve"))], inc=[("dve", 1)])
                    ln_stats_act(E, t32_sb[:, :], 0, wait=[("dve", E.n("dve"))])
                    r2 = rstd_from_vars(E, 1)
                    E.op("dve", lambda v: v.tensor_scalar(
                        t32_sb[:, :], t32_sb[:, :],
                        small[:, 20:21], small[:, 16:17], ALU.mult, ALU.add),
                        wait=[("dve", r2)], inc=[("dve", 1)])
                    E.op("dve", lambda v, qt=qt: v.tensor_add(
                        x1_sb[:, :], t32_sb[:, :], xc_sb[qt][:, :]),
                        wait=(([r_odma.back(1)] if r_odma.back(1) else [])
                              + [("dve", E.n("dve"))]),
                        inc=[("dve", 1)])
                    fin = E.n("dve")
                    r_x1slot.push(fin)
                    E.op("pool", lambda p, qa=qa: p.dma_start(
                        out_hbm[qa * 128:(qa + 1) * 128, :], x1_sb[:, :]),
                        wait=[("dve", fin)], inc=[("stg", 16)])
                    r_odma.push(E.dma_mark("pool", "stg"))

    def section(name):
        def f(eng):
            E = Em(name, eng, sems)
            program(E)
        return f

    block.tensor(section("pe"))
    block.scalar(section("act"))
    block.vector(section("dve"))
    block.gpsimd(section("pool"))
    block.sync(section("sp"))

    ostk.close()
    return nc



def _consts(S):
    inv = 1.0 / (10000.0 ** (np.arange(0, DH, 2, dtype=np.float64) / DH))
    t = np.arange(S, dtype=np.float64)
    freq_row = inv[np.arange(128) % 32]
    ang = freq_row[:, None] * t[None, :]
    cos_t = np.cos(ang).astype(ml_dtypes.bfloat16)
    sin_t = np.sin(ang).astype(ml_dtypes.bfloat16)
    ident = np.eye(128, dtype=ml_dtypes.bfloat16)
    perm = np.zeros((128, 128), dtype=np.float32)
    for m in range(128):
        base = (m // 64) * 64
        mm = m % 64
        if mm < 32:
            perm[base + mm + 32, m] = -1.0
        else:
            perm[base + mm - 32, m] = 1.0
    perm = perm.astype(ml_dtypes.bfloat16)
    kk, qq = np.meshgrid(np.arange(128), np.arange(128), indexing="ij")
    m1 = np.where(kk > qq, 0.0, 1.0).astype(np.float32)
    mask2 = np.concatenate([m1, m1], axis=1).astype(ml_dtypes.bfloat16)
    el = np.zeros((65, 128), dtype=np.float32)
    el[0, 0:64] = 1.0
    el[64, 64:128] = 1.0
    el = el.astype(ml_dtypes.bfloat16)
    elr = np.zeros((128, 128), dtype=np.float32)
    elr[0, 0:64] = 1.0
    elr[64, 64:128] = 1.0
    elr = elr.astype(ml_dtypes.bfloat16)
    ones1 = np.ones((128, 2), dtype=ml_dtypes.bfloat16)
    onesr = np.ones((1, 128), dtype=ml_dtypes.bfloat16)
    sel = np.zeros((8, 512), dtype=np.float32)
    for i in range(4):
        sel[2 * i, i * 128: i * 128 + 64] = 1.0
        sel[2 * i + 1, i * 128 + 64: i * 128 + 128] = 1.0
    sel = sel.astype(ml_dtypes.bfloat16)
    return dict(ident=ident, perm=perm, cos_t=cos_t, sin_t=sin_t, mask2=mask2,
                el=el, elr=elr, ones1=ones1, onesr=onesr, sel=sel)


def _prep_inputs(S, L0_attn_w_qkv, L0_attn_w_proj, L0_attn_b_proj,
                 L0_ffn_W_enc, L0_ffn_b_enc, L0_ffn_W_s2c, L0_ffn_b_s2c,
                 L0_ffn_Wr, L0_ffn_br, L0_ffn_W1, L0_ffn_b1, L0_ffn_W2, L0_ffn_b2,
                 L0_ffn_W_c2s, L0_ffn_b_c2s, L0_ffn_W_dec, L0_ffn_b_dec,
                 L0_ffn_ln_g, L0_ffn_ln_b):
    def lay(a):
        return np.ascontiguousarray(np.asarray(a, dtype=np.float32))

    w1 = lay(L0_ffn_W1)
    w1p = np.zeros((128, 4096), dtype=np.float32)
    for i in range(4):
        for hc in range(8):
            w1p[0:64, (i * 8 + hc) * 128:(i * 8 + hc) * 128 + 128] = \
                w1[2 * i, :, hc * 128:(hc + 1) * 128]
            w1p[64:128, (i * 8 + hc) * 128:(i * 8 + hc) * 128 + 128] = \
                w1[2 * i + 1, :, hc * 128:(hc + 1) * 128]
    w2 = lay(L0_ffn_W2)
    w2p = np.zeros((128, 4096), dtype=np.float32)
    for ee in range(8):
        for hc in range(8):
            w2p[:, (ee * 8 + hc) * 64:(ee * 8 + hc) * 64 + 64] = \
                w2[ee, hc * 128:(hc + 1) * 128, :]
    b1r = lay(L0_ffn_b1).reshape(8, 8, 128).reshape(64, 128)
    b2 = lay(L0_ffn_b2)
    b2r = np.zeros((4, 128), dtype=np.float32)
    for i in range(4):
        b2r[i, 0:64] = b2[2 * i]
        b2r[i, 64:128] = b2[2 * i + 1]

    return dict(
        wqkv=lay(L0_attn_w_qkv), wproj=lay(L0_attn_w_proj),
        bproj=lay(L0_attn_b_proj).reshape(1, D),
        wenc=lay(L0_ffn_W_enc), benc=lay(L0_ffn_b_enc).reshape(16, 128),
        ws2c=lay(L0_ffn_W_s2c), bs2c=lay(L0_ffn_b_s2c).reshape(1, M),
        wr=lay(L0_ffn_Wr), br=lay(L0_ffn_br).reshape(1, NE),
        w1=w1p, b1=b1r, w2=w2p, b2=b2r,
        wc2s=lay(L0_ffn_W_c2s), bc2s=lay(L0_ffn_b_c2s).reshape(16, 128),
        wdec=lay(L0_ffn_W_dec), bdec=lay(L0_ffn_b_dec).reshape(1, D),
        lng=lay(L0_ffn_ln_g).reshape(1, D), lnb=lay(L0_ffn_ln_b).reshape(1, D),
        **_consts(S),
    )


def kernel(L0_attn_x, L0_attn_w_qkv, L0_attn_w_proj, L0_attn_b_proj,
           L0_ffn_W_enc, L0_ffn_b_enc, L0_ffn_W_s2c, L0_ffn_b_s2c,
           L0_ffn_Wr, L0_ffn_br, L0_ffn_W1, L0_ffn_b1, L0_ffn_W2, L0_ffn_b2,
           L0_ffn_W_c2s, L0_ffn_b_c2s, L0_ffn_W_dec, L0_ffn_b_dec,
           L0_ffn_ln_g, L0_ffn_ln_b):
    x = np.ascontiguousarray(np.asarray(L0_attn_x, dtype=np.float32))
    B, S, _ = x.shape
    base = _prep_inputs(S, L0_attn_w_qkv, L0_attn_w_proj, L0_attn_b_proj,
                        L0_ffn_W_enc, L0_ffn_b_enc, L0_ffn_W_s2c, L0_ffn_b_s2c,
                        L0_ffn_Wr, L0_ffn_br, L0_ffn_W1, L0_ffn_b1, L0_ffn_W2,
                        L0_ffn_b2, L0_ffn_W_c2s, L0_ffn_b_c2s, L0_ffn_W_dec,
                        L0_ffn_b_dec, L0_ffn_ln_g, L0_ffn_ln_b)
    in_maps = []
    for b in range(B):
        m = dict(base)
        m["x"] = np.ascontiguousarray(x[b])
        in_maps.append(m)
    import os as _os
    nc = build_program(S, phases=int(_os.environ.get("KPHASES", "3")))
    res = run_bass_kernel_spmd(nc, in_maps, core_ids=list(range(B)))
    out = np.stack([np.asarray(r["out"], dtype=np.float32) for r in res.results],
                   axis=0)
    return out


# version 05 (final)
# speedup vs baseline: 1.2908x; 1.0015x over previous
import numpy as np
import ml_dtypes
from contextlib import ExitStack

import concourse.bass as bass
import concourse.mybir as mybir
from concourse.bass_utils import run_bass_kernel_spmd

F32 = mybir.dt.float32
BF16 = mybir.dt.bfloat16
AF = mybir.ActivationFunctionType
ALU = mybir.AluOpType
AX = mybir.AxisListType

D = 1024
DH = 64
NPAIR = 8
H = 2048
M = 64
NE = 8
EPS = 1e-5
NEG = -60000.0

ENG_ORDER = ("pe", "act", "dve", "pool", "sp")


class Roll:

    def __init__(self):
        self.v = []

    def push(self, m):
        self.v.append(m)

    def back(self, k):
        return self.v[-k] if len(self.v) >= k else 0


class Em:
    def __init__(self, name, eng, sems):
        self.name = name
        self.eng = eng
        self.sems = sems
        self.cnt = {k: 0 for k in sems}

    def n(self, s):
        return self.cnt[s]

    def op(self, engine, fn, wait=(), inc=()):
        if self.name == engine:
            for s, v in wait:
                if v > 0:
                    self.eng.wait_ge(self.sems[s], v)
            ins = fn(self.eng)
            for s, k in inc:
                ins.then_inc(self.sems[s], k)
        for s, k in inc:
            self.cnt[s] += k

    def wait(self, engine, *sv):
        if self.name == engine:
            for s, v in sv:
                if v > 0:
                    self.eng.wait_ge(self.sems[s], v)

    def dma_mark(self, queue_eng, queue_sem):
        total = self.n(queue_sem)
        self.op(queue_eng, lambda e: e.nop(), wait=[(queue_sem, total)],
                inc=[(queue_eng, 1)])
        return (queue_eng, self.n(queue_eng))

    def barrier(self):
        for en in ENG_ORDER:
            self.op(en, lambda e: e.nop(), inc=[(en, 1)])
        targets = [(s, self.n(s)) for s in ("pe", "act", "dve", "pool", "sp",
                                            "in", "ing", "st", "stg")]
        for en in ENG_ORDER:
            self.wait(en, *targets)


def build_program(S, phases=3):
    import os as _os
    KAPARTS = int(_os.environ.get("KAPARTS", "255"))
    KBPARTS = int(_os.environ.get("KBPARTS", "255"))
    KBEPI = int(_os.environ.get("KBEPI", "15"))
    KNPAIR = int(_os.environ.get("KNPAIR", "8"))
    KBPROJ = int(_os.environ.get("KBPROJ", "63"))
    NQG = S // 512
    nc = bass.Bass("TRN2", target_bir_lowering=False, detect_race_conditions=False)

    x_in = nc.dram_tensor("x", [S, D], F32, kind="ExternalInput").ap()
    wqkv_in = nc.dram_tensor("wqkv", [D, 3 * D], F32, kind="ExternalInput").ap()
    wproj_in = nc.dram_tensor("wproj", [D, D], F32, kind="ExternalInput").ap()
    bproj_in = nc.dram_tensor("bproj", [1, D], F32, kind="ExternalInput").ap()
    wenc_in = nc.dram_tensor("wenc", [D, H], F32, kind="ExternalInput").ap()
    benc_in = nc.dram_tensor("benc", [16, 128], F32, kind="ExternalInput").ap()
    ws2c_in = nc.dram_tensor("ws2c", [H, M], F32, kind="ExternalInput").ap()
    bs2c_in = nc.dram_tensor("bs2c", [1, M], F32, kind="ExternalInput").ap()
    wr_in = nc.dram_tensor("wr", [M, NE], F32, kind="ExternalInput").ap()
    br_in = nc.dram_tensor("br", [1, NE], F32, kind="ExternalInput").ap()
    w1_in = nc.dram_tensor("w1", [128, 4096], F32, kind="ExternalInput").ap()
    b1_in = nc.dram_tensor("b1", [64, 128], F32, kind="ExternalInput").ap()
    w2_in = nc.dram_tensor("w2", [128, 4096], F32, kind="ExternalInput").ap()
    b2_in = nc.dram_tensor("b2", [4, 128], F32, kind="ExternalInput").ap()
    wc2s_in = nc.dram_tensor("wc2s", [M, H], F32, kind="ExternalInput").ap()
    bc2s_in = nc.dram_tensor("bc2s", [16, 128], F32, kind="ExternalInput").ap()
    wdec_in = nc.dram_tensor("wdec", [H, D], F32, kind="ExternalInput").ap()
    bdec_in = nc.dram_tensor("bdec", [1, D], F32, kind="ExternalInput").ap()
    lng_in = nc.dram_tensor("lng", [1, D], F32, kind="ExternalInput").ap()
    lnb_in = nc.dram_tensor("lnb", [1, D], F32, kind="ExternalInput").ap()
    ident_in = nc.dram_tensor("ident", [128, 128], BF16, kind="ExternalInput").ap()
    perm_in = nc.dram_tensor("perm", [128, 128], BF16, kind="ExternalInput").ap()
    cos_in = nc.dram_tensor("cos_t", [128, S], BF16, kind="ExternalInput").ap()
    sin_in = nc.dram_tensor("sin_t", [128, S], BF16, kind="ExternalInput").ap()
    mask_in = nc.dram_tensor("mask2", [128, 256], BF16, kind="ExternalInput").ap()
    el_in = nc.dram_tensor("el", [65, 128], BF16, kind="ExternalInput").ap()
    elr_in = nc.dram_tensor("elr", [128, 128], BF16, kind="ExternalInput").ap()
    ones1_in = nc.dram_tensor("ones1", [128, 2], BF16, kind="ExternalInput").ap()
    onesr_in = nc.dram_tensor("onesr", [1, 128], BF16, kind="ExternalInput").ap()
    sel_in = nc.dram_tensor("sel", [8, 512], BF16, kind="ExternalInput").ap()

    out_hbm = nc.dram_tensor("out", [S, D], F32, kind="ExternalOutput").ap()
    q_scr = nc.dram_tensor("q_scr", [NQG, 8, 128, 512], BF16, kind="Internal").ap()
    x1_scr = nc.dram_tensor("x1_scr", [S, D], F32, kind="Internal").ap()

    ostk = ExitStack()
    oe = ostk.enter_context

    id_sb = nc.alloc_sbuf_tensor("id_sb", [128, 128], BF16).ap()
    onesr_sb = nc.alloc_sbuf_tensor("onesr_sb", [1, 128], BF16).ap()
    c_col = nc.alloc_sbuf_tensor("c_col", [128, 2], F32).ap()[:, 0:1]
    small = nc.alloc_sbuf_tensor("small", [128, 64], F32).ap()
    stats_sb = nc.alloc_sbuf_tensor("stats_sb", [128, 48], F32).ap()

    pbank = nc.alloc_psum_tensor("pbank", [128, 4096], F32).ap()

    cur = [0]
    hold = [None]

    def carve_bf(n, p=128):
        c0 = cur[0]
        cur[0] = (cur[0] + n + 31) & ~31
        if hold[0] is None:
            return None
        return hold[0][0:p, c0:c0 + n]

    def carve_f32(n, p=128):
        c0 = cur[0]
        cur[0] = (cur[0] + 2 * n + 31) & ~31
        if hold[0] is None:
            return None
        return hold[0][:, c0:c0 + 2 * n].bitcast(F32)[0:p, 0:n]

    sems = {}
    for k in ("in", "ing", "st", "stg", "pe", "act", "dve", "pool", "sp"):
        sems[k] = oe(nc.semaphore(f"s_{k}"))

    block = oe(nc.Block())

    T = {}

    def mk_layout():
        cur[0] = 0
        T["kf"] = carve_bf(NPAIR * S)
        T["vres"] = carve_bf((S // 128) * 16 * 65)
        ab_end = cur[0]
        T["wqkv"] = carve_bf(8 * 3 * D)
        T["xnT"] = carve_bf(8 * 512)
        T["cos"] = carve_bf(512)
        T["sin"] = carve_bf(512)
        T["perm"] = carve_bf(128)
        T["ones1"] = carve_bf(2)
        T["qcb0"] = carve_bf(512)
        T["qcb1"] = carve_bf(512)
        T["rotb"] = carve_bf(512)
        T["t1b"] = carve_bf(512)
        T["rop0"] = carve_bf(512)
        T["rop1"] = carve_bf(512)
        T["xna"] = carve_bf(D)
        for i in range(4):
            T[f"xa{i}"] = carve_bf(D)
        a_end = cur[0]
        cur[0] = ab_end
        T["q"] = carve_bf(8 * 512)
        T["wproj"] = carve_bf(8 * D)
        T["p0"] = carve_bf(1024)
        T["p1"] = carve_bf(1024)
        T["l"] = carve_bf(512)
        T["elr"] = carve_bf(128)
        T["rl"] = carve_f32(512)
        T["rlb"] = carve_f32(512, p=64)
        T["tb"] = carve_bf(512, p=64)
        T["pv0"] = carve_f32(512)
        T["pv1"] = carve_f32(512)
        T["attn"] = carve_bf(8 * 512)
        T["mask"] = carve_bf(256)
        T["bproj_bc"] = carve_f32(D)
        T["t32b"] = carve_f32(D)
        T["x1b"] = carve_f32(D)
        T["xb0"] = carve_f32(D)
        T["xb1"] = carve_f32(D)
        b_end = cur[0]
        cur[0] = 0
        T["wenc"] = carve_bf(8 * H)
        T["wdec"] = carve_bf(16 * D)
        T["ws2c"] = carve_bf(16 * M)
        T["wc2s"] = carve_bf(H)
        T["w1"] = carve_bf(4096)
        T["w2"] = carve_bf(4096)
        T["wr"] = carve_f32(8, p=64)
        T["sel"] = carve_bf(512, p=8)
        T["benc"] = carve_f32(16)
        T["bc2s"] = carve_f32(16)
        T["b1"] = carve_f32(64)
        T["b2"] = carve_f32(4)
        T["bs2c"] = carve_f32(2, p=64)
        T["br"] = carve_f32(8)
        T["lng"] = carve_f32(D)
        T["lnb"] = carve_f32(D)
        T["bdec_bc"] = carve_f32(D)
        T["xn2T"] = carve_bf(8 * 512)
        T["spik"] = carve_bf(16 * 512)
        T["smoe"] = carve_bf(16 * 512)
        T["cdup"] = carve_bf(512)
        T["cont32"] = carve_f32(512, p=64)
        T["h0"] = carve_bf(8 * 512)
        T["h1"] = carve_bf(8 * 512)
        T["gate"] = carve_bf(32)
        T["gT"] = carve_bf(512, p=8)
        T["gbc"] = carve_f32(512)
        T["tp"] = carve_f32(512)
        T["tp2"] = carve_f32(512, p=64)
        T["eo32"] = carve_f32(512, p=64)
        T["eod"] = carve_bf(512)
        T["rsm"] = carve_f32(16)
        T["xnc"] = carve_bf(D)
        T["t32c"] = carve_f32(D)
        T["x1c"] = carve_f32(D)
        for i in range(4):
            T[f"xc{i}"] = carve_f32(D)
        c_end = cur[0]
        return max(a_end, b_end, c_end)

    need = mk_layout()
    arena = nc.alloc_sbuf_tensor("arena", [128, need + 16], BF16).ap()
    hold[0] = arena
    mk_layout()
    T["bs2c"] = T["bs2c"][:, 0:1]
    dly_sb = nc.alloc_sbuf_tensor("dly_sb", [128, 1024], BF16).ap().bitcast(F32)
    dump_sb = nc.alloc_sbuf_tensor("dump_sb", [128, 512], BF16).ap()


    def ln_stats_act(E, src_ap, i, wait=()):
        for hf in range(2):
            E.op("act", lambda a, hf=hf: a.activation(
                dump_sb[:, :], src_ap[:, hf * 512:(hf + 1) * 512], AF.Copy,
                accum_out=stats_sb[:, 12 + 2 * i + hf:13 + 2 * i + hf]),
                wait=(list(wait) if hf == 0 else []), inc=[("act", 1)])
            E.op("act", lambda a, hf=hf: a.activation(
                dump_sb[:, :], src_ap[:, hf * 512:(hf + 1) * 512], AF.Square,
                accum_out=stats_sb[:, 24 + 2 * i + hf:25 + 2 * i + hf]),
                inc=[("act", 1)])
        am = E.n("act")
        E.op("dve", lambda v, i=i: v.tensor_add(
            stats_sb[:, 32 + i:33 + i], stats_sb[:, 12 + 2 * i:13 + 2 * i],
            stats_sb[:, 13 + 2 * i:14 + 2 * i]), wait=[("act", am)])
        dve_pad(E)
        E.op("dve", lambda v, i=i: v.tensor_add(
            stats_sb[:, 36 + i:37 + i], stats_sb[:, 24 + 2 * i:25 + 2 * i],
            stats_sb[:, 25 + 2 * i:26 + 2 * i]))
        dve_pad(E)
        E.op("dve", lambda v, i=i: v.tensor_scalar(
            small[:, 8 + 2 * i:9 + 2 * i], stats_sb[:, 32 + i:33 + i],
            1.0 / D, None, ALU.mult))
        dve_pad(E)
        E.op("dve", lambda v, i=i: v.tensor_mul(
            stats_sb[:, 20 + i:21 + i], small[:, 8 + 2 * i:9 + 2 * i],
            small[:, 8 + 2 * i:9 + 2 * i]))
        dve_pad(E)
        E.op("dve", lambda v, i=i: v.scalar_tensor_tensor(
            small[:, 9 + 2 * i:10 + 2 * i], stats_sb[:, 36 + i:37 + i], 1.0 / D,
            stats_sb[:, 20 + i:21 + i], ALU.mult, ALU.subtract))
        dve_pad(E)
        E.op("dve", lambda v: v.engine_nop(), inc=[("dve", 1)])
        return E.n("dve")

    def dve_pad(E):
        E.op("dve", lambda v: v.tensor_scalar(dly_sb[:, 0:64], dly_sb[:, 0:64],
                                              1.0, None, ALU.mult))

    def rsqrt_newton(E, v_ap, out_ap, t1_ap, t2_ap, n_iter=3):
        I32 = mybir.dt.int32

        def ss(fn):
            E.op("dve", fn)
            dve_pad(E)

        dve_pad(E)
        ss(lambda e: e.tensor_scalar(t1_ap.bitcast(I32), v_ap.bitcast(I32), 1, None,
                                     ALU.logical_shift_right))
        ss(lambda e: e.tensor_scalar(t2_ap.bitcast(I32), t1_ap.bitcast(I32),
                                     -1, None, ALU.bitwise_xor))
        ss(lambda e: e.tensor_scalar(out_ap.bitcast(I32), t2_ap.bitcast(I32),
                                     0x5F3759E0, None, ALU.add))
        for _ in range(n_iter):
            ss(lambda e: e.tensor_mul(t1_ap, out_ap, out_ap))
            ss(lambda e: e.tensor_mul(t1_ap, t1_ap, v_ap))
            ss(lambda e: e.tensor_scalar(t2_ap, t1_ap, -0.5, 1.5, ALU.mult, ALU.add))
            ss(lambda e: e.tensor_mul(out_ap, out_ap, t2_ap))
        E.op("dve", lambda v: v.engine_nop(), wait=[], inc=[("dve", 1)])

    def rstd_from_vars(E, n):
        if n == 1:
            var_ap = small[:, 9:10]
            mean_ap = small[:, 8:9]
        else:
            var_ap = small[:, 8:8 + 2 * n].rearrange("p (a b) -> p a b", b=2)[:, :, 1]
            mean_ap = small[:, 8:8 + 2 * n].rearrange("p (a b) -> p a b", b=2)[:, :, 0]
        E.op("dve", lambda v: v.tensor_scalar(
            small[:, 16:16 + n], var_ap,
            EPS, None, ALU.add), wait=[("dve", E.n("dve"))], inc=[("dve", 1)])
        rsqrt_newton(E, small[:, 16:16 + n], small[:, 20:20 + n],
                     stats_sb[:, 0:n], stats_sb[:, 4:4 + n])
        E.op("dve", lambda v: v.scalar_tensor_tensor(
            small[:, 16:16 + n], mean_ap,
            -1.0, small[:, 20:20 + n], ALU.mult, ALU.mult),
            wait=[("dve", E.n("dve"))], inc=[("dve", 1)])
        return E.n("dve")

    def pe_guard(E, pe_val, extra=()):
        E.op("dve", lambda v: v.tensor_scalar(dly_sb[:, :], dly_sb[:, :], 1.0,
                                              None, ALU.mult),
             wait=[("pe", pe_val)] + list(extra) + [("dve", E.n("dve"))],
             inc=[("dve", 1)])
        return E.n("dve")


    def program(E):
        E.op("sp", lambda s: s.dma_start(id_sb[:, :], ident_in[:, :]), inc=[("in", 16)])
        E.op("sp", lambda s: s.dma_start(onesr_sb[:, :], onesr_in[:, :]), inc=[("in", 16)])
        E.op("dve", lambda v: v.memset(dly_sb[:, :], 0.0), inc=[("dve", 1)])

        phase_a(E, T["kf"], T["vres"])
        E.barrier()
        if phases >= 2:
            phase_b(E, T["kf"], T["vres"])
            E.barrier()
        if phases >= 3:
            phase_c(E)
        E.wait("sp", ("st", E.n("st")), ("stg", E.n("stg")))


    def phase_a(E, kf, vres):
        if True:
            wqkv_sb = T["wqkv"]
            xnT = T["xnT"]
            cos_sb = T["cos"]
            sin_sb = T["sin"]
            perm_sb = T["perm"]
            ones1_sb = T["ones1"]
            qcb = [T["qcb0"], T["qcb1"]]
            rotb = T["rotb"]
            t1b = T["t1b"]
            rop = [T["rop0"], T["rop1"]]
            xn_sb = T["xna"]
            xa_sb = [T[f"xa{i}"] for i in range(4)]

            ps_qkv = [pbank[:, 0:512], pbank[:, 512:1024]]
            ps_rot = [pbank[:, 1024:1536], pbank[:, 1536:2048]]
            ps_tr = [pbank[:, 2048:2112].bitcast(BF16)[:, 0:128],
                     pbank[:, 2560:2624].bitcast(BF16)[:, 0:128]]
            ps_v = [pbank[:, 3072:3584], pbank[:, 3584:4096]]

            E.op("sp", lambda s: s.dma_start(perm_sb[:, :], perm_in[:, :]), inc=[("in", 16)])
            E.op("sp", lambda s: s.dma_start(ones1_sb[:, :], ones1_in[:, :]), inc=[("in", 16)])
            m_aconst = E.dma_mark("sp", "in")
            for dc in range(8):
                E.op("pool", lambda g, dc=dc: g.dma_start(
                    wqkv_sb[:, dc * 3 * D:(dc + 1) * 3 * D],
                    wqkv_in[dc * 128:(dc + 1) * 128, :]), inc=[("ing", 16)])
            m_wq = E.dma_mark("pool", "ing")
            E.op("dve", lambda v: v.memset(
                vres[:, :].rearrange("p (k o) -> p k o", o=65)[:, :, 64], 1.0),
                inc=[("dve", 1)])
            E.op("dve", lambda v: v.memset(small[:, 0:8], 0.0),
                 wait=[("dve", E.n("dve"))], inc=[("dve", 1)])

            r_psqkv = Roll()
            r_psrot = Roll()
            r_qcb = Roll()
            r_rop = Roll()
            r_pstr = Roll()
            r_xslot = Roll()
            r_xn = Roll()
            r_rotb = Roll()
            r_psv = Roll()
            r_nrm = Roll()
            r_nrmpe = Roll()
            tabs_prev = Roll()

            for g in range(NQG):
                if not (KAPARTS & 1):
                    break
                war = tabs_prev.back(1)
                E.op("sp", lambda s, g=g: s.dma_start(
                    cos_sb[:, :], cos_in[:, g * 512:(g + 1) * 512]),
                    wait=[("dve", war)], inc=[("in", 16)])
                E.op("sp", lambda s, g=g: s.dma_start(
                    sin_sb[:, :], sin_in[:, g * 512:(g + 1) * 512]), inc=[("in", 16)])
                m_tab = E.dma_mark("sp", "in")

                for qt in range(4):
                    qa = g * 4 + qt
                    wb = r_xslot.back(4)
                    E.op("pool", lambda p, qa=qa, qt=qt: p.dma_start(
                        xa_sb[qt][:, :], x_in[qa * 128:(qa + 1) * 128, :]),
                        wait=(([("dve", wb[1]), ("dve", tabs_prev.back(1)),
                                ("pe", r_nrmpe.back(1))]
                               if wb else []) if qt == 0 else []),
                        inc=[("ing", 16)])
                m_xa = E.dma_mark("pool", "ing")
                if not (KAPARTS & 16):
                    continue
                for i in range(4):
                    ln_stats_act(E, xa_sb[i][:, :], i, wait=([m_xa] if i == 0 else []))
                if not (KAPARTS & 32):
                    continue
                rstd_done = rstd_from_vars(E, 4)

                if not (KAPARTS & 64):
                    continue
                for qt in range(4):
                    E.op("dve", lambda v, qt=qt: v.tensor_scalar(
                        xn_sb[:, :], xa_sb[qt][:, :],
                        small[:, 20 + qt:21 + qt], small[:, 16 + qt:17 + qt],
                        ALU.mult, ALU.add),
                        wait=[("dve", rstd_done), ("pe", r_xn.back(1)),
                              ("dve", E.n("dve"))],
                        inc=[("dve", 1)])
                    apply_done = E.n("dve")
                    for dc in range(8 if (KAPARTS & 128) else 0):
                        half = dc % 2
                        E.op("pe", lambda t, dc=dc, half=half: t.transpose(
                            ps_tr[half], xn_sb[:, dc * 128:(dc + 1) * 128], id_sb[:, :]),
                            wait=[("dve", apply_done), m_aconst,
                                  ("act", r_pstr.back(2)),
                                  ("dve", r_nrm.back(1))],
                            inc=[("pe", 1)])
                        trd = E.n("pe")
                        E.op("act", lambda a, dc=dc, qt=qt, half=half: a.activation(
                            xnT[:, dc * 512 + qt * 128: dc * 512 + qt * 128 + 128],
                            ps_tr[half], AF.Copy),
                            wait=[("pe", trd)], inc=[("act", 1)])
                        r_pstr.push(E.n("act"))
                    r_xn.push(E.n("pe"))
                    r_xslot.push((0, E.n("dve")))
                xnt_done_a = E.n("act")
                xnt_done = E.n("dve")

                for fc in range(16 if (KAPARTS & 2) else 0):
                    par = fc % 2
                    wq_war = r_psqkv.back(2)
                    for dc in range(8):
                        E.op("pe", lambda t, fc=fc, dc=dc, par=par: t.matmul(
                            ps_qkv[par][:, :],
                            wqkv_sb[:, dc * 3 * D + fc * 128: dc * 3 * D + fc * 128 + 128],
                            xnT[:, dc * 512:(dc + 1) * 512],
                            start=(dc == 0), stop=(dc == 7)),
                            wait=([("act", xnt_done_a), m_wq,
                                   ("act", wq_war[0] if wq_war else 0),
                                   ("dve", wq_war[1] if wq_war else 0)] if dc == 0 else []),
                            inc=([("pe", 1)] if dc == 7 else []))
                    qkv_done = E.n("pe")
                    E.op("act", lambda a, par=par: a.activation(
                        qcb[par][:, :], ps_qkv[par][:, :], AF.Copy),
                        wait=[("pe", qkv_done), ("dve", r_qcb.back(2))],
                        inc=[("act", 1)])
                    qc_done = E.n("act")
                    E.op("pe", lambda t, par=par: t.matmul(
                        ps_rot[par][:, :], perm_sb[:, :], qcb[par][:, :],
                        start=True, stop=True),
                        wait=[("act", qc_done), m_aconst,
                              ("act", r_psrot.back(2)), ("dve", r_nrm.back(2))],
                        inc=[("pe", 1)])
                    rot_done = E.n("pe")
                    E.op("act", lambda a, par=par: a.activation(
                        rotb[:, :], ps_rot[par][:, :], AF.Copy),
                        wait=[("pe", rot_done), ("dve", r_rotb.back(1)),
                              ("pe", r_nrmpe.back(1))],
                        inc=[("act", 1)])
                    r_psrot.push(E.n("act"))
                    rotb_done = E.n("act")
                    E.op("dve", lambda v, par=par: v.tensor_mul(
                        t1b[:, :], qcb[par][:, :], cos_sb[:, :]),
                        wait=[m_tab, ("act", qc_done), ("dve", E.n("dve"))],
                        inc=[("dve", 1)])
                    r_qcb.push(E.n("dve"))
                    E.op("dve", lambda v: v.tensor_mul(
                        rotb[:, :], rotb[:, :], sin_sb[:, :]),
                        wait=[("act", rotb_done), ("dve", E.n("dve"))], inc=[("dve", 1)])
                    if fc < 8:
                        dest = rop[par][:, :]
                        wm = r_rop.back(2)
                        dwait = [wm] if wm else []
                    else:
                        dest = kf[:, (fc - 8) * S + g * 512:(fc - 8) * S + g * 512 + 512]
                        dwait = []
                    E.op("dve", lambda v, dest=dest: v.tensor_add(
                        dest, t1b[:, :], rotb[:, :]),
                        wait=dwait + [("dve", E.n("dve"))], inc=[("dve", 1)])
                    rope_done = E.n("dve")
                    E.op("dve", lambda v, dest=dest: v.tensor_mul(rotb[:, :], dest, dest),
                         wait=[("dve", E.n("dve"))], inc=[("dve", 1)])
                    sq_done = E.n("dve")
                    r_rotb.push(E.n("dve"))
                    E.op("pe", lambda t, par=par: t.matmul(
                        ps_rot[par][0:1, :], ones1_sb[:, 0:1], rotb[:, :],
                        start=True, stop=True),
                        wait=[("dve", sq_done), ("dve", r_nrm.back(1))],
                        inc=[("pe", 1)])
                    nrm_done = E.n("pe")
                    r_nrmpe.push(nrm_done)
                    tgt = 0 if fc < 8 else 1
                    gn = pe_guard(E, nrm_done)
                    E.op("dve", lambda v, par=par: v.reduce_max(
                        small[0:1, 2:3], ps_rot[par][0:1, :], axis=AX.X),
                        wait=[("dve", gn)], inc=[("dve", 1)])
                    E.op("dve", lambda v, tgt=tgt: v.tensor_max(
                        small[0:1, tgt:tgt + 1], small[0:1, tgt:tgt + 1],
                        small[0:1, 2:3]),
                        wait=[("dve", E.n("dve"))], inc=[("dve", 1)])
                    r_nrm.push(E.n("dve"))
                    if fc < 8:
                        E.op("sp", lambda p, g=g, fc=fc, par=par: p.dma_start(
                            q_scr[g, fc, :, :], rop[par][:, :]),
                            wait=[("dve", rope_done)], inc=[("st", 16)])
                        r_rop.push(E.dma_mark("sp", "st"))
                    r_psqkv.push((E.n("act"), E.n("dve")))
                tabs_prev.push(E.n("dve"))

                for qt in range(4 if (KAPARTS & 4) else 0):
                    kt = g * 4 + qt
                    for half in range(2):
                        par = half
                        for dc in range(8):
                            E.op("pe", lambda t, qt=qt, half=half, dc=dc, par=par: t.matmul(
                                ps_v[par][:, :],
                                xnT[:, dc * 512 + qt * 128: dc * 512 + qt * 128 + 128],
                                wqkv_sb[:, dc * 3 * D + 2048 + half * 512:
                                        dc * 3 * D + 2048 + half * 512 + 512],
                                start=(dc == 0), stop=(dc == 7)),
                                wait=([("dve", r_psv.back(2))] if dc == 0 else []),
                                inc=([("pe", 1)] if dc == 7 else []))
                        vmm = E.n("pe")
                        base = kt * 16 * 65 + half * 8 * 65
                        gv = pe_guard(E, vmm)
                        E.op("dve", lambda v, base=base, par=par: v.tensor_copy(
                            vres[:, base:base + 8 * 65].rearrange(
                                "p (h o) -> p h o", o=65)[:, :, 0:64],
                            ps_v[par][:, :].rearrange("p (h d) -> p h d", d=64)),
                            wait=[("dve", gv)], inc=[("dve", 1)])
                        r_psv.push(E.n("dve"))

            if not (KAPARTS & 8):
                return
            def ssd(fn):
                E.op("dve", fn)
                dve_pad(E)
            dve_pad(E)
            ssd(lambda v: v.tensor_mul(small[0:1, 2:3], small[0:1, 0:1], small[0:1, 1:2]))
            rsqrt_newton(E, small[0:1, 2:3], small[0:1, 3:4],
                         small[0:1, 4:5], small[0:1, 5:6])
            ssd(lambda v: v.tensor_mul(small[0:1, 4:5], small[0:1, 2:3], small[0:1, 3:4]))
            ssd(lambda v: v.tensor_scalar(small[0:1, 5:6], small[0:1, 4:5], -0.125,
                                          None, ALU.mult))
            ssd(lambda v: v.tensor_copy(rotb[0:1, 0:1], small[0:1, 5:6]))
            E.op("dve", lambda v: v.engine_nop(), inc=[("dve", 1)])
            cval = E.n("dve")
            E.op("pe", lambda t: t.matmul(ps_rot[0][:, 0:1], onesr_sb[0:1, :],
                                          rotb[0:1, 0:1], start=True, stop=True),
                 wait=[("dve", cval), ("dve", r_nrm.back(1))], inc=[("pe", 1)])
            cbc = E.n("pe")
            E.op("dve", lambda v: v.tensor_copy(c_col[:, :], ps_rot[0][:, 0:1]),
                 wait=[("pe", cbc), ("dve", E.n("dve"))], inc=[("dve", 1)])


    def phase_b(E, kf, vres):
        if True:
            q_sb = T["q"]
            wproj_sb = T["wproj"]
            p_sb = [T["p0"], T["p1"]]
            l_sb = T["l"]
            elr_sb = T["elr"]
            rl_sb = T["rl"]
            rlb_sb = T["rlb"]
            tb_sb = T["tb"]
            pv0_sb = T["pv0"]
            pv1_sb = T["pv1"]
            attn_sb = T["attn"]
            mask_sb = T["mask"]
            bproj_bc = T["bproj_bc"]
            t32_sb = T["t32b"]
            x1_sb = T["x1b"]
            xb_sb = [T["xb0"], T["xb1"]]

            ps_s = [pbank[:, 0:1024], pbank[:, 1024:2048]]
            ps_pv = [pbank[0:65, 2048:2560], pbank[0:65, 2560:3072]]
            ps_bc = pbank[:, 3072:3584]
            ps_proj = pbank[:, 3584:4096]

            E.op("sp", lambda s: s.dma_start(elr_sb[:, :], elr_in[:, :]), inc=[("in", 16)])
            E.op("sp", lambda s: s.dma_start(mask_sb[:, :], mask_in[:, :]), inc=[("in", 16)])
            E.op("sp", lambda s: s.dma_start(
                bproj_bc[:, :], bproj_in[0:1, :].partition_broadcast(128)),
                inc=[("in", 16)])
            m_bconst = E.dma_mark("sp", "in")
            for dc in range(8):
                E.op("pool", lambda g2, dc=dc: g2.dma_start(
                    wproj_sb[:, dc * D:(dc + 1) * D],
                    wproj_in[dc * 128:(dc + 1) * 128, :]), inc=[("ing", 16)])
            m_bw = E.dma_mark("pool", "ing")
            E.op("dve", lambda v: v.memset(l_sb[:, :], 0.0),
                 wait=[("dve", E.n("dve"))], inc=[("dve", 1)])

            r_pss = Roll()
            r_pexp = Roll()
            r_pvfree = Roll()
            r_pvout = Roll()
            r_recip = Roll()
            r_bcB = Roll()
            r_rl = Roll()
            r_x1 = Roll()
            r_x1dma = Roll()
            r_pev = Roll()
            r_xb = Roll()
            stlast = Roll()
            r_qscr = Roll()

            for g in range(NQG if (KBPARTS & 1) else 0):
                NK = 4 * g + 4
                for fc in range(8):
                    E.op("sp", lambda s, g=g, fc=fc: s.dma_start(
                        q_sb[:, fc * 512:(fc + 1) * 512], q_scr[g, fc, :, :]),
                        wait=([("pe", stlast.back(1)),
                               ("st", r_qscr.back(1))] if fc == 0 else []),
                        inc=[("in", 16)])
                m_q = E.dma_mark("sp", "in")

                ncs = min(NPAIR, KNPAIR) if (KBPARTS & 2) else 0
                st_mark = {}

                def emit_st(c, kc):
                    par = kc % 2
                    j = kc - 4 * g
                    off = 128 * j if j > 0 else 0
                    kbase = c * S + kc * 128
                    E.op("pe", lambda t, kbase=kbase: t.ldweights(
                        kf[0:64, kbase:kbase + 128], tile_position=(0, 0)),
                        wait=[m_q, ("act", r_pss.back(2)), m_bconst])
                    E.op("pe", lambda t, kbase=kbase: t.ldweights(
                        kf[64:128, kbase:kbase + 128], tile_position=(64, 0)))
                    E.op("pe", lambda t, kbase=kbase, par=par, off=off, c=c: t.matmul(
                        ps_s[par][:, off:512], kf[0:64, kbase:kbase + 128],
                        q_sb[0:64, c * 512 + off:(c + 1) * 512],
                        start=True, stop=True, tile_position=(0, 0)))
                    E.op("pe", lambda t, kbase=kbase, par=par, off=off, c=c: t.matmul(
                        ps_s[par][:, 512 + off:1024], kf[64:128, kbase:kbase + 128],
                        q_sb[64:128, c * 512 + off:(c + 1) * 512],
                        start=True, stop=True, tile_position=(64, 0)),
                        inc=[("pe", 1)])
                    st_mark[(c, kc)] = E.n("pe")

                for c in range(ncs):
                    if c == 0:
                        emit_st(0, 0)
                    for kc in range(NK):
                        par = kc % 2
                        j = kc - 4 * g
                        off = 128 * j if j > 0 else 0
                        E.op("act", lambda a, par=par, off=off: a.activation(
                            p_sb[par][:, :].rearrange("p (a n) -> p a n", a=2)[:, :, off:512],
                            ps_s[par][:, :].rearrange("p (a n) -> p a n", a=2)[:, :, off:512],
                            AF.Exp, scale=0.125, bias=c_col[:, 0:1]),
                            wait=[("pe", st_mark[(c, kc)]), ("pe", r_pexp.back(2))],
                            inc=[("act", 1)])
                        r_pss.push(E.n("act"))
                        exp_done = E.n("act")
                        mask_done = 0
                        if j >= 0 and _os.environ.get("KNOMASK", "") != "1":
                            E.op("dve", lambda v, par=par, off=off: v.tensor_mul(
                                p_sb[par][:, :].rearrange("p (a n) -> p a n", a=2)
                                [:, :, off:off + 128],
                                p_sb[par][:, :].rearrange("p (a n) -> p a n", a=2)
                                [:, :, off:off + 128],
                                mask_sb[:, :].rearrange("p (a n) -> p a n", a=2)),
                                wait=[("act", exp_done), ("dve", E.n("dve"))],
                                inc=[("dve", 1)])
                            mask_done = E.n("dve")
                        if kc + 1 < NK:
                            emit_st(c, kc + 1)
                        elif c + 1 < ncs:
                            emit_st(c + 1, 0)
                        for hh in range(2 if (KBPARTS & 4) else 0):
                            h = 2 * c + hh
                            vbase = kc * 16 * 65 + h * 65
                            E.op("pe", lambda t, vbase=vbase, hh=hh, par=par, off=off,
                                 kc=kc, NK=NK: t.matmul(
                                ps_pv[hh][:, off:512], vres[:, vbase:vbase + 65],
                                p_sb[par][:, hh * 512 + off: hh * 512 + 512],
                                start=(kc == 0), stop=(kc == NK - 1),
                                skip_group_check=True),
                                wait=(([("act", exp_done), ("dve", mask_done)]
                                       + ([("act", r_pvfree.back(1))]
                                          if (hh == 0 and kc == 0) else []))
                                      if hh == 0 else []),
                                inc=([("pe", 1)] if hh == 1 else []))
                        r_pexp.push(E.n("pe"))
                    pv_done = E.n("pe")
                    if not (KBEPI & 8):
                        continue
                    E.op("act", lambda a: a.activation(
                        pv0_sb[0:65, :], ps_pv[0][:, :], AF.Copy),
                        wait=[("pe", pv_done), ("dve", r_pvout.back(1))],
                        inc=[("act", 1)])
                    E.op("act", lambda a: a.activation(
                        pv1_sb[0:65, :], ps_pv[1][:, :], AF.Copy),
                        inc=[("act", 1)])
                    pvev = E.n("act")
                    r_pvfree.push(pvev)
                    E.op("dve", lambda v: v.tensor_copy(l_sb[0:1, :], pv0_sb[64:65, :]),
                         wait=[("act", pvev), ("dve", E.n("dve"))], inc=[("dve", 1)])
                    E.op("dve", lambda v: v.tensor_copy(l_sb[64:65, :], pv1_sb[64:65, :]),
                         wait=[("dve", E.n("dve"))], inc=[("dve", 1)])
                    lrows = E.n("dve")
                    E.op("pe", lambda t: t.matmul(ps_bc[0:64, :], elr_sb[:, 0:64],
                                                  l_sb[:, :], start=True, stop=True),
                         wait=[("dve", lrows), ("act", r_rl.back(1)),
                               ("dve", r_bcB.back(1)), ("dve", r_pev.back(1))])
                    E.op("pe", lambda t: t.matmul(ps_proj[0:64, :], elr_sb[:, 64:128],
                                                  l_sb[:, :], start=True, stop=True),
                         wait=[("dve", r_pev.back(1))],
                         inc=[("pe", 1)])
                    bcd = E.n("pe")
                    E.op("act", lambda a: a.activation(
                        rl_sb[0:64, :], ps_bc[0:64, :], AF.Copy),
                        wait=[("pe", bcd), ("dve", r_recip.back(1))],
                        inc=[("act", 1)])
                    E.op("act", lambda a: a.activation(
                        rlb_sb[0:64, :], ps_proj[0:64, :], AF.Copy),
                        inc=[("act", 1)])
                    rlev = E.n("act")
                    r_rl.push(rlev)
                    E.op("dve", lambda v: v.reciprocal(rl_sb[0:64, :], rl_sb[0:64, :]),
                         wait=[("act", rlev), ("dve", E.n("dve"))], inc=[("dve", 1)])
                    E.op("dve", lambda v: v.reciprocal(rlb_sb[0:64, :], rlb_sb[0:64, :]),
                         wait=[("dve", E.n("dve"))], inc=[("dve", 1)])
                    r_recip.push(E.n("dve"))
                    E.op("dve", lambda v, c=c: v.tensor_mul(
                        attn_sb[0:64, c * 512:(c + 1) * 512],
                        pv0_sb[0:64, :], rl_sb[0:64, :]),
                        wait=[("dve", E.n("dve"))], inc=[("dve", 1)])
                    E.op("dve", lambda v: v.tensor_mul(
                        tb_sb[0:64, :], pv1_sb[0:64, :], rlb_sb[0:64, :]),
                        wait=[("dve", E.n("dve"))], inc=[("dve", 1)])
                    tbd = E.n("dve")
                    r_pvout.push(tbd)
                    E.op("pe", lambda t: t.matmul(
                        ps_bc[64:128, :], id_sb[0:64, 0:64], tb_sb[0:64, :],
                        start=True, stop=True, tile_position=(0, 64)),
                        wait=[("dve", tbd)], inc=[("pe", 1)])
                    mvd = E.n("pe")
                    g3 = pe_guard(E, mvd)
                    E.op("dve", lambda v, c=c: v.tensor_copy(
                        attn_sb[64:128, c * 512:(c + 1) * 512], ps_bc[64:128, :]),
                        wait=[("dve", g3)], inc=[("dve", 1)])
                    r_bcB.push(E.n("dve"))
                stlast.push(E.n("pe"))
                attn_ready = E.n("dve")

                for qt in range(4 if (KBPARTS & 16) else 0):
                    qa = g * 4 + qt
                    slot = qt % 2
                    wb = r_xb.back(2)
                    E.op("sp", lambda s, qa=qa, slot=slot: s.dma_start(
                        xb_sb[slot][:, :], x_in[qa * 128:(qa + 1) * 128, :]),
                        wait=[("dve", wb)] if wb else [], inc=[("in", 16)])
                    m_xb = E.dma_mark("sp", "in")
                    if not (KBPROJ & 2):
                        continue
                    for half in range(2):
                        ps_pr = ps_proj if half == 0 else ps_bc
                        for dc in range(8):
                            E.op("pe", lambda t, qt=qt, half=half, dc=dc, ps_pr=ps_pr:
                                 t.matmul(
                                ps_pr[:, :],
                                attn_sb[:, dc * 512 + qt * 128: dc * 512 + qt * 128 + 128],
                                wproj_sb[:, dc * D + half * 512: dc * D + half * 512 + 512],
                                start=(dc == 0), stop=(dc == 7)),
                                wait=([("dve", attn_ready), m_bw,
                                       ("dve", r_pev.back(2))] if dc == 0 else []),
                                inc=([("pe", 1)] if dc == 7 else []))
                        pj = E.n("pe")
                        if not (KBPROJ & 32):
                            continue
                        gpj = pe_guard(E, pj, extra=[m_bconst])
                        E.op("dve", lambda v, half=half, ps_pr=ps_pr: v.tensor_add(
                            t32_sb[:, half * 512:(half + 1) * 512], ps_pr[:, :],
                            bproj_bc[:, half * 512:(half + 1) * 512]),
                            wait=[("dve", gpj)],
                            inc=[("dve", 1)])
                        r_pev.push(E.n("dve"))
                    if not (KBPROJ & 4):
                        continue
                    KST = int(_os.environ.get("KST", "7"))
                    if KST & 1:
                        ln_stats_act(E, t32_sb[:, :], 0,
                                     wait=[("dve", E.n("dve"))])
                    rd = E.n("dve")
                    if KST & 2:
                        rd = rstd_from_vars(E, 1)
                    if not (KST & 4):
                        continue
                    E.op("dve", lambda v: v.tensor_scalar(
                        x1_sb[:, :], t32_sb[:, :],
                        small[:, 20:21], small[:, 16:17], ALU.mult, ALU.add),
                        wait=[("dve", rd)]
                             + ([r_x1dma.back(1)] if r_x1dma.back(1) else []),
                        inc=[("dve", 1)])
                    if not (KBPROJ & 8):
                        continue
                    E.op("dve", lambda v, slot=slot: v.tensor_add(
                        x1_sb[:, :], x1_sb[:, :], xb_sb[slot][:, :]),
                        wait=[m_xb, ("dve", E.n("dve"))],
                        inc=[("dve", 1)])
                    r_x1.push(E.n("dve"))
                    r_xb.push(E.n("dve"))
                    fin = E.n("dve")
                    E.op("pool", lambda p, qa=qa: p.dma_start(
                        x1_scr[qa * 128:(qa + 1) * 128, :], x1_sb[:, :]),
                        wait=[("dve", fin)], inc=[("stg", 16)])
                    r_x1dma.push(E.dma_mark("pool", "stg"))


    def phase_c(E):
        if True:
            wenc_sb = T["wenc"]
            wdec_sb = T["wdec"]
            ws2c_sb = T["ws2c"]
            wc2s_sb = T["wc2s"]
            w1_sb = T["w1"]
            w2_sb = T["w2"]
            wr_sb = T["wr"]
            sel_sb = T["sel"]
            benc_sb = T["benc"]
            bc2s_sb = T["bc2s"]
            b1_sb = T["b1"]
            b2_sb = T["b2"]
            bs2c_sb = T["bs2c"]
            br_sb = T["br"]
            lng_sb = T["lng"]
            lnb_sb = T["lnb"]
            bdec_bc = T["bdec_bc"]
            xn2T = T["xn2T"]
            spik = T["spik"]
            smoe = T["smoe"]
            cdup = T["cdup"]
            cont32 = T["cont32"]
            h_sb = [T["h0"], T["h1"]]
            gate_sb = T["gate"]
            gT_sb = T["gT"]
            gbc_sb = T["gbc"]
            tpair_sb = T["tp"]
            tpair2_sb = T["tp2"]
            eo32 = T["eo32"]
            eodup = T["eod"]
            rsm_sb = T["rsm"]
            xn_sb = T["xnc"]
            t32_sb = T["t32c"]
            x1_sb = T["x1c"]
            xc_sb = [T[f"xc{i}"] for i in range(4)]

            ps_c = [pbank[:, 0:512], pbank[:, 512:1024]]
            ps_y = [pbank[:, 1024:1536], pbank[:, 1536:2048]]
            ps_cont = pbank[0:64, 2048:2560]
            ps_g = pbank[:, 2560:3072]
            ps_trc = [pbank[:, 3072:3136].bitcast(BF16)[:, 0:128],
                      pbank[:, 3584:3648].bitcast(BF16)[:, 0:128]]

            for dc in range(8):
                E.op("pool", lambda g2, dc=dc: g2.dma_start(
                    wenc_sb[:, dc * H:(dc + 1) * H],
                    wenc_in[dc * 128:(dc + 1) * 128, :]), inc=[("ing", 16)])
            for hc in range(16):
                E.op("pool", lambda g2, hc=hc: g2.dma_start(
                    wdec_sb[:, hc * D:(hc + 1) * D],
                    wdec_in[hc * 128:(hc + 1) * 128, :]), inc=[("ing", 16)])
                E.op("pool", lambda g2, hc=hc: g2.dma_start(
                    ws2c_sb[:, hc * M:(hc + 1) * M],
                    ws2c_in[hc * 128:(hc + 1) * 128, :]), inc=[("ing", 16)])
            E.op("pool", lambda g2: g2.dma_start(wc2s_sb[0:64, :], wc2s_in[:, :]),
                 inc=[("ing", 16)])
            E.op("pool", lambda g2: g2.dma_start(wc2s_sb[64:128, :], wc2s_in[:, :]),
                 inc=[("ing", 16)])
            E.op("pool", lambda g2: g2.dma_start(w1_sb[:, :], w1_in[:, :]), inc=[("ing", 16)])
            E.op("pool", lambda g2: g2.dma_start(w2_sb[:, :], w2_in[:, :]), inc=[("ing", 16)])
            E.op("pool", lambda g2: g2.dma_start(sel_sb[:, :], sel_in[:, :]), inc=[("ing", 16)])
            E.op("sp", lambda s: s.dma_start(wr_sb[:, :], wr_in[:, :]), inc=[("in", 16)])
            E.op("sp", lambda s: s.dma_start(
                br_sb[:, :], br_in[0:1, :].partition_broadcast(128)), inc=[("in", 16)])
            E.op("sp", lambda s: s.dma_start(
                lng_sb[:, :], lng_in[0:1, :].partition_broadcast(128)), inc=[("in", 16)])
            E.op("sp", lambda s: s.dma_start(
                lnb_sb[:, :], lnb_in[0:1, :].partition_broadcast(128)), inc=[("in", 16)])
            E.op("sp", lambda s: s.dma_start(
                bdec_bc[:, :], bdec_in[0:1, :].partition_broadcast(128)), inc=[("in", 16)])
            with nc.allow_non_contiguous_dma(reason="one-time tiny bias columns"):
                E.op("sp", lambda s: s.dma_start(
                    bs2c_sb[:, :], bs2c_in[0:1, :].rearrange("o m -> m o")),
                    inc=[("in", 16)])
                E.op("sp", lambda s: s.dma_start(
                    benc_sb[:, :], benc_in[:, :].rearrange("a p -> p a")),
                    inc=[("in", 16)])
                E.op("sp", lambda s: s.dma_start(
                    bc2s_sb[:, :], bc2s_in[:, :].rearrange("a p -> p a")),
                    inc=[("in", 16)])
                E.op("sp", lambda s: s.dma_start(
                    b1_sb[:, :], b1_in[:, :].rearrange("a p -> p a")),
                    inc=[("in", 16)])
                E.op("sp", lambda s: s.dma_start(
                    b2_sb[:, :], b2_in[:, :].rearrange("a p -> p a")),
                    inc=[("in", 16)])
            m_cconst = E.dma_mark("sp", "in")
            m_cw = E.dma_mark("pool", "ing")
            E.op("dve", lambda v: v.tensor_scalar(
                benc_sb[:, :], benc_sb[:, :], 0.5, None, ALU.mult),
                wait=[m_cconst, ("dve", E.n("dve"))], inc=[("dve", 1)])
            E.op("dve", lambda v: v.tensor_scalar(
                bc2s_sb[:, :], bc2s_sb[:, :], 0.5, None, ALU.mult),
                wait=[("dve", E.n("dve"))], inc=[("dve", 1)])
            chalf = E.n("dve")
            r_spikw = Roll()
            r_smoew = Roll()

            r_x1slot = Roll()
            r_xnc = Roll()
            r_pstrc = Roll()
            r_psc = Roll()
            r_spk = Roll()
            r_contw = Roll()
            r_gate = Roll()
            r_gT = Roll()
            r_hread = Roll()
            r_gbc = Roll()
            r_psy = Roll()
            r_smoe = Roll()
            r_dev = Roll()
            r_odma = Roll()

            for g in range(NQG):
                for qt in range(4):
                    qa = g * 4 + qt
                    wb = r_x1slot.back(1)
                    E.op("sp", lambda s, qa=qa, qt=qt: s.dma_start(
                        xc_sb[qt][:, :], x1_scr[qa * 128:(qa + 1) * 128, :]),
                        wait=([("dve", wb)] if (wb and qt == 0) else []),
                        inc=[("in", 16)])
                m_xc = E.dma_mark("sp", "in")
                for i in range(4):
                    ln_stats_act(E, xc_sb[i][:, :], i, wait=([m_xc] if i == 0 else []))
                c_rstd = rstd_from_vars(E, 4)
                for qt in range(4):
                    E.op("dve", lambda v, qt=qt: v.tensor_scalar(
                        xn_sb[:, :], xc_sb[qt][:, :],
                        small[:, 20 + qt:21 + qt], small[:, 16 + qt:17 + qt],
                        ALU.mult, ALU.add),
                        wait=[("dve", c_rstd), ("pe", r_xnc.back(1)),
                              ("dve", E.n("dve"))], inc=[("dve", 1)])
                    ap2 = E.n("dve")
                    for dc in range(8):
                        half = dc % 2
                        E.op("pe", lambda t, dc=dc, half=half: t.transpose(
                            ps_trc[half], xn_sb[:, dc * 128:(dc + 1) * 128], id_sb[:, :]),
                            wait=[("dve", ap2), ("act", r_pstrc.back(2))],
                            inc=[("pe", 1)])
                        trd = E.n("pe")
                        E.op("act", lambda a, dc=dc, qt=qt, half=half: a.activation(
                            xn2T[:, dc * 512 + qt * 128: dc * 512 + qt * 128 + 128],
                            ps_trc[half], AF.Copy),
                            wait=[("pe", trd)], inc=[("act", 1)])
                        r_pstrc.push(E.n("act"))
                    r_xnc.push(E.n("pe"))
                xn2_done = E.n("act")

                for hc in range(16):
                    par = hc % 2
                    wpc = r_psc.back(2)
                    for dc in range(8):
                        E.op("pe", lambda t, hc=hc, dc=dc, par=par: t.matmul(
                            ps_c[par][:, :],
                            wenc_sb[:, dc * H + hc * 128: dc * H + hc * 128 + 128],
                            xn2T[:, dc * 512:(dc + 1) * 512],
                            start=(dc == 0), stop=(dc == 7)),
                            wait=([("act", xn2_done), m_cw,
                                   ("act", wpc[0] if wpc else 0),
                                   ("dve", wpc[1] if wpc else 0)] if dc == 0 else []),
                            inc=([("pe", 1)] if dc == 7 else []))
                    encd = E.n("pe")
                    E.op("act", lambda a, hc=hc, par=par: a.activation(
                        spik[:, hc * 512:(hc + 1) * 512], ps_c[par][:, :], AF.Tanh,
                        scale=0.5, bias=benc_sb[:, hc:hc + 1]),
                        wait=[("pe", encd), ("dve", chalf),
                              ("dve", r_spikw.back(16))], inc=[("act", 1)])
                    tnh = E.n("act")
                    E.op("dve", lambda v, hc=hc: v.tensor_scalar(
                        spik[:, hc * 512:(hc + 1) * 512],
                        spik[:, hc * 512:(hc + 1) * 512],
                        0.5, 0.5, ALU.mult, ALU.add),
                        wait=[("act", tnh), ("dve", E.n("dve"))], inc=[("dve", 1)])
                    r_spikw.push(E.n("dve"))
                    r_psc.push((E.n("act"), 0))
                spikes_done = E.n("dve")

                wcont = r_contw.back(1)
                for hc in range(16):
                    E.op("pe", lambda t, hc=hc: t.matmul(
                        ps_cont[0:64, :], ws2c_sb[:, hc * M:(hc + 1) * M],
                        spik[:, hc * 512:(hc + 1) * 512],
                        start=(hc == 0), stop=(hc == 15)),
                        wait=([("dve", spikes_done), ("act", wcont[0] if wcont else 0),
                               ("dve", wcont[1] if wcont else 0)] if hc == 0 else []),
                        inc=([("pe", 1)] if hc == 15 else []))
                contd = E.n("pe")
                gcont = pe_guard(E, contd, extra=[m_cconst])
                E.op("dve", lambda v: v.tensor_scalar(
                    cdup[0:64, :], ps_cont[0:64, :], bs2c_sb[0:64, 0:1], None, ALU.add),
                    wait=[("dve", gcont)],
                    inc=[("dve", 1)])
                E.op("dve", lambda v: v.tensor_scalar(
                    cont32[0:64, :], ps_cont[0:64, :], bs2c_sb[0:64, 0:1], None, ALU.add),
                    wait=[("dve", E.n("dve"))], inc=[("dve", 1)])
                r_contw.push((E.n("act"), E.n("dve")))
                cont_done = E.n("dve")
                E.op("pe", lambda t: t.matmul(
                    ps_y[0][64:128, :], id_sb[0:64, 0:64], cdup[0:64, :],
                    start=True, stop=True, tile_position=(0, 64)),
                    wait=[("dve", cont_done), ("dve", r_psy.back(1))],
                    inc=[("pe", 1)])
                dupd = E.n("pe")
                gdup = pe_guard(E, dupd)
                E.op("dve", lambda v: v.tensor_copy(cdup[64:128, :], ps_y[0][64:128, :]),
                     wait=[("dve", gdup)], inc=[("dve", 1)])
                cdup_done = E.n("dve")

                for qt in range(4):
                    E.op("pe", lambda t, qt=qt: t.matmul(
                        ps_g[0:128, 0:8], cont32[:, qt * 128:(qt + 1) * 128],
                        wr_sb[:, :], start=True, stop=True),
                        wait=[("dve", cont_done), m_cconst,
                              ("dve", cdup_done if qt == 0 else r_gate.back(1))],
                        inc=[("pe", 1)])
                    rd = E.n("pe")
                    grd = pe_guard(E, rd)

                    def sg(fn):
                        E.op("dve", fn)
                        dve_pad(E)
                    E.op("dve", lambda v: v.tensor_add(
                        rsm_sb[:, 0:8], ps_g[0:128, 0:8], br_sb[:, 0:8]),
                        wait=[("dve", grd)])
                    dve_pad(E)
                    sg(lambda v: v.reduce_max(rsm_sb[:, 8:9], rsm_sb[:, 0:8],
                                              axis=AX.X, negate=True))
                    E.op("dve", lambda v: v.engine_nop(), inc=[("dve", 1)])
                    E.op("act", lambda a: a.activation(
                        rsm_sb[:, 0:8], rsm_sb[:, 0:8], AF.Exp, bias=rsm_sb[:, 8:9]),
                        wait=[("dve", E.n("dve"))], inc=[("act", 1)])
                    exd = E.n("act")
                    E.op("dve", lambda v: v.reduce_max(rsm_sb[:, 9:10], rsm_sb[:, 0:8],
                                                       axis=AX.X),
                         wait=[("act", exd)])
                    dve_pad(E)
                    sg(lambda v: v.tensor_scalar(stats_sb[:, 0:8], rsm_sb[:, 0:8],
                                                 rsm_sb[:, 9:10], None, ALU.is_ge))
                    sg(lambda v: v.tensor_mul(stats_sb[:, 8:16], rsm_sb[:, 0:8],
                                              stats_sb[:, 0:8]))
                    sg(lambda v: v.tensor_sub(stats_sb[:, 8:16], rsm_sb[:, 0:8],
                                              stats_sb[:, 8:16]))
                    sg(lambda v: v.reduce_max(rsm_sb[:, 10:11], stats_sb[:, 8:16],
                                              axis=AX.X))
                    sg(lambda v: v.tensor_scalar(stats_sb[:, 0:8], rsm_sb[:, 0:8],
                                                 rsm_sb[:, 10:11], None, ALU.is_ge))
                    sg(lambda v: v.tensor_mul(stats_sb[:, 8:16], rsm_sb[:, 0:8],
                                              stats_sb[:, 0:8]))
                    sg(lambda v: v.reduce_sum(rsm_sb[:, 11:12], stats_sb[:, 8:16],
                                              axis=AX.X))
                    sg(lambda v: v.reciprocal(rsm_sb[:, 12:13], rsm_sb[:, 11:12]))
                    sg(lambda v, qt=qt: v.tensor_scalar(
                        gate_sb[:, qt * 8:(qt + 1) * 8], stats_sb[:, 8:16],
                        rsm_sb[:, 12:13], None, ALU.mult))
                    E.op("dve", lambda v: v.engine_nop(), inc=[("dve", 1)])
                    r_gate.push(E.n("dve"))
                    gdone = E.n("dve")
                    E.op("pe", lambda t, qt=qt: t.transpose(
                        ps_trc[qt % 2][0:8, :], gate_sb[:, qt * 8:(qt + 1) * 8],
                        id_sb[:, :]),
                        wait=[("dve", gdone), ("dve", r_gT.back(2))],
                        inc=[("pe", 1)])
                    gtd = E.n("pe")
                    E.op("act", lambda a, qt=qt: a.activation(
                        gT_sb[0:8, qt * 128:(qt + 1) * 128], ps_trc[qt % 2][0:8, :],
                        AF.Copy),
                        wait=[("pe", gtd)], inc=[("act", 1)])
                    r_gT.push(E.n("act"))
                    r_pstrc.push(E.n("act"))
                    r_pstrc.push(E.n("act"))
                gT_done = E.n("dve")

                for i in range(4):
                    hread = r_hread.back(1)
                    wev = (0, 0)
                    for hc in range(8):
                        wof = (i * 8 + hc) * 128
                        E.op("pe", lambda t, wof=wof: t.ldweights(
                            w1_sb[0:64, wof:wof + 128], tile_position=(0, 0)),
                            wait=(([("dve", cdup_done),
                                    ("act", hread[0] if hread else 0),
                                    ("dve", hread[1] if hread else 0),
                                    ("dve", r_psy.back(1))] if hc == 0 else [])
                                  + [("act", wev[0]), ("dve", wev[1])]))
                        E.op("pe", lambda t, wof=wof: t.ldweights(
                            w1_sb[64:128, wof:wof + 128], tile_position=(64, 0)))
                        pc0 = ps_c[0] if hc % 2 == 0 else ps_y[0]
                        pc1 = ps_c[1] if hc % 2 == 0 else ps_y[1]
                        E.op("pe", lambda t, wof=wof, pc0=pc0: t.matmul(
                            pc0[0:128, :], w1_sb[0:64, wof:wof + 128], cdup[0:64, :],
                            start=True, stop=True, tile_position=(0, 0)))
                        E.op("pe", lambda t, wof=wof, pc1=pc1: t.matmul(
                            pc1[0:128, :], w1_sb[64:128, wof:wof + 128], cdup[64:128, :],
                            start=True, stop=True, tile_position=(64, 0)),
                            inc=[("pe", 1)])
                        w1d = E.n("pe")
                        E.op("act", lambda a, i=i, hc=hc, pc1=pc1: a.activation(
                            h_sb[1][:, hc * 512:(hc + 1) * 512], pc1[0:128, :], AF.Relu,
                            bias=b1_sb[:, (2 * i + 1) * 8 + hc:(2 * i + 1) * 8 + hc + 1]),
                            wait=[("pe", w1d), m_cconst], inc=[("act", 1)])
                        E.op("dve", lambda v, i=i, hc=hc, pc0=pc0: v.tensor_scalar(
                            h_sb[0][:, hc * 512:(hc + 1) * 512], pc0[0:128, :],
                            b1_sb[:, (2 * i) * 8 + hc:(2 * i) * 8 + hc + 1],
                            0.0, ALU.add, ALU.max),
                            wait=[("pe", w1d), ("dve", E.n("dve"))], inc=[("dve", 1)])
                        wev = (E.n("act"), E.n("dve"))
                    h_a = E.n("act")
                    h_d = E.n("dve")
                    E.op("pe", lambda t, i=i: t.matmul(
                        ps_g[:, :], sel_sb[:, i * 128:(i + 1) * 128], gT_sb[0:8, :],
                        start=True, stop=True),
                        wait=[("dve", gT_done), ("dve", r_gbc.back(1))],
                        inc=[("pe", 1)])
                    gbcd = E.n("pe")
                    ggbc = pe_guard(E, gbcd)
                    E.op("dve", lambda v: v.tensor_copy(gbc_sb[:, :], ps_g[:, :]),
                         wait=[("dve", ggbc)], inc=[("dve", 1)])
                    gbc_done = E.n("dve")
                    wy = r_psy.back(1)
                    for hc in range(8):
                        o0 = ((2 * i) * 8 + hc) * 64
                        o1 = ((2 * i + 1) * 8 + hc) * 64
                        E.op("pe", lambda t, o0=o0, hc=hc: t.matmul(
                            ps_y[0][0:64, :], w2_sb[:, o0:o0 + 64],
                            h_sb[0][:, hc * 512:(hc + 1) * 512],
                            start=(hc == 0), stop=(hc == 7), tile_position=(0, 0),
                            skip_group_check=True),
                            wait=([("act", h_a), ("dve", h_d),
                                   ("dve", wy if wy else 0)] if hc == 0 else []))
                        E.op("pe", lambda t, o1=o1, hc=hc: t.matmul(
                            ps_y[1][64:128, :], w2_sb[:, o1:o1 + 64],
                            h_sb[1][:, hc * 512:(hc + 1) * 512],
                            start=(hc == 0), stop=(hc == 7), tile_position=(0, 64),
                            skip_group_check=True),
                            inc=([("pe", 1)] if hc == 7 else []))
                    w2d = E.n("pe")
                    r_hread.push((h_a, h_d))
                    E.op("dve", lambda v, i=i: v.scalar_tensor_tensor(
                        tpair_sb[0:64, :], ps_y[0][0:64, :], b2_sb[0:64, i:i + 1],
                        gbc_sb[0:64, :], ALU.add, ALU.mult),
                        wait=[("pe", w2d), ("dve", E.n("dve"))], inc=[("dve", 1)])
                    E.op("dve", lambda v, i=i: v.scalar_tensor_tensor(
                        tpair2_sb[0:64, :], ps_y[1][64:128, :], b2_sb[64:128, i:i + 1],
                        gbc_sb[64:128, :], ALU.add, ALU.mult),
                        wait=[("dve", E.n("dve"))], inc=[("dve", 1)])
                    if i == 0:
                        E.op("dve", lambda v: v.tensor_add(
                            eo32[0:64, :], tpair_sb[0:64, :], tpair2_sb[0:64, :]),
                            wait=[("dve", E.n("dve"))], inc=[("dve", 1)])
                    else:
                        E.op("dve", lambda v: v.tensor_add(
                            eo32[0:64, :], eo32[0:64, :], tpair_sb[0:64, :]),
                            wait=[("dve", E.n("dve"))], inc=[("dve", 1)])
                        E.op("dve", lambda v: v.tensor_add(
                            eo32[0:64, :], eo32[0:64, :], tpair2_sb[0:64, :]),
                            wait=[("dve", E.n("dve"))], inc=[("dve", 1)])
                    r_psy.push(E.n("dve"))
                    r_gbc.push(E.n("dve"))
                E.op("dve", lambda v: v.tensor_copy(eodup[0:64, :], eo32[0:64, :]),
                     wait=[("dve", E.n("dve"))], inc=[("dve", 1)])
                eod = E.n("dve")
                E.op("pe", lambda t: t.matmul(
                    ps_y[0][64:128, :], id_sb[0:64, 0:64], eodup[0:64, :],
                    start=True, stop=True, tile_position=(0, 64)),
                    wait=[("dve", eod), ("dve", r_psy.back(1))], inc=[("pe", 1)])
                eodd = E.n("pe")
                geod = pe_guard(E, eodd)
                E.op("dve", lambda v: v.tensor_copy(eodup[64:128, :], ps_y[0][64:128, :]),
                     wait=[("dve", geod)], inc=[("dve", 1)])
                r_psy.push(E.n("dve"))
                eodup_done = E.n("dve")

                for pi in range(8):
                    wpc = r_psc.back(2)
                    pcs = [ps_c[0], ps_c[1]] if pi % 2 == 0 else [ps_y[0], ps_y[1]]
                    E.op("pe", lambda t, pi=pi: t.ldweights(
                        wc2s_sb[0:64, pi * 128:(pi + 1) * 128], tile_position=(0, 0)),
                        wait=[("dve", eodup_done),
                              ("act", wpc[0] if wpc else 0),
                              ("dve", wpc[1] if wpc else 0)])
                    E.op("pe", lambda t, pi=pi: t.ldweights(
                        wc2s_sb[64:128, (pi + 8) * 128:(pi + 9) * 128],
                        tile_position=(64, 0)))
                    E.op("pe", lambda t, pi=pi, pcs=pcs: t.matmul(
                        pcs[0][0:128, :], wc2s_sb[0:64, pi * 128:(pi + 1) * 128],
                        eodup[0:64, :], start=True, stop=True, tile_position=(0, 0)))
                    E.op("pe", lambda t, pi=pi, pcs=pcs: t.matmul(
                        pcs[1][0:128, :], wc2s_sb[64:128, (pi + 8) * 128:(pi + 9) * 128],
                        eodup[64:128, :], start=True, stop=True, tile_position=(64, 0)),
                        inc=[("pe", 1)])
                    c2d = E.n("pe")
                    for which in range(2):
                        hcv = pi if which == 0 else pi + 8
                        E.op("act", lambda a, hcv=hcv, which=which, pcs=pcs: a.activation(
                            smoe[:, hcv * 512:(hcv + 1) * 512], pcs[which][0:128, :],
                            AF.Tanh, scale=0.5, bias=bc2s_sb[:, hcv:hcv + 1]),
                            wait=[("pe", c2d), ("dve", chalf),
                                  ("dve", r_smoew.back(16))], inc=[("act", 1)])
                        t2d = E.n("act")
                        E.op("dve", lambda v, hcv=hcv: v.tensor_scalar(
                            smoe[:, hcv * 512:(hcv + 1) * 512],
                            smoe[:, hcv * 512:(hcv + 1) * 512],
                            0.5, 0.5, ALU.mult, ALU.add),
                            wait=[("act", t2d), ("dve", E.n("dve"))], inc=[("dve", 1)])
                        r_smoew.push(E.n("dve"))
                    r_psc.push((E.n("act"), 0))
                smoe_done = E.n("dve")

                for qt in range(4):
                    qa = g * 4 + qt
                    slot = qt % 2
                    for half in range(2):
                        par = half
                        wpc = r_psc.back(2)
                        for hc in range(16):
                            E.op("pe", lambda t, qt=qt, half=half, hc=hc, par=par: t.matmul(
                                ps_c[par][:, :],
                                smoe[:, hc * 512 + qt * 128: hc * 512 + qt * 128 + 128],
                                wdec_sb[:, hc * D + half * 512: hc * D + half * 512 + 512],
                                start=(hc == 0), stop=(hc == 15)),
                                wait=([("dve", smoe_done),
                                       ("act", wpc[0] if wpc else 0),
                                       ("dve", wpc[1] if wpc else 0)] if hc == 0 else []),
                                inc=([("pe", 1)] if hc == 15 else []))
                        dcd = E.n("pe")
                        gdc = pe_guard(E, dcd, extra=[m_cconst])
                        E.op("dve", lambda v, half=half, par=par: v.tensor_add(
                            t32_sb[:, half * 512:(half + 1) * 512], ps_c[par][:, :],
                            bdec_bc[:, half * 512:(half + 1) * 512]),
                            wait=[("dve", gdc)],
                            inc=[("dve", 1)])
                        r_psc.push((E.n("act"), E.n("dve")))
                    ln_stats_act(E, t32_sb[:, :], 0, wait=[("dve", E.n("dve"))])
                    r1 = rstd_from_vars(E, 1)
                    E.op("dve", lambda v: v.tensor_scalar(
                        t32_sb[:, :], t32_sb[:, :],
                        small[:, 20:21], small[:, 16:17], ALU.mult, ALU.add),
                        wait=[("dve", r1)], inc=[("dve", 1)])
                    E.op("dve", lambda v: v.tensor_mul(t32_sb[:, :], t32_sb[:, :],
                                                       lng_sb[:, :]),
                         wait=[m_cconst, ("dve", E.n("dve"))],
                         inc=[("dve", 1)])
                    E.op("dve", lambda v: v.tensor_add(t32_sb[:, :], t32_sb[:, :],
                                                       lnb_sb[:, :]),
                         wait=[("dve", E.n("dve"))], inc=[("dve", 1)])
                    ln_stats_act(E, t32_sb[:, :], 0, wait=[("dve", E.n("dve"))])
                    r2 = rstd_from_vars(E, 1)
                    E.op("dve", lambda v: v.tensor_scalar(
                        t32_sb[:, :], t32_sb[:, :],
                        small[:, 20:21], small[:, 16:17], ALU.mult, ALU.add),
                        wait=[("dve", r2)], inc=[("dve", 1)])
                    E.op("dve", lambda v, qt=qt: v.tensor_add(
                        x1_sb[:, :], t32_sb[:, :], xc_sb[qt][:, :]),
                        wait=(([r_odma.back(1)] if r_odma.back(1) else [])
                              + [("dve", E.n("dve"))]),
                        inc=[("dve", 1)])
                    fin = E.n("dve")
                    r_x1slot.push(fin)
                    E.op("pool", lambda p, qa=qa: p.dma_start(
                        out_hbm[qa * 128:(qa + 1) * 128, :], x1_sb[:, :]),
                        wait=[("dve", fin)], inc=[("stg", 16)])
                    r_odma.push(E.dma_mark("pool", "stg"))

    def section(name):
        def f(eng):
            E = Em(name, eng, sems)
            program(E)
        return f

    block.tensor(section("pe"))
    block.scalar(section("act"))
    block.vector(section("dve"))
    block.gpsimd(section("pool"))
    block.sync(section("sp"))

    ostk.close()
    return nc



def _consts(S):
    inv = 1.0 / (10000.0 ** (np.arange(0, DH, 2, dtype=np.float64) / DH))
    t = np.arange(S, dtype=np.float64)
    freq_row = inv[np.arange(128) % 32]
    ang = freq_row[:, None] * t[None, :]
    cos_t = np.cos(ang).astype(ml_dtypes.bfloat16)
    sin_t = np.sin(ang).astype(ml_dtypes.bfloat16)
    ident = np.eye(128, dtype=ml_dtypes.bfloat16)
    perm = np.zeros((128, 128), dtype=np.float32)
    for m in range(128):
        base = (m // 64) * 64
        mm = m % 64
        if mm < 32:
            perm[base + mm + 32, m] = -1.0
        else:
            perm[base + mm - 32, m] = 1.0
    perm = perm.astype(ml_dtypes.bfloat16)
    kk, qq = np.meshgrid(np.arange(128), np.arange(128), indexing="ij")
    m1 = np.where(kk > qq, 0.0, 1.0).astype(np.float32)
    mask2 = np.concatenate([m1, m1], axis=1).astype(ml_dtypes.bfloat16)
    el = np.zeros((65, 128), dtype=np.float32)
    el[0, 0:64] = 1.0
    el[64, 64:128] = 1.0
    el = el.astype(ml_dtypes.bfloat16)
    elr = np.zeros((128, 128), dtype=np.float32)
    elr[0, 0:64] = 1.0
    elr[64, 64:128] = 1.0
    elr = elr.astype(ml_dtypes.bfloat16)
    ones1 = np.ones((128, 2), dtype=ml_dtypes.bfloat16)
    onesr = np.ones((1, 128), dtype=ml_dtypes.bfloat16)
    sel = np.zeros((8, 512), dtype=np.float32)
    for i in range(4):
        sel[2 * i, i * 128: i * 128 + 64] = 1.0
        sel[2 * i + 1, i * 128 + 64: i * 128 + 128] = 1.0
    sel = sel.astype(ml_dtypes.bfloat16)
    return dict(ident=ident, perm=perm, cos_t=cos_t, sin_t=sin_t, mask2=mask2,
                el=el, elr=elr, ones1=ones1, onesr=onesr, sel=sel)


def _prep_inputs(S, L0_attn_w_qkv, L0_attn_w_proj, L0_attn_b_proj,
                 L0_ffn_W_enc, L0_ffn_b_enc, L0_ffn_W_s2c, L0_ffn_b_s2c,
                 L0_ffn_Wr, L0_ffn_br, L0_ffn_W1, L0_ffn_b1, L0_ffn_W2, L0_ffn_b2,
                 L0_ffn_W_c2s, L0_ffn_b_c2s, L0_ffn_W_dec, L0_ffn_b_dec,
                 L0_ffn_ln_g, L0_ffn_ln_b):
    def lay(a):
        return np.ascontiguousarray(np.asarray(a, dtype=np.float32))

    w1 = lay(L0_ffn_W1)
    w1p = np.zeros((128, 4096), dtype=np.float32)
    for i in range(4):
        for hc in range(8):
            w1p[0:64, (i * 8 + hc) * 128:(i * 8 + hc) * 128 + 128] = \
                w1[2 * i, :, hc * 128:(hc + 1) * 128]
            w1p[64:128, (i * 8 + hc) * 128:(i * 8 + hc) * 128 + 128] = \
                w1[2 * i + 1, :, hc * 128:(hc + 1) * 128]
    w2 = lay(L0_ffn_W2)
    w2p = np.zeros((128, 4096), dtype=np.float32)
    for ee in range(8):
        for hc in range(8):
            w2p[:, (ee * 8 + hc) * 64:(ee * 8 + hc) * 64 + 64] = \
                w2[ee, hc * 128:(hc + 1) * 128, :]
    b1r = lay(L0_ffn_b1).reshape(8, 8, 128).reshape(64, 128)
    b2 = lay(L0_ffn_b2)
    b2r = np.zeros((4, 128), dtype=np.float32)
    for i in range(4):
        b2r[i, 0:64] = b2[2 * i]
        b2r[i, 64:128] = b2[2 * i + 1]

    return dict(
        wqkv=lay(L0_attn_w_qkv), wproj=lay(L0_attn_w_proj),
        bproj=lay(L0_attn_b_proj).reshape(1, D),
        wenc=lay(L0_ffn_W_enc), benc=lay(L0_ffn_b_enc).reshape(16, 128),
        ws2c=lay(L0_ffn_W_s2c), bs2c=lay(L0_ffn_b_s2c).reshape(1, M),
        wr=lay(L0_ffn_Wr), br=lay(L0_ffn_br).reshape(1, NE),
        w1=w1p, b1=b1r, w2=w2p, b2=b2r,
        wc2s=lay(L0_ffn_W_c2s), bc2s=lay(L0_ffn_b_c2s).reshape(16, 128),
        wdec=lay(L0_ffn_W_dec), bdec=lay(L0_ffn_b_dec).reshape(1, D),
        lng=lay(L0_ffn_ln_g).reshape(1, D), lnb=lay(L0_ffn_ln_b).reshape(1, D),
        **_consts(S),
    )


def kernel(L0_attn_x, L0_attn_w_qkv, L0_attn_w_proj, L0_attn_b_proj,
           L0_ffn_W_enc, L0_ffn_b_enc, L0_ffn_W_s2c, L0_ffn_b_s2c,
           L0_ffn_Wr, L0_ffn_br, L0_ffn_W1, L0_ffn_b1, L0_ffn_W2, L0_ffn_b2,
           L0_ffn_W_c2s, L0_ffn_b_c2s, L0_ffn_W_dec, L0_ffn_b_dec,
           L0_ffn_ln_g, L0_ffn_ln_b):
    x = np.ascontiguousarray(np.asarray(L0_attn_x, dtype=np.float32))
    B, S, _ = x.shape
    base = _prep_inputs(S, L0_attn_w_qkv, L0_attn_w_proj, L0_attn_b_proj,
                        L0_ffn_W_enc, L0_ffn_b_enc, L0_ffn_W_s2c, L0_ffn_b_s2c,
                        L0_ffn_Wr, L0_ffn_br, L0_ffn_W1, L0_ffn_b1, L0_ffn_W2,
                        L0_ffn_b2, L0_ffn_W_c2s, L0_ffn_b_c2s, L0_ffn_W_dec,
                        L0_ffn_b_dec, L0_ffn_ln_g, L0_ffn_ln_b)
    in_maps = []
    for b in range(B):
        m = dict(base)
        m["x"] = np.ascontiguousarray(x[b])
        in_maps.append(m)
    import os as _os
    nc = build_program(S, phases=int(_os.environ.get("KPHASES", "3")))
    res = run_bass_kernel_spmd(nc, in_maps, core_ids=list(range(B)))
    out = np.stack([np.asarray(r["out"], dtype=np.float32) for r in res.results],
                   axis=0)
    return out
